# version 05
# baseline (speedup 1.0000x reference)
import numpy as np
import ml_dtypes
import concourse.bass as bass
import concourse.mybir as mybir
from concourse.bass_utils import run_bass_kernel_spmd

F32 = mybir.dt.float32
BF16 = mybir.dt.bfloat16
AF = mybir.ActivationFunctionType
ALU = mybir.AluOpType
AX = mybir.AxisListType.X
BF = ml_dtypes.bfloat16

B, N, C = 4, 8192, 1024
H = 16
NOWN = 4096
T = 512
NP1 = N // T
NP3 = NOWN // T
NKC = 8
NE, NQ, NMC, NOC = 8, 4, 8, 8
MLP = 4096
QM = MLP // NQ

SEM_NAMES = [
    "d_stage", "d_stgg", "d_wqk",
    "d_x1a", "d_x1b", "d_t1a", "d_t1b",
    "d_x3a", "d_x3b",
    "d_xp4a", "d_xp4b", "d_w4a", "d_w4b",
    "d_yla", "d_ylb", "d_b2a", "d_b2b",
    "s_ysta", "s_ystb", "s_xlw", "s_x32",
    "d_x5a", "d_x5b", "s_outa", "s_outb",
    "c_pe", "c_act", "c_dve", "c_pool",
]


class Walk:
    def __init__(self, who, eng, sems):
        self.who = who
        self.eng = eng
        self.sems = sems
        self.S = {k: 0 for k in sems}

    def inc(self, instr, sem, k=1):
        if instr is not None:
            instr.then_inc(self.sems[sem], k)
        self.S[sem] += k
        return self.S[sem]

    def wait(self, who, sem, val):
        if val <= 0:
            return
        if self.who == who:
            self.eng.wait_ge(self.sems[sem], val)

    def on(self, who):
        return self.who == who

    OWN = {"vector": "c_dve", "scalar": "c_act", "gpsimd": "c_pool",
           "tensor": "c_pe"}

    def step(self, who, instr):
        sem = self.OWN[who]
        v = self.inc(instr, sem, 1)
        self.wait(who, sem, v)
        return v


class Carver:

    def __init__(self, arena, elems):
        self.arena = arena
        self.total = elems
        self.off = 0
        self.high = 0

    def take(self, elems, parts=128):
        off = (self.off + 15) & ~15
        self.off = off + elems
        self.high = max(self.high, self.off)
        assert self.off <= self.total, (self.off, self.total)
        return self.arena[0:parts, off:off + elems]

    def mark(self):
        return self.off

    def reset(self, off):
        self.off = off


def _emit(P, tn):
    S = P.S
    nc = tn["nc"]
    d = tn

    e = P.eng if P.on("sync") else None
    for dst, src in [
        (d["selz_sb"], d["selz"]), (d["selbc_sb"], d["selbc"]),
        (d["ident8_sb"], d["ident8"]), (d["ident_sb"], d["ident"]),
        (d["maskKM_sb"], d["maskKM"]), (d["unit8_sb"], d["unit8"]),
        (d["bqk_q_sb"], d["bqk_q"]), (d["bqk_qn_sb"], d["bqk_qn"]),
        (d["lepe_w_sb"], d["lepe_wp"]),
        (d["lepe_b_sb"], d["lepe_bp"]), (d["lepe_bt"], d["lepe_bt_h"]),
        (d["lepe_w1p_sb"], d["lepe_w1p"]), (d["wg_sb"], d["wg"]),
        (d["bgate_sb"], d["bgate"]), (d["b1p_sb"], d["b1p"]),
        (d["ident64_sb"], d["ident64"]),
    ]:
        i = e.dma_start(out=dst[:, :], in_=src[:, :]) if e else None
        P.inc(i, "d_stage", 16)
    g = P.eng if P.on("gpsimd") else None
    i = g.dma_start(out=d["xseq_bf"][:, :], in_=d["xseq"][:, :]) if g else None
    P.inc(i, "d_stgg", 16)
    i = g.dma_start(out=d["xpad_bf"][:, :], in_=d["xpad"][:, :]) if g else None
    P.inc(i, "d_stgg", 16)
    i = g.dma_start(out=d["bqkk_sb"][:, :], in_=d["bqk_k"][:, :]) if g else None
    P.inc(i, "d_stgg", 16)
    i = (g.dma_start(
        out=d["wqk_sb"].rearrange("p (kc f) -> p kc f", kc=NKC),
        in_=d["wqkT"].rearrange("(kc p) f -> p kc f", p=128)[:, :, C:2 * C],
    ) if g else None)
    P.inc(i, "d_wqk", 16)
    d_stage_all = S["d_stage"]
    d_stgg_all = S["d_stgg"]

    for dst, val in [(d["ones_r_bf"], 1.0), (d["ones_c_bf"], 1.0),
                     (d["one1_bf"], 1.0), (d["ones_r_f"], 1.0),
                     (d["ones_c_f"], 1.0), (d["logits_sb"], 0.0)]:
        ms = P.eng.memset(dst[:, :], val) if P.on("vector") else None
        v_const = P.step("vector", ms)

    wqk_sb = d["wqk_sb"]
    ones_r_bf, ones_c_bf, one1_bf = d["ones_r_bf"], d["ones_c_bf"], d["one1_bf"]
    ones_r_f, ones_c_f = d["ones_r_f"], d["ones_c_f"]

    p1_xfm, p1_xtm, p1_ktm = d["p1_xfm"], d["p1_xtm"], d["p1_ktm"]
    p1_r, p1_m = d["p1_r"], d["p1_m"]
    ps_kv, ps_ks, ps_kt, ps_sm = d["ps_kv"], d["ps_ks"], d["ps_kt"], d["ps_sm"]

    xfm_free = [0, 0]
    xfm_done = [0, 0]
    xtm_free = [0, 0]
    xtm_done = [0, 0]
    elu_done = [0, 0, 0, 0]
    ktm_free = [0, 0]
    first_ks = [True, True]
    first_kv = [True, True]
    reg = 0

    def p1_load(panel):
        par = panel % 2
        sem = "d_x1a" if par == 0 else "d_x1b"
        t0 = panel * T
        P.wait("sync", "d_stage", d_stage_all)
        P.wait("sync", "d_stgg", d_stgg_all)
        P.wait("sync", "c_pe", xfm_free[par])
        e = P.eng if P.on("sync") else None
        for cc in range(NKC):
            i = (e.dma_start_transpose(
                p1_xfm[:, (par * NKC + cc) * T:(par * NKC + cc + 1) * T],
                d["xseq_bf"][t0:t0 + T, cc * 128:(cc + 1) * 128],
            ) if e else None)
            P.inc(i, sem, 16)
        xfm_done[par] = S[sem]

    def p1_load_tm(panel):
        par = panel % 2
        sem = "d_t1a" if par == 0 else "d_t1b"
        P.wait("scalar", "d_stage", d_stage_all)
        P.wait("scalar", "d_stgg", d_stgg_all)
        P.wait("scalar", "c_pe", xtm_free[par])
        e = P.eng if P.on("scalar") else None
        i = (e.dma_start(
            out=p1_xtm.rearrange("p (b j f) -> p b j f", b=2, j=4)[:, par],
            in_=d["xseq_bf"].rearrange("(pp j p) f -> pp p j f", j=4, p=128)[panel],
        ) if e else None)
        P.inc(i, sem, 16)
        xtm_done[par] = S[sem]

    p1_load(0)
    p1_load_tm(0)
    for panel in range(NP1):
        par = panel % 2
        semf = "d_x1a" if par == 0 else "d_x1b"
        semt = "d_t1a" if par == 0 else "d_t1b"
        need_f = xfm_done[par]
        need_t = xtm_done[par]
        if panel + 1 < NP1:
            p1_load(panel + 1)
            p1_load_tm(panel + 1)
        pe_j = S["c_pe"]
        for j in range(4):
            jp = j % 2
            for half in range(2):
                bpar = reg % 4
                reg += 1
                P.wait("tensor", semf, need_f)
                if panel == 0 and j == 0:
                    P.wait("tensor", "d_wqk", 16)
                    P.wait("tensor", "d_stage", d_stage_all)
                    P.wait("tensor", "d_stgg", d_stgg_all)
                    P.wait("tensor", "c_dve", v_const)
                P.wait("tensor", "c_dve", elu_done[bpar])
                mm = None
                for kc in range(NKC):
                    if P.on("tensor"):
                        mm = P.eng.matmul(
                            ps_kt[:, bpar * T:(bpar + 1) * T],
                            p1_xfm[:, (par * NKC + kc) * T + j * 128:
                                   (par * NKC + kc) * T + (j + 1) * 128],
                            wqk_sb[:, kc * 1024 + half * T:kc * 1024 + (half + 1) * T],
                            start=(kc == 0), stop=False, skip_group_check=True)
                if P.on("tensor"):
                    mm = P.eng.matmul(
                        ps_kt[:, bpar * T:(bpar + 1) * T],
                        ones_r_bf[:, :], d["bqkk_sb"][:, half * T:(half + 1) * T],
                        start=False, stop=True, skip_group_check=True)
                pe_reg = P.inc(mm, "c_pe", 1)

                P.wait("scalar", "c_pe", pe_reg)
                P.wait("scalar", "c_dve", elu_done[bpar])
                a1 = (P.eng.activation(p1_r[:, bpar * T:(bpar + 1) * T],
                                       ps_kt[:, bpar * T:(bpar + 1) * T], AF.Relu)
                      if P.on("scalar") else None)
                P.step("scalar", a1)
                a2 = (P.eng.activation(p1_m[:, bpar * T:(bpar + 1) * T],
                                       ps_kt[:, bpar * T:(bpar + 1) * T], AF.Relu,
                                       scale=-1.0)
                      if P.on("scalar") else None)
                P.step("scalar", a2)
                a3 = (P.eng.activation(p1_m[:, bpar * T:(bpar + 1) * T],
                                       p1_m[:, bpar * T:(bpar + 1) * T], AF.Exp,
                                       scale=-1.0)
                      if P.on("scalar") else None)
                a3v = P.step("scalar", a3)
                P.wait("vector", "c_act", a3v)
                P.wait("vector", "c_pe", ktm_free[jp])
                d2 = (P.eng.tensor_tensor(
                    p1_ktm[:, jp * C + half * T:jp * C + (half + 1) * T],
                    p1_r[:, bpar * T:(bpar + 1) * T],
                    p1_m[:, bpar * T:(bpar + 1) * T], ALU.add)
                    if P.on("vector") else None)
                elu_done[bpar] = P.step("vector", d2)

            P.wait("tensor", "c_dve", elu_done[(reg - 2) % 4])
            P.wait("tensor", "c_dve", elu_done[(reg - 1) % 4])
            P.wait("tensor", semt, need_t)
            mm = None
            for half in range(2):
                if P.on("tensor"):
                    mm = P.eng.matmul(
                        ps_ks[:, half * T:(half + 1) * T],
                        ones_c_bf[:, :],
                        p1_ktm[:, jp * C + half * T:jp * C + (half + 1) * T],
                        start=first_ks[half], stop=False, skip_group_check=True)
                first_ks[half] = False
            for hh in range(H):
                bank = hh // 8
                st = first_kv[bank] and (hh % 8 == 0)
                if P.on("tensor"):
                    mm = P.eng.matmul(
                        ps_kv[:, hh * 64:(hh + 1) * 64],
                        p1_ktm[:, jp * C + hh * 64:jp * C + hh * 64 + 64],
                        p1_xtm[:, (par * 4 + j) * C + hh * 64:
                               (par * 4 + j) * C + hh * 64 + 64],
                        start=st, stop=False, skip_group_check=True)
                if hh % 8 == 0:
                    first_kv[bank] = False
            ktm_free[jp] = pe_j = P.inc(mm, "c_pe", 1)
        xtm_free[par] = pe_j
        xfm_free[par] = pe_j
    pe1_done = S["c_pe"]

    P.wait("gpsimd", "c_pe", pe1_done)
    i = (P.eng.dma_start(
        out=wqk_sb.rearrange("p (kc f) -> p kc f", kc=NKC),
        in_=d["wqkT"].rearrange("(kc p) f -> p kc f", p=128)[:, :, 0:C],
    ) if P.on("gpsimd") else None)
    P.inc(i, "d_wqk", 16)
    wqk_q_ready = S["d_wqk"]

    P.wait("scalar", "c_pe", pe1_done)
    a = (P.eng.activation(d["kv_bf"][:, :], ps_kv[:, :], AF.Identity, scale=1.0 / N)
         if P.on("scalar") else None)
    P.step("scalar", a)
    a = (P.eng.activation(d["kmean_r"][:, :], ps_ks[:, :], AF.Identity, scale=1.0 / N)
         if P.on("scalar") else None)
    act_km = P.step("scalar", a)

    P.wait("tensor", "c_act", act_km)
    mm = None
    for hf in range(2):
        if P.on("tensor"):
            mm = P.eng.matmul(
                ps_kt[64:128, hf * T:(hf + 1) * T],
                d["ident64_sb"][:, :],
                d["kv_bf"][:, hf * T:(hf + 1) * T],
                start=True, stop=True, skip_group_check=True)
    pe_kv2 = P.inc(mm, "c_pe", 1)
    P.wait("scalar", "c_pe", pe_kv2)
    a = (P.eng.activation(d["kv2_bf"][64:128, :], ps_kt[64:128, 0:1024], AF.Identity)
         if P.on("scalar") else None)
    P.step("scalar", a)
    mm = None
    for cc in range(NKC):
        if P.on("tensor"):
            mm = P.eng.matmul(
                ps_sm[:, 0:8],
                d["kmean_r"][:, cc * 128:(cc + 1) * 128],
                d["unit8_sb"][:, cc * 8:(cc + 1) * 8],
                start=(cc == 0), stop=(cc == NKC - 1), skip_group_check=True)
    pe_kme = P.inc(mm, "c_pe", 1)
    P.wait("scalar", "c_pe", pe_kme)
    a = (P.eng.activation(d["kme_pm"][:, :], ps_sm[:, 0:8], AF.Identity)
         if P.on("scalar") else None)
    act_kme = P.step("scalar", a)
    P.wait("vector", "c_act", act_kme)
    for cc in range(NKC):
        dd = (P.eng.tensor_scalar(
            d["KM_bf"][:, cc * 16:(cc + 1) * 16],
            d["maskKM_sb"][:, cc * 16:(cc + 1) * 16],
            d["kme_pm"][:, cc:cc + 1], None, ALU.mult)
            if P.on("vector") else None)
        dve_km = P.step("vector", dd)
    P.wait("tensor", "c_dve", dve_km)
    mm = None
    for cc in range(NKC):
        if P.on("tensor"):
            mm = P.eng.matmul(
                ps_sm[0:16, 8:9],
                d["KM_bf"][:, cc * 16:(cc + 1) * 16],
                ones_c_bf[:, :],
                start=(cc == 0), stop=(cc == NKC - 1), skip_group_check=True)
    pe_zb = P.inc(mm, "c_pe", 1)
    P.wait("scalar", "c_pe", pe_zb)
    a = (P.eng.activation(d["zden_b"][:, :], ps_sm[0:16, 8:9], AF.Identity)
         if P.on("scalar") else None)
    act_zb = P.step("scalar", a)
    P.wait("vector", "c_act", act_zb)
    dd = (P.eng.tensor_scalar(d["zden_b"][:, :], d["zden_b"][:, :], 1e-6, None,
                              ALU.add) if P.on("vector") else None)
    dve_zb = P.step("vector", dd)

    p3_xe, p3_qbf = d["p3_xe"], d["p3_qbf"]
    p3_r, p3_m, p3_z, p3_zbf = d["p3_r"], d["p3_m"], d["p3_z"], d["p3_zbf"]
    p3_zb, p3_t4 = d["p3_zb"], d["p3_t4"]
    p3_ln, p3_sq, p3_st, p3_ab = d["p3_ln"], d["p3_sq"], d["p3_st"], d["p3_ab"]
    p3_lt, p3_gs = d["p3_lt"], d["p3_gs"]
    ps3_q = d["ps4"]
    ps3_h4 = d["ps4"][:, 4 * T:5 * T]
    ps3_s = d["ps3_s"]
    ps3_b5 = d["ps4"][:, 5 * T:6 * T]
    ps3_b6 = d["ps4"][:, 6 * T:7 * T]
    ps3_g7 = d["ps4"][:, 7 * T:8 * T]

    x3_free_pool = [0, 0]
    x3_free_pe = [0, 0]
    x3_done = [0, 0]
    q_elu_done = [0, 0]
    ln_first_done = [0, 0]
    t4_done = [0, 0]
    zb_act = [0, 0]
    zb_free = [0, 0]
    ln_free_pe = [0, 0]
    ln_stores = [(0, 0), (0, 0)]
    z_evac_act = 0
    gs_free = [0, 0, 0, 0]
    logit_tp_done = 0
    qbf_free = 0
    zbf_free = 0
    g_evac_last = 0
    a_pe_end = [0, 0]
    a_dve_end = [0, 0]

    pe2_done = S["c_pe"]
    dve2_done = S["c_dve"]

    def p3_load(panel):
        par = panel % 2
        sem = "d_x3a" if par == 0 else "d_x3b"
        t0 = panel * T
        P.wait("sync", "d_stage", d_stage_all)
        P.wait("sync", "d_stgg", d_stgg_all)
        if panel <= 1:
            P.wait("sync", "c_pe", pe2_done)
            P.wait("sync", "c_dve", dve2_done)
        P.wait("sync", "c_dve", x3_free_pool[par])
        P.wait("sync", "c_pe", x3_free_pe[par])
        e = P.eng if P.on("sync") else None
        for cc in range(NKC):
            i = (e.dma_start_transpose(
                p3_xe[:, (par * NKC + cc) * 544:(par * NKC + cc + 1) * 544],
                d["xpad_bf"][t0:t0 + 544, cc * 128:(cc + 1) * 128],
            ) if e else None)
            P.inc(i, sem, 16)
        x3_done[par] = S[sem]

    def emit_A(panel):
        nonlocal qbf_free, zbf_free, z_evac_act
        par = panel % 2
        sem = "d_x3a" if par == 0 else "d_x3b"
        need_x = x3_done[par]
        t0 = panel * T
        if panel + 1 < NP3:
            p3_load(panel + 1)

        if panel == 0:
            P.wait("tensor", "d_wqk", wqk_q_ready)
            P.wait("tensor", "c_act", act_zb)

        for cc in range(NKC):
            bpar = cc % 2
            qb = cc % 4
            ps_q = ps3_q[:, qb * T:(qb + 1) * T]
            P.wait("tensor", sem, need_x)
            P.wait("tensor", "c_dve", q_elu_done[bpar])
            if qb >= 2:
                P.wait("tensor", "c_act", zb_act[qb - 2])
            else:
                P.wait("tensor", "c_dve", ln_first_done[qb])
            mm = None
            for kc in range(NKC):
                if P.on("tensor"):
                    mm = P.eng.matmul(
                        ps_q,
                        wqk_sb[:, kc * 1024 + cc * 128:kc * 1024 + (cc + 1) * 128],
                        p3_xe[:, (par * NKC + kc) * 544 + 16:(par * NKC + kc) * 544 + 528],
                        start=(kc == 0), stop=(kc == NKC - 1), skip_group_check=True)
            pe_q = P.inc(mm, "c_pe", 1)
            P.wait("scalar", "c_pe", pe_q)
            P.wait("scalar", "c_dve", q_elu_done[bpar])
            a1 = (P.eng.activation(p3_r[:, bpar * T:(bpar + 1) * T], ps_q,
                                   AF.Relu, bias=d["bqk_q_sb"][:, cc:cc + 1])
                  if P.on("scalar") else None)
            P.step("scalar", a1)
            a2 = (P.eng.activation(p3_m[:, bpar * T:(bpar + 1) * T], ps_q,
                                   AF.Relu, scale=-1.0,
                                   bias=d["bqk_qn_sb"][:, cc:cc + 1])
                  if P.on("scalar") else None)
            P.step("scalar", a2)
            a3 = (P.eng.activation(p3_m[:, bpar * T:(bpar + 1) * T],
                                   p3_m[:, bpar * T:(bpar + 1) * T], AF.Exp,
                                   scale=-1.0)
                  if P.on("scalar") else None)
            a3v = P.step("scalar", a3)
            P.wait("vector", "c_act", a3v)
            P.wait("vector", "c_pe", qbf_free)
            d3 = (P.eng.tensor_tensor(p3_qbf[:, cc * T:(cc + 1) * T],
                                      p3_r[:, bpar * T:(bpar + 1) * T],
                                      p3_m[:, bpar * T:(bpar + 1) * T], ALU.add)
                  if P.on("vector") else None)
            q_elu_done[bpar] = P.step("vector", d3)
        q_all = S["c_dve"]

        P.wait("tensor", "c_dve", q_all)
        P.wait("tensor", "c_act", z_evac_act)
        mm = None
        for cc in range(NKC):
            if P.on("tensor"):
                mm = P.eng.matmul(
                    ps3_h4[0:16, :],
                    d["KM_bf"][:, cc * 16:(cc + 1) * 16],
                    p3_qbf[:, cc * T:(cc + 1) * T],
                    start=(cc == 0), stop=(cc == NKC - 1), skip_group_check=True)
        pe_zden = P.inc(mm, "c_pe", 1)
        P.wait("scalar", "c_pe", pe_zden)
        P.wait("scalar", "c_dve", dve_zb)
        a = (P.eng.activation(p3_z[:, :], ps3_h4[0:16, :], AF.Identity,
                              bias=d["zden_b"][:, :]) if P.on("scalar") else None)
        z_evac_act = P.step("scalar", a)
        P.wait("vector", "c_act", z_evac_act)
        dd = P.eng.reciprocal(p3_z[:, :], p3_z[:, :]) if P.on("vector") else None
        P.step("vector", dd)
        P.wait("vector", "c_pe", zbf_free)
        dd = P.eng.tensor_copy(p3_zbf[:, :], p3_z[:, :]) if P.on("vector") else None
        z_ready = P.step("vector", dd)

        for cc in range(NKC):
            cpar = cc % 2
            ps_at = ps3_q[:, cpar * T:(cpar + 1) * T]
            ps_zb = ps3_q[:, (2 + cpar) * T:(3 + cpar) * T]
            P.wait("tensor", "c_dve", z_ready)
            P.wait("tensor", "c_act", zb_act[cpar])
            mm = (P.eng.matmul(
                ps_zb,
                d["selz_sb"][:, cc * 128:(cc + 1) * 128],
                p3_zbf[:, :], start=True, stop=True, skip_group_check=True)
                if P.on("tensor") else None)
            pe_zbc = P.inc(mm, "c_pe", 1)
            if cc == NKC - 1:
                zbf_free = pe_zbc
            P.wait("scalar", "c_pe", pe_zbc)
            P.wait("scalar", "c_dve", zb_free[cpar])
            a = (P.eng.activation(p3_zb[:, cpar * T:(cpar + 1) * T], ps_zb,
                                  AF.Identity) if P.on("scalar") else None)
            zb_act[cpar] = P.step("scalar", a)

            P.wait("tensor", "c_dve", ln_first_done[cpar])
            mm = None
            for hi in range(2):
                hh = 2 * cc + hi
                kvop = (d["kv_bf"][:, hh * 64:(hh + 1) * 64] if hi == 0 else
                        d["kv2_bf"][64:128, hh * 64:(hh + 1) * 64])
                if P.on("tensor"):
                    mm = P.eng.matmul(
                        ps_at[hi * 64:(hi + 1) * 64, :],
                        kvop,
                        p3_qbf[hi * 64:(hi + 1) * 64, cc * T:(cc + 1) * T],
                        start=True, stop=True, skip_group_check=True)
            pe_at = P.inc(mm, "c_pe", 1)
            if cc == NKC - 1:
                qbf_free = pe_at

            lnoff = (par * NKC + cc) * T
            P.wait("vector", "c_pe", pe_at)
            P.wait("vector", "c_act", zb_act[cpar])
            P.wait("vector", "c_pe", ln_free_pe[par])
            P.wait("vector", "s_x32", ln_stores[par][0])
            P.wait("vector", "s_xlw", ln_stores[par][1])
            dd = (P.eng.tensor_tensor(p3_ln[:, lnoff:lnoff + T],
                                      ps_at, p3_zb[:, cpar * T:(cpar + 1) * T],
                                      ALU.mult) if P.on("vector") else None)
            zb_free[cpar] = ln_first_done[cpar] = P.step("vector", dd)

            xb = (par * NKC + cc) * 544
            P.wait("vector", sem, need_x)
            P.wait("vector", "c_dve", t4_done[cpar])
            P.wait("vector", "c_dve", v_const)
            pl = (P.eng.scalar_tensor_tensor(p3_t4[:, cpar * T:(cpar + 1) * T],
                                             p3_xe[:, xb + 15:xb + 527],
                                             d["lepe_w_sb"][:, cc * 3:cc * 3 + 1],
                                             d["lepe_bt"][:, cc * T:(cc + 1) * T],
                                             ALU.mult, ALU.add)
                  if P.on("vector") else None)
            P.step("vector", pl)
            pl = (P.eng.scalar_tensor_tensor(p3_t4[:, cpar * T:(cpar + 1) * T],
                                             p3_xe[:, xb + 16:xb + 528],
                                             d["lepe_w1p_sb"][:, cc:cc + 1],
                                             p3_t4[:, cpar * T:(cpar + 1) * T],
                                             ALU.mult, ALU.add)
                  if P.on("vector") else None)
            P.step("vector", pl)
            pl = (P.eng.scalar_tensor_tensor(p3_t4[:, cpar * T:(cpar + 1) * T],
                                             p3_xe[:, xb + 17:xb + 529],
                                             d["lepe_w_sb"][:, cc * 3 + 2:cc * 3 + 3],
                                             p3_t4[:, cpar * T:(cpar + 1) * T],
                                             ALU.mult, ALU.add)
                  if P.on("vector") else None)
            P.step("vector", pl)
            dd = (P.eng.tensor_tensor(p3_ln[:, lnoff:lnoff + T],
                                      p3_ln[:, lnoff:lnoff + T],
                                      p3_t4[:, cpar * T:(cpar + 1) * T], ALU.add)
                  if P.on("vector") else None)
            t4_done[cpar] = P.step("vector", dd)
        x3_free_pool[par] = S["c_dve"]
        a_pe_end[par] = S["c_pe"]
        a_dve_end[par] = S["c_dve"]

    def emit_B(panel):
        nonlocal logit_tp_done, g_evac_last
        par = panel % 2
        t0 = panel * T
        lnb = par * NKC * T
        ln_ready = a_dve_end[par]

        mm = None
        for cc in range(NKC):
            P.wait("vector", "c_dve", ln_ready)
            dd = (P.eng.tensor_tensor(p3_sq[:, :], p3_ln[:, lnb + cc * T:lnb + (cc + 1) * T],
                                      p3_ln[:, lnb + cc * T:lnb + (cc + 1) * T], ALU.mult)
                  if P.on("vector") else None)
            sqv = P.step("vector", dd)
            P.wait("tensor", "c_dve", sqv)
            if cc == 0:
                P.wait("tensor", "c_act", g_evac_last)
            if P.on("tensor"):
                mm = P.eng.matmul(ps3_s[:, 0:T], ones_c_f[:, :],
                                  p3_ln[:, lnb + cc * T:lnb + (cc + 1) * T],
                                  start=(cc == 0), stop=(cc == NKC - 1),
                                  skip_group_check=True)
                mm = P.eng.matmul(ps3_s[:, T:2 * T], ones_c_f[:, :], p3_sq[:, :],
                                  start=(cc == 0), stop=(cc == NKC - 1),
                                  skip_group_check=True)
            pe_st = P.inc(mm, "c_pe", 1)
            P.wait("vector", "c_pe", pe_st)
        pe_stats = S["c_pe"]

        P.wait("scalar", "c_pe", pe_stats)
        a = (P.eng.activation(p3_st[:, 0:T], ps3_s[:, 0:T], AF.Identity, scale=1.0 / C)
             if P.on("scalar") else None)
        P.step("scalar", a)
        a = (P.eng.activation(p3_st[:, T:2 * T], ps3_s[:, T:2 * T], AF.Identity,
                              scale=1.0 / C) if P.on("scalar") else None)
        av = P.step("scalar", a)
        P.wait("vector", "c_act", av)
        dd = (P.eng.tensor_tensor(p3_st[:, 2 * T:3 * T], p3_st[:, 0:T],
                                  p3_st[:, 0:T], ALU.mult) if P.on("vector") else None)
        P.step("vector", dd)
        dd = (P.eng.scalar_tensor_tensor(p3_st[:, 2 * T:3 * T], p3_st[:, T:2 * T],
                                         1e-5, p3_st[:, 2 * T:3 * T],
                                         ALU.add, ALU.subtract)
              if P.on("vector") else None)
        dvar = P.step("vector", dd)
        P.wait("scalar", "c_dve", dvar)
        a = (P.eng.activation(p3_st[:, 2 * T:3 * T], p3_st[:, 2 * T:3 * T], AF.Ln)
             if P.on("scalar") else None)
        P.step("scalar", a)
        a = (P.eng.activation(p3_st[:, 2 * T:3 * T], p3_st[:, 2 * T:3 * T], AF.Exp,
                              scale=-0.5) if P.on("scalar") else None)
        arstd = P.step("scalar", a)
        P.wait("vector", "c_act", arstd)
        dd = (P.eng.tensor_tensor(p3_st[:, 3 * T:4 * T], p3_st[:, 0:T],
                                  p3_st[:, 2 * T:3 * T], ALU.mult)
              if P.on("vector") else None)
        P.step("vector", dd)
        dd = (P.eng.tensor_scalar(p3_st[:, 3 * T:4 * T], p3_st[:, 3 * T:4 * T],
                                  -1.0, None, ALU.mult) if P.on("vector") else None)
        dbsh = P.step("vector", dd)
        P.wait("tensor", "c_dve", dbsh)
        mm = (P.eng.matmul(ps3_b5, ones_r_f[:, :], p3_st[:, 2 * T:3 * T],
                           start=True, stop=True, skip_group_check=True)
              if P.on("tensor") else None)
        pe_a = P.inc(mm, "c_pe", 1)
        P.wait("scalar", "c_pe", pe_a)
        a = (P.eng.activation(p3_ab[:, 0:T], ps3_b5, AF.Identity)
             if P.on("scalar") else None)
        aa = P.step("scalar", a)
        P.wait("tensor", "c_act", aa)
        mm = (P.eng.matmul(ps3_b5, ones_r_f[:, :], p3_st[:, 3 * T:4 * T],
                           start=True, stop=True, skip_group_check=True)
              if P.on("tensor") else None)
        pe_b = P.inc(mm, "c_pe", 1)
        P.wait("scalar", "c_pe", pe_b)
        a = (P.eng.activation(p3_ab[:, T:2 * T], ps3_b5, AF.Identity)
             if P.on("scalar") else None)
        ab_ready = P.step("scalar", a)

        gate_first = True
        norm_done = 0
        for cc in range(NKC):
            P.wait("vector", "c_act", ab_ready)
            dd = (P.eng.tensor_tensor(p3_ln[:, lnb + cc * T:lnb + (cc + 1) * T],
                                      p3_ln[:, lnb + cc * T:lnb + (cc + 1) * T],
                                      p3_ab[:, 0:T], ALU.mult)
                  if P.on("vector") else None)
            P.step("vector", dd)
            dd = (P.eng.tensor_tensor(p3_ln[:, lnb + cc * T:lnb + (cc + 1) * T],
                                      p3_ln[:, lnb + cc * T:lnb + (cc + 1) * T],
                                      p3_ab[:, T:2 * T], ALU.add)
                  if P.on("vector") else None)
            norm_done = P.step("vector", dd)
            P.wait("tensor", "c_dve", norm_done)
            mm = (P.eng.matmul(ps3_g7[0:8, :], d["wg_sb"][:, cc * 8:(cc + 1) * 8],
                               p3_ln[:, lnb + cc * T:lnb + (cc + 1) * T],
                               start=gate_first, stop=(cc == NKC - 1),
                               skip_group_check=True)
                  if P.on("tensor") else None)
            gate_first = False
            P.inc(mm, "c_pe", 1)
        pe_gate = S["c_pe"]
        ln_free_pe[par] = pe_gate
        norm_all = norm_done

        P.wait("sync", "c_dve", norm_all)
        e = P.eng if P.on("sync") else None
        for cc in range(NKC):
            i = (e.dma_start(out=d["xln32_d"][cc, :, t0:t0 + T],
                             in_=p3_ln[:, lnb + cc * T:lnb + (cc + 1) * T]) if e else None)
            P.inc(i, "s_x32", 16)
        P.wait("gpsimd", "c_dve", norm_all)
        g = P.eng if P.on("gpsimd") else None
        for cc in range(NKC):
            i2 = (g.dma_start(out=d["xlnbf_d"][cc, :, t0:t0 + T],
                              in_=p3_ln[:, lnb + cc * T:lnb + (cc + 1) * T]) if g else None)
            P.inc(i2, "s_xlw", 16)
        ln_stores[par] = (S["s_x32"], S["s_xlw"])

        P.wait("scalar", "c_pe", pe_gate)
        P.wait("scalar", "c_pe", logit_tp_done)
        a = (P.eng.activation(d["logits_sb"][0:8, :], ps3_g7[0:8, :], AF.Identity,
                              bias=d["bgate_sb"][:, :]) if P.on("scalar") else None)
        a_log = P.step("scalar", a)
        for jb in range(4):
            P.wait("tensor", "c_act", a_log)
            P.wait("tensor", "c_act", g_evac_last)
            mm = (P.eng.transpose(ps3_b6[:, 0:128],
                                  d["logits_sb"][:, jb * 128:(jb + 1) * 128],
                                  d["ident_sb"][:, :]) if P.on("tensor") else None)
            pe_t = P.inc(mm, "c_pe", 1)
            logit_tp_done = pe_t
            P.wait("vector", "c_pe", pe_t)
            dd = (P.eng.tensor_copy(p3_lt[:, jb * 8:(jb + 1) * 8], ps3_b6[:, 0:8])
                  if P.on("vector") else None)
            P.step("vector", dd)
            lt = p3_lt[:, jb * 8:(jb + 1) * 8]
            e8 = p3_gs[:, jb * 8:(jb + 1) * 8]
            oh8 = p3_gs[:, 32 + jb * 8:32 + (jb + 1) * 8]
            mx = p3_gs[:, 64 + jb:65 + jb]
            sm = p3_gs[:, 68 + jb:69 + jb]
            pc = p3_gs[:, 72 + jb:73 + jb]
            P.wait("vector", "c_pe", gs_free[jb])
            dd = P.eng.reduce_max(mx, lt, axis=AX) if P.on("vector") else None
            P.step("vector", dd)
            dd = (P.eng.tensor_scalar(e8, lt, mx, None, ALU.subtract)
                  if P.on("vector") else None)
            P.step("vector", dd)
            dd = (P.eng.tensor_scalar(oh8, lt, mx, None, ALU.is_equal)
                  if P.on("vector") else None)
            dv = P.step("vector", dd)
            P.wait("scalar", "c_dve", dv)
            a = P.eng.activation(e8, e8, AF.Exp) if P.on("scalar") else None
            ae = P.step("scalar", a)
            P.wait("vector", "c_act", ae)
            dd = P.eng.reduce_sum(sm, e8, axis=AX) if P.on("vector") else None
            P.step("vector", dd)
            dd = P.eng.reciprocal(pc, sm) if P.on("vector") else None
            P.step("vector", dd)
            dd = (P.eng.tensor_scalar(oh8, oh8, pc, None, ALU.mult)
                  if P.on("vector") else None)
            dg = P.step("vector", dd)
            P.wait("tensor", "c_dve", dg)
            mm = (P.eng.transpose(ps3_b6[0:8, 0:128], oh8, d["ident_sb"][:, :])
                  if P.on("tensor") else None)
            pe_g = P.inc(mm, "c_pe", 1)
            gs_free[jb] = pe_g
            P.wait("scalar", "c_pe", pe_g)
            a = (P.eng.activation(d["g_bf"][:, t0 + jb * 128:t0 + (jb + 1) * 128],
                                  ps3_b6[0:8, 0:128], AF.Identity)
                 if P.on("scalar") else None)
            g_evac_last = P.step("scalar", a)
        x3_free_pe[par] = max(x3_free_pe[par], a_pe_end[par])

    p3_load(0)
    emit_A(0)
    for panel in range(1, NP3):
        emit_A(panel)
        emit_B(panel - 1)
    emit_B(NP3 - 1)

    pe3_done = S["c_pe"]
    act3_done = S["c_act"]
    dve3_done = S["c_dve"]
    pool3_done = S["c_pool"]
    xlw_all = S["s_xlw"]
    x32_all = S["s_x32"]

    p4_w1, p4_w2, p4_xp = d["p4_w1"], d["p4_w2"], d["p4_xp"]
    p4_h, p4_hr, p4_gb = d["p4_h"], d["p4_hr"], d["p4_gb"]
    p4_gr, p4_b2, p4_yb = d["p4_gr"], d["p4_b2"], d["p4_yb"]
    ps4 = d["ps4"]

    w_free = [0, 0]
    w_done = [0, 0]
    xp_free = [0, 0]
    xp_done = [0, 0]
    yl_done = [0, 0]
    b2_free = [0, 0]
    b2_done = [0, 0]
    bank_evac = [0] * NOC
    h_done_m = [0] * NMC
    h_evac = [0, 0]
    hsb_used = 0
    gb_act = [0, 0]
    gr_act = [0, 0]
    gb_free = [0, 0]
    gr_free = [0, 0]
    hr_free = [0, 0]

    def w_load(it):
        e_, q_ = divmod(it, NQ)
        par = it % 2
        sem = "d_w4a" if par == 0 else "d_w4b"
        P.wait("gpsimd", "c_pe", w_free[par])
        if it == 0:
            P.wait("gpsimd", "c_pe", pe3_done)
            P.wait("gpsimd", "c_pool", pool3_done)
            P.wait("gpsimd", "c_dve", dve3_done)
            P.wait("gpsimd", "c_act", act3_done)
        g = P.eng if P.on("gpsimd") else None
        i = (g.dma_start(
            out=p4_w1.rearrange("p (b kc m) -> p b kc m", b=2, kc=NKC)[:, par],
            in_=d["w1"].rearrange("e (kc p) m -> e p kc m", p=128)[e_][:, :, q_ * QM:(q_ + 1) * QM],
        ) if g else None)
        P.inc(i, sem, 16)
        i = (g.dma_start(
            out=p4_w2.rearrange("p (b mc o) -> p b mc o", b=2, mc=NMC)[:, par],
            in_=d["w2"].rearrange("e (mc p) o -> e p mc o", p=128, mc=32)[e_][:, q_ * NMC:(q_ + 1) * NMC, :],
        ) if g else None)
        P.inc(i, sem, 16)
        w_done[par] = S[sem]

    def b2_load(e_):
        par = e_ % 2
        sem = "d_b2a" if par == 0 else "d_b2b"
        P.wait("gpsimd", "c_pe", b2_free[par])
        if e_ == 0:
            P.wait("gpsimd", "c_pe", pe3_done)
        g = P.eng if P.on("gpsimd") else None
        i = (g.dma_start(out=p4_b2[:, par * C:(par + 1) * C],
                         in_=d["b2r"][e_:e_ + 1, :]) if g else None)
        P.inc(i, sem, 16)
        b2_done[par] = S[sem]

    def xp_load(it, p):
        ip = (it * NP3 + p) % 2
        sem = "d_xp4a" if ip == 0 else "d_xp4b"
        P.wait("sync", "c_pe", xp_free[ip])
        if it == 0 and p <= 1:
            P.wait("sync", "s_xlw", xlw_all)
            P.wait("sync", "c_pe", pe3_done)
            P.wait("sync", "c_pool", pool3_done)
            P.wait("sync", "c_dve", dve3_done)
            P.wait("sync", "c_act", act3_done)
        e = P.eng if P.on("sync") else None
        i = (e.dma_start(
            out=p4_xp.rearrange("p (b kc t) -> p b kc t", b=2, kc=NKC)[:, ip],
            in_=d["xlnbf_d"].rearrange("kc p t -> p kc t")[:, :, p * T:(p + 1) * T],
        ) if e else None)
        P.inc(i, sem, 16)
        xp_done[ip] = S[sem]

    def y_load(it, p):
        ip = (it * NP3 + p) % 2
        sem = "d_yla" if ip == 0 else "d_ylb"
        ssem = "s_ysta" if ip == 0 else "s_ystb"
        P.wait("sync", ssem, S[ssem])
        e = P.eng if P.on("sync") else None
        i = (e.dma_start(
            out=p4_yb.rearrange("p (b oc t) -> p b oc t", b=2, oc=NOC)[:, ip],
            in_=d["y_d"].rearrange("oc p t -> p oc t")[:, :, p * T:(p + 1) * T],
        ) if e else None)
        P.inc(i, sem, 16)
        yl_done[ip] = S[sem]

    w_load(0)
    b2_load(0)
    xp_load(0, 0)
    for it in range(NE * NQ):
        e_, q_ = divmod(it, NQ)
        wpar = it % 2
        wsem = "d_w4a" if wpar == 0 else "d_w4b"
        this_w = w_done[wpar]
        if it + 1 < NE * NQ:
            w_load(it + 1)
        if q_ == NQ - 1 and e_ + 1 < NE:
            b2_load(e_ + 1)
        for p in range(NP3):
            ip = (it * NP3 + p) % 2
            xsem = "d_xp4a" if ip == 0 else "d_xp4b"
            this_xp = xp_done[ip]
            nit, np_ = (it, p + 1) if p + 1 < NP3 else (it + 1, 0)
            if nit < NE * NQ:
                xp_load(nit, np_)
            if it >= 1:
                y_load(it, p)

            if it == 0 and p == 0:
                P.wait("tensor", "c_act", act3_done)
                P.wait("tensor", "c_dve", dve3_done)
            P.wait("tensor", "c_dve", bank_evac[2])
            P.wait("tensor", "c_dve", bank_evac[3])
            P.wait("tensor", "c_act", gr_act[ip])
            P.wait("tensor", "c_act", gb_act[ip])
            mm = (P.eng.matmul(ps4[0:1, 2 * T:3 * T],
                               d["ident8_sb"][:, e_:e_ + 1],
                               d["g_bf"][:, p * T:(p + 1) * T],
                               start=True, stop=True, skip_group_check=True)
                  if P.on("tensor") else None)
            pe_gr = P.inc(mm, "c_pe", 1)
            mm = (P.eng.matmul(ps4[:, 3 * T:4 * T],
                               d["selbc_sb"][:, e_ * 128:(e_ + 1) * 128],
                               d["g_bf"][:, p * T:(p + 1) * T],
                               start=True, stop=True, skip_group_check=True)
                  if P.on("tensor") else None)
            pe_gb = P.inc(mm, "c_pe", 1)
            P.wait("scalar", "c_pe", pe_gr)
            P.wait("scalar", "c_pe", gr_free[ip])
            a = (P.eng.activation(p4_gr[:, ip * T:(ip + 1) * T], ps4[0:1, 2 * T:3 * T],
                                  AF.Identity) if P.on("scalar") else None)
            gr_act[ip] = P.step("scalar", a)
            P.wait("scalar", "c_pe", pe_gb)
            P.wait("scalar", "c_pool", gb_free[ip])
            a = (P.eng.activation(p4_gb[:, ip * T:(ip + 1) * T], ps4[:, 3 * T:4 * T],
                                  AF.Identity) if P.on("scalar") else None)
            gb_act[ip] = P.step("scalar", a)

            P.wait("tensor", wsem, this_w)
            P.wait("tensor", xsem, this_xp)
            for m in range(NMC):
                hb = m % 2
                P.wait("tensor", "c_dve", bank_evac[hb])
                P.wait("tensor", "c_act", h_evac[hb])
                mm = None
                for kc in range(NKC):
                    if P.on("tensor"):
                        mm = P.eng.matmul(
                            ps4[:, hb * T:(hb + 1) * T],
                            p4_w1[:, (wpar * NKC + kc) * QM + m * 128:
                                  (wpar * NKC + kc) * QM + (m + 1) * 128],
                            p4_xp[:, (ip * NKC + kc) * T:(ip * NKC + kc + 1) * T],
                            start=(kc == 0), stop=(kc == NKC - 1),
                            skip_group_check=True)
                pe_h = P.inc(mm, "c_pe", 1)
                hp = m % 2
                P.wait("scalar", "c_pe", pe_h)
                P.wait("scalar", "c_pool", hr_free[hp])
                a = (P.eng.activation(
                    p4_hr[:, hp * T:(hp + 1) * T],
                    ps4[:, hb * T:(hb + 1) * T], AF.Relu,
                    bias=d["b1p_sb"][:, e_ * 32 + q_ * NMC + m:e_ * 32 + q_ * NMC + m + 1])
                    if P.on("scalar") else None)
                h_evac[hb] = P.step("scalar", a)
                P.wait("gpsimd", "c_act", h_evac[hb])
                P.wait("gpsimd", "c_act", gb_act[ip])
                P.wait("gpsimd", "c_pe", hsb_used)
                pl = (P.eng.tensor_tensor(p4_h[:, m * T:(m + 1) * T],
                                          p4_hr[:, hp * T:(hp + 1) * T],
                                          p4_gb[:, ip * T:(ip + 1) * T], ALU.mult)
                      if P.on("gpsimd") else None)
                hr_free[hp] = gb_free[ip] = P.step("gpsimd", pl)
                h_done_m[m] = S["c_pool"]

            P.wait("tensor", "c_act", gr_act[ip])
            last_oc_mm = 0
            for oc in range(NOC):
                P.wait("tensor", "c_dve", bank_evac[oc])
                if oc < 2:
                    P.wait("tensor", "c_act", h_evac[oc])
                mm = None
                first = True
                if q_ == 0:
                    bsem = "d_b2a" if e_ % 2 == 0 else "d_b2b"
                    P.wait("tensor", bsem, b2_done[e_ % 2])
                    if P.on("tensor"):
                        mm = P.eng.matmul(
                            ps4[:, oc * T:(oc + 1) * T],
                            p4_b2[:, (e_ % 2) * C + oc * 128:(e_ % 2) * C + (oc + 1) * 128],
                            p4_gr[:, ip * T:(ip + 1) * T],
                            start=True, stop=False, skip_group_check=True)
                    first = False
                for m in range(NMC):
                    if oc == 0:
                        P.wait("tensor", "c_pool", h_done_m[m])
                    if P.on("tensor"):
                        mm = P.eng.matmul(
                            ps4[:, oc * T:(oc + 1) * T],
                            p4_w2[:, (wpar * NMC + m) * C + oc * 128:
                                  (wpar * NMC + m) * C + (oc + 1) * 128],
                            p4_h[:, m * T:(m + 1) * T],
                            start=first, stop=(m == NMC - 1),
                            skip_group_check=True)
                    first = False
                pe_oc = P.inc(mm, "c_pe", 1)
                last_oc_mm = pe_oc
                P.wait("vector", "c_pe", pe_oc)
                if it == 0:
                    if oc == 0:
                        ssem = "s_ysta" if ip == 0 else "s_ystb"
                        P.wait("vector", ssem, S[ssem])
                    dd = (P.eng.tensor_copy(
                        p4_yb[:, (ip * NOC + oc) * T:(ip * NOC + oc + 1) * T],
                        ps4[:, oc * T:(oc + 1) * T]) if P.on("vector") else None)
                else:
                    ysem = "d_yla" if ip == 0 else "d_ylb"
                    P.wait("vector", ysem, yl_done[ip])
                    dd = (P.eng.tensor_tensor(
                        p4_yb[:, (ip * NOC + oc) * T:(ip * NOC + oc + 1) * T],
                        ps4[:, oc * T:(oc + 1) * T],
                        p4_yb[:, (ip * NOC + oc) * T:(ip * NOC + oc + 1) * T],
                        ALU.add) if P.on("vector") else None)
                bank_evac[oc] = P.step("vector", dd)
            hsb_used = last_oc_mm
            b2_free[e_ % 2] = last_oc_mm
            gr_free[ip] = last_oc_mm
            w_free[wpar] = last_oc_mm
            xp_free[ip] = last_oc_mm

            P.wait("sync", "c_dve", bank_evac[NOC - 1])
            e = P.eng if P.on("sync") else None
            i = (e.dma_start(
                out=d["y_d"].rearrange("oc p t -> p oc t")[:, :, p * T:(p + 1) * T],
                in_=p4_yb.rearrange("p (b oc t) -> p b oc t", b=2, oc=NOC)[:, ip],
            ) if e else None)
            P.inc(i, "s_ysta" if ip == 0 else "s_ystb", 16)
    yst_a = S["s_ysta"]
    yst_b = S["s_ystb"]
    pe4_done = S["c_pe"]
    act4_done = S["c_act"]
    dve4_done = S["c_dve"]
    pool4_done = S["c_pool"]

    p5_x, p5_y, p5_sq = d["p5_x"], d["p5_y"], d["p5_sq"]
    p5_st, p5_ab, p5_otm = d["p5_st"], d["p5_ab"], d["p5_otm"]
    ps5_s, ps5_c, ps5_t = d["ps5_s"], d["ps5_c"], d["ps5_t"]

    x5_free = [0, 0]
    x5_done = [0, 0]

    def p5_load(panel):
        par = panel % 2
        sem = "d_x5a" if par == 0 else "d_x5b"
        P.wait("sync", "s_ysta", yst_a)
        P.wait("sync", "s_ystb", yst_b)
        P.wait("sync", "s_x32", x32_all)
        P.wait("sync", "c_pe", x5_free[par])
        if panel <= 1:
            P.wait("sync", "c_pe", pe4_done)
            P.wait("sync", "c_act", act4_done)
            P.wait("sync", "c_pool", pool4_done)
            P.wait("sync", "c_dve", dve4_done)
        e = P.eng if P.on("sync") else None
        i = (e.dma_start(
            out=p5_x.rearrange("p (b kc t) -> p b kc t", b=2, kc=NKC)[:, par],
            in_=d["xln32_d"].rearrange("kc p t -> p kc t")[:, :, panel * T:(panel + 1) * T],
        ) if e else None)
        P.inc(i, sem, 16)
        i = (e.dma_start(
            out=p5_y.rearrange("p (b oc t) -> p b oc t", b=2, oc=NOC)[:, par],
            in_=d["y_d"].rearrange("oc p t -> p oc t")[:, :, panel * T:(panel + 1) * T],
        ) if e else None)
        P.inc(i, sem, 16)
        x5_done[par] = S[sem]

    otm_free = [0, 0]
    tp_evac = [0, 0, 0, 0]
    norm_done_by = [0, 0]
    c_pe_end = [0, 0]

    def emit_C(panel):
        par = panel % 2
        sem = "d_x5a" if par == 0 else "d_x5b"
        need = x5_done[par]
        for cc in range(NKC):
            P.wait("vector", sem, need)
            dd = (P.eng.tensor_tensor(
                p5_x[:, (par * NKC + cc) * T:(par * NKC + cc + 1) * T],
                p5_x[:, (par * NKC + cc) * T:(par * NKC + cc + 1) * T],
                p5_y[:, (par * NKC + cc) * T:(par * NKC + cc + 1) * T], ALU.add)
                if P.on("vector") else None)
            P.step("vector", dd)
        mm = None
        for cc in range(NKC):
            dd = (P.eng.tensor_tensor(
                p5_sq[:, :],
                p5_x[:, (par * NKC + cc) * T:(par * NKC + cc + 1) * T],
                p5_x[:, (par * NKC + cc) * T:(par * NKC + cc + 1) * T], ALU.mult)
                if P.on("vector") else None)
            sqv = P.step("vector", dd)
            P.wait("tensor", "c_dve", sqv)
            if P.on("tensor"):
                mm = P.eng.matmul(ps5_s[:, 0:T], ones_c_f[:, :],
                                  p5_x[:, (par * NKC + cc) * T:(par * NKC + cc + 1) * T],
                                  start=(cc == 0), stop=(cc == NKC - 1),
                                  skip_group_check=True)
                mm = P.eng.matmul(ps5_s[:, T:2 * T], ones_c_f[:, :], p5_sq[:, :],
                                  start=(cc == 0), stop=(cc == NKC - 1),
                                  skip_group_check=True)
            pe_st = P.inc(mm, "c_pe", 1)
            P.wait("vector", "c_pe", pe_st)
        pe_stats = S["c_pe"]
        P.wait("scalar", "c_pe", pe_stats)
        a = (P.eng.activation(p5_st[:, 0:T], ps5_s[:, 0:T], AF.Identity, scale=1.0 / C)
             if P.on("scalar") else None)
        P.step("scalar", a)
        a = (P.eng.activation(p5_st[:, T:2 * T], ps5_s[:, T:2 * T], AF.Identity,
                              scale=1.0 / C) if P.on("scalar") else None)
        av = P.step("scalar", a)
        P.wait("vector", "c_act", av)
        dd = (P.eng.tensor_tensor(p5_st[:, 2 * T:3 * T], p5_st[:, 0:T], p5_st[:, 0:T],
                                  ALU.mult) if P.on("vector") else None)
        P.step("vector", dd)
        dd = (P.eng.scalar_tensor_tensor(p5_st[:, 2 * T:3 * T], p5_st[:, T:2 * T],
                                         1e-5, p5_st[:, 2 * T:3 * T],
                                         ALU.add, ALU.subtract)
              if P.on("vector") else None)
        dvar = P.step("vector", dd)
        P.wait("scalar", "c_dve", dvar)
        a = (P.eng.activation(p5_st[:, 2 * T:3 * T], p5_st[:, 2 * T:3 * T], AF.Ln)
             if P.on("scalar") else None)
        P.step("scalar", a)
        a = (P.eng.activation(p5_st[:, 2 * T:3 * T], p5_st[:, 2 * T:3 * T], AF.Exp,
                              scale=-0.5) if P.on("scalar") else None)
        arstd = P.step("scalar", a)
        P.wait("vector", "c_act", arstd)
        dd = (P.eng.tensor_tensor(p5_st[:, 3 * T:4 * T], p5_st[:, 0:T],
                                  p5_st[:, 2 * T:3 * T], ALU.mult)
              if P.on("vector") else None)
        P.step("vector", dd)
        dd = (P.eng.tensor_scalar(p5_st[:, 3 * T:4 * T], p5_st[:, 3 * T:4 * T], -1.0,
                                  None, ALU.mult) if P.on("vector") else None)
        dbsh = P.step("vector", dd)
        P.wait("tensor", "c_dve", dbsh)
        mm = (P.eng.matmul(ps5_c[:, :], ones_r_f[:, :], p5_st[:, 2 * T:3 * T],
                           start=True, stop=True, skip_group_check=True)
              if P.on("tensor") else None)
        pe_a = P.inc(mm, "c_pe", 1)
        P.wait("scalar", "c_pe", pe_a)
        a = (P.eng.activation(p5_ab[:, 0:T], ps5_c[:, :], AF.Identity)
             if P.on("scalar") else None)
        aa = P.step("scalar", a)
        P.wait("tensor", "c_act", aa)
        mm = (P.eng.matmul(ps5_c[:, :], ones_r_f[:, :], p5_st[:, 3 * T:4 * T],
                           start=True, stop=True, skip_group_check=True)
              if P.on("tensor") else None)
        pe_b = P.inc(mm, "c_pe", 1)
        P.wait("scalar", "c_pe", pe_b)
        a = (P.eng.activation(p5_ab[:, T:2 * T], ps5_c[:, :], AF.Identity)
             if P.on("scalar") else None)
        ab_ready = P.step("scalar", a)
        for cc in range(NKC):
            P.wait("vector", "c_act", ab_ready)
            dd = (P.eng.tensor_tensor(
                p5_x[:, (par * NKC + cc) * T:(par * NKC + cc + 1) * T],
                p5_x[:, (par * NKC + cc) * T:(par * NKC + cc + 1) * T],
                p5_ab[:, 0:T], ALU.mult) if P.on("vector") else None)
            P.step("vector", dd)
            dd = (P.eng.tensor_tensor(
                p5_x[:, (par * NKC + cc) * T:(par * NKC + cc + 1) * T],
                p5_x[:, (par * NKC + cc) * T:(par * NKC + cc + 1) * T],
                p5_ab[:, T:2 * T], ALU.add) if P.on("vector") else None)
            norm_done_by[par] = P.step("vector", dd)

    def emit_D(panel):
        par = panel % 2
        t0 = panel * T
        nd = norm_done_by[par]
        for j in range(4):
            jp = j % 2
            for cc in range(NKC):
                tb = cc % 4
                P.wait("tensor", "c_dve", nd)
                P.wait("tensor", "c_dve", tp_evac[tb])
                mm = (P.eng.transpose(
                    ps5_t[:, tb * 512:tb * 512 + 128],
                    p5_x[:, (par * NKC + cc) * T + j * 128:
                         (par * NKC + cc) * T + (j + 1) * 128],
                    d["ident_sb"][:, :]) if P.on("tensor") else None)
                pe_t = P.inc(mm, "c_pe", 1)
                P.wait("vector", "c_pe", pe_t)
                osem = "s_outa" if jp == 0 else "s_outb"
                P.wait("vector", osem, otm_free[jp])
                dd = (P.eng.tensor_copy(
                    p5_otm[:, jp * C + cc * 128:jp * C + (cc + 1) * 128],
                    ps5_t[:, tb * 512:tb * 512 + 128]) if P.on("vector") else None)
                tp_evac[tb] = P.step("vector", dd)
            P.wait("sync", "c_dve", tp_evac[(NKC - 1) % 4])
            e = P.eng if P.on("sync") else None
            i = (e.dma_start(out=d["out"][t0 + j * 128:t0 + (j + 1) * 128, :],
                             in_=p5_otm[:, jp * C:(jp + 1) * C]) if e else None)
            otm_free[jp] = P.inc(i, "s_outa" if jp == 0 else "s_outb", 16)
        x5_free[par] = S["c_pe"]

    p5_load(0)
    emit_C(0)
    for panel in range(1, NP3):
        p5_load(panel)
        emit_D(panel - 1)
        emit_C(panel)
    emit_D(NP3 - 1)

    P.wait("sync", "s_outa", S["s_outa"])
    P.wait("sync", "s_outb", S["s_outb"])


def logits_sb_view(d):
    return d["logits_sb"]


def build_program():
    nc = bass.Bass("TRN2", target_bir_lowering=False, debug=False)
    d = {"nc": nc}

    def dram(nm, shp, dt, kind):
        d[nm] = nc.dram_tensor(nm, shp, dt, kind=kind).ap()

    dram("xseq", [N, C], F32, "ExternalInput")
    dram("xpad", [NOWN + 32, C], F32, "ExternalInput")
    dram("wqkT", [C, 2 * C], F32, "ExternalInput")
    dram("bqk_q", [128, 8], F32, "ExternalInput")
    dram("bqk_qn", [128, 8], F32, "ExternalInput")
    dram("bqk_k", [1, C], F32, "ExternalInput")
    dram("lepe_wp", [128, 24], F32, "ExternalInput")
    dram("lepe_bp", [128, 8], F32, "ExternalInput")
    dram("lepe_bt_h", [128, NKC * T], BF16, "ExternalInput")
    dram("lepe_w1p", [128, 8], F32, "ExternalInput")
    dram("wg", [128, 64], F32, "ExternalInput")
    dram("bgate", [8, 1], F32, "ExternalInput")
    dram("b1p", [128, 256], F32, "ExternalInput")
    dram("b2r", [8, C], F32, "ExternalInput")
    dram("w1", [NE, C, MLP], F32, "ExternalInput")
    dram("w2", [NE, MLP, C], F32, "ExternalInput")
    dram("ident", [128, 128], F32, "ExternalInput")
    dram("maskKM", [128, 128], F32, "ExternalInput")
    dram("selz", [16, C], BF16, "ExternalInput")
    dram("selbc", [8, C], BF16, "ExternalInput")
    dram("ident8", [8, 8], BF16, "ExternalInput")
    dram("ident64", [64, 64], BF16, "ExternalInput")
    dram("unit8", [1, 64], F32, "ExternalInput")
    dram("out", [NOWN, C], F32, "ExternalOutput")
    dram("xseq_bf", [N, C], BF16, "Internal")
    dram("xpad_bf", [NOWN + 32, C], BF16, "Internal")
    dram("xln32_d", [NKC, 128, NOWN], F32, "Internal")
    dram("xlnbf_d", [NKC, 128, NOWN], BF16, "Internal")
    dram("y_d", [NOC, 128, NOWN], BF16, "Internal")

    BF_ELEMS = 67 * 1024
    F32_ELEMS = 18 * 1024

    import contextlib
    with contextlib.ExitStack() as ctx:
        arena_bf = ctx.enter_context(nc.sbuf_tensor("arena_bf", [128, BF_ELEMS], BF16))
        arena_f = ctx.enter_context(nc.sbuf_tensor("arena_f", [128, F32_ELEMS], F32))
        cb = Carver(arena_bf, BF_ELEMS)
        cf = Carver(arena_f, F32_ELEMS)

        d["kv_bf"] = cb.take(1024, 64)
        d["kv2_bf"] = cb.take(1024)
        d["ident64_sb"] = cb.take(64, 64)
        d["g_bf"] = cb.take(4096, 8)
        d["selz_sb"] = cb.take(1024, 16)
        d["selbc_sb"] = cb.take(1024, 8)
        d["ident8_sb"] = cb.take(16, 8)[:, 0:8]
        d["KM_bf"] = cb.take(128)
        d["ones_r_bf"] = cb.take(128, 1)
        d["ones_c_bf"] = cb.take(16)[:, 0:1]
        d["one1_bf"] = cb.take(16, 1)[:, 0:1]
        d["bqkk_sb"] = cb.take(1024, 1)
        d["ident_sb"] = cf.take(128)
        d["maskKM_sb"] = cf.take(128)
        d["unit8_sb"] = cf.take(64, 1)
        d["ones_r_f"] = cf.take(128, 1)
        d["ones_c_f"] = cf.take(16)[:, 0:1]
        d["bqk_q_sb"] = cf.take(16)[:, 0:8]
        d["bqk_qn_sb"] = cf.take(16)[:, 0:8]
        d["lepe_w_sb"] = cf.take(32)[:, 0:24]
        d["lepe_b_sb"] = cf.take(16)[:, 0:8]
        d["lepe_w1p_sb"] = cf.take(16)[:, 0:8]
        d["wg_sb"] = cf.take(64)
        d["bgate_sb"] = cf.take(16, 8)[:, 0:1]
        d["b1p_sb"] = cf.take(256)
        d["kme_pm"] = cf.take(16)[:, 0:8]
        d["zden_b"] = cf.take(16, 16)[:, 0:1]
        d["kmean_r"] = cf.take(1024, 1)
        d["logits_sb"] = cf.take(512)

        ov_bf = cb.mark()
        ov_f = cf.mark()
        d["wqk_sb"] = cb.take(NKC * 1024)
        d["lepe_bt"] = cb.take(NKC * T)
        p13_bf = cb.mark()
        d["p1_xfm"] = cb.take(2 * NKC * T)
        d["p1_xtm"] = cb.take(2 * 4 * C)
        d["p1_ktm"] = cb.take(2 * C)
        cb_p1_end = cb.mark()
        cb.reset(p13_bf)
        d["p3_xe"] = cb.take(2 * NKC * 544)
        d["p3_qbf"] = cb.take(NKC * T)
        d["p3_zbf"] = cb.take(512, 16)
        d["p3_zb"] = cb.take(2 * T)
        cb_p3_end = cb.mark()
        p13_f = cf.mark()
        d["p1_r"] = cf.take(4 * T)
        d["p1_m"] = cf.take(4 * T)
        cf_p1_end = cf.mark()
        cf.reset(p13_f)
        d["p3_r"] = cf.take(2 * T)
        d["p3_m"] = cf.take(2 * T)
        d["p3_z"] = cf.take(T, 16)
        d["p3_t4"] = cf.take(2 * T)
        d["p3_ln"] = cf.take(2 * NKC * T)
        d["p3_sq"] = cf.take(T)
        d["p3_st"] = cf.take(4 * T, 1)
        d["p3_ab"] = cf.take(2 * T)
        d["p3_lt"] = cf.take(32)
        d["p3_gs"] = cf.take(96)
        cf_p3_end = cf.mark()
        cb.reset(ov_bf)
        d["p4_w1"] = cb.take(2 * NKC * QM)
        d["p4_w2"] = cb.take(2 * NMC * C)
        d["p4_xp"] = cb.take(2 * NKC * T)
        d["p4_h"] = cb.take(NMC * T)
        d["p4_gb"] = cb.take(2 * T)
        d["p4_gr"] = cb.take(2 * T, 1)
        d["p4_b2"] = cb.take(2 * C, 1)
        d["p4_yb"] = cb.take(2 * NOC * T)
        cb_p4_end = cb.mark()
        cb.reset(ov_bf)
        d["p5_y"] = cb.take(2 * NOC * T)
        cb_p5_end = cb.mark()
        cf.reset(ov_f)
        d["p4_hr"] = cf.take(2 * T)
        cf_p4_end = cf.mark()
        cf.reset(ov_f)
        d["p5_x"] = cf.take(2 * NKC * T)
        d["p5_sq"] = cf.take(T)
        d["p5_st"] = cf.take(4 * T, 1)
        d["p5_ab"] = cf.take(2 * T)
        d["p5_otm"] = cf.take(2 * C)
        cf_p5_end = cf.mark()
        assert max(cb_p1_end, cb_p3_end, cb_p4_end, cb_p5_end) <= BF_ELEMS, (
            cb_p1_end, cb_p3_end, cb_p4_end, cb_p5_end)
        assert max(cf_p1_end, cf_p3_end, cf_p4_end, cf_p5_end) <= F32_ELEMS, (
            cf_p1_end, cf_p3_end, cf_p4_end, cf_p5_end)

        with contextlib.ExitStack() as pctx:
            psA = pctx.enter_context(nc.psum_tensor("psA", [128, 4096], F32))
            d["ps_kv"] = psA[0:64, 0:1024]
            d["ps_ks"] = psA[0:1, 1024:2048]
            d["ps_kt"] = psA[:, 2048:4096]
            d["ps_sm"] = psA[:, 3072:3584]
            d["ps3_a"] = psA[:, 0:512]
            d["ps3_b"] = psA[:, 512:1024]
            d["ps3_c"] = psA[:, 1024:1536]
            d["ps3_d"] = psA[:, 1536:2048]
            d["ps3_e"] = psA[:, 2048:2560]
            d["ps3_s"] = psA[0:1, 2560:3584]
            d["ps4"] = psA[:, 0:4096]
            d["ps5_s"] = psA[0:1, 0:1024]
            d["ps5_c"] = psA[:, 1024:1536]
            d["ps5_t"] = psA[:, 1536:3584]

            sems = {}
            with contextlib.ExitStack() as sctx:
                for nm in SEM_NAMES:
                    sems[nm] = sctx.enter_context(nc.semaphore(nm))
                with nc.Block() as block:

                    @block.sync
                    def _(eng):
                        _emit(Walk("sync", eng, sems), d)

                    @block.vector
                    def _(eng):
                        _emit(Walk("vector", eng, sems), d)

                    @block.scalar
                    def _(eng):
                        _emit(Walk("scalar", eng, sems), d)

                    @block.tensor
                    def _(eng):
                        _emit(Walk("tensor", eng, sems), d)

                    @block.gpsimd
                    def _(eng):
                        _emit(Walk("gpsimd", eng, sems), d)
    return nc


def _prep_inputs(inputs):
    x = np.asarray(inputs["L0_attn_x"], dtype=np.float32)
    Wqk = np.asarray(inputs["L0_attn_W_qk"], dtype=np.float32)
    bqk = np.asarray(inputs["L0_attn_b_qk"], dtype=np.float32)
    lw = np.asarray(inputs["L0_attn_lepe_w"], dtype=np.float32)
    lb = np.asarray(inputs["L0_attn_lepe_b"], dtype=np.float32)
    wgate = np.asarray(inputs["L0_ffn_w_gate"], dtype=np.float32)
    bg = np.asarray(inputs["L0_ffn_b_gate"], dtype=np.float32)
    W1 = np.ascontiguousarray(np.asarray(inputs["L0_ffn_W1"], dtype=np.float32))
    b1 = np.asarray(inputs["L0_ffn_b1"], dtype=np.float32)
    W2 = np.ascontiguousarray(np.asarray(inputs["L0_ffn_W2"], dtype=np.float32))
    b2 = np.asarray(inputs["L0_ffn_b2"], dtype=np.float32)

    common = dict(
        wqkT=np.ascontiguousarray(Wqk.T),
        bqk_q=np.ascontiguousarray(bqk[:C].reshape(8, 128).T),
        bqk_qn=np.ascontiguousarray(-bqk[:C].reshape(8, 128).T),
        bqk_k=np.ascontiguousarray(bqk[C:].reshape(1, C)),
        lepe_wp=np.ascontiguousarray(lw.reshape(8, 128, 3).transpose(1, 0, 2).reshape(128, 24)),
        lepe_bp=np.ascontiguousarray(lb.reshape(8, 128).T),
        lepe_bt_h=np.ascontiguousarray(
            np.repeat(lb.reshape(8, 128).T.astype(BF)[:, :, None], T, axis=2
                      ).reshape(128, NKC * T)),
        lepe_w1p=np.ascontiguousarray(
            (lw.reshape(8, 128, 3).transpose(1, 0, 2)[:, :, 1] + 1.0).reshape(128, 8)),
        wg=np.ascontiguousarray(wgate.reshape(8, 128, 8).transpose(1, 0, 2).reshape(128, 64)),
        bgate=np.ascontiguousarray(bg.reshape(8, 1)),
        b1p=np.ascontiguousarray(b1.reshape(8, 32, 128).transpose(2, 0, 1).reshape(128, 256)),
        b2r=np.ascontiguousarray(b2),
        w1=W1, w2=W2,
        ident=np.eye(128, dtype=np.float32),
    )
    cc_idx = np.arange(128)[None, :] // 64
    hrow = np.arange(16)[:, None]
    selz = np.zeros((16, C), dtype=BF)
    maskKM = np.zeros((128, 128), dtype=np.float32)
    for cc in range(8):
        selz[:, cc * 128:(cc + 1) * 128] = (hrow == (2 * cc + cc_idx)).astype(BF)
        maskKM[:, cc * 16:(cc + 1) * 16] = (
            np.arange(16)[None, :] == (2 * cc + np.arange(128)[:, None] // 64)
        ).astype(np.float32)
    selbc = np.zeros((8, C), dtype=BF)
    for e in range(8):
        selbc[e, e * 128:(e + 1) * 128] = BF(1.0)
    unit8 = np.zeros((1, 64), dtype=np.float32)
    for cc in range(8):
        unit8[0, cc * 8 + cc] = 1.0
    common.update(
        maskKM=maskKM, selz=selz, selbc=selbc,
        ident8=np.eye(8, dtype=np.float32).astype(BF),
        ident64=np.eye(64, dtype=np.float32).astype(BF), unit8=unit8,
    )
    in_maps = []
    for core in range(8):
        s, half = divmod(core, 2)
        own = x[s, half * NOWN:(half + 1) * NOWN]
        prev = (x[s, NOWN - 16:NOWN] if half == 1
                else np.zeros((16, C), np.float32))
        nxt = (x[s, NOWN:NOWN + 16] if half == 0
               else np.zeros((16, C), np.float32))
        m = dict(common)
        m["xseq"] = np.ascontiguousarray(x[s])
        m["xpad"] = np.ascontiguousarray(np.concatenate([prev, own, nxt], axis=0))
        in_maps.append(m)
    return in_maps


def kernel(**inputs):
    in_maps = _prep_inputs(inputs)
    nc = build_program()
    res = run_bass_kernel_spmd(nc, in_maps, core_ids=list(range(8)))
    full = np.empty((B, N, C), dtype=np.float32)
    for core in range(8):
        s, half = divmod(core, 2)
        full[s, half * NOWN:(half + 1) * NOWN] = res.results[core]["out"]
    return full


# version 06
# speedup vs baseline: 1.0965x; 1.0965x over previous
import numpy as np
import ml_dtypes
import concourse.bass as bass
import concourse.mybir as mybir
from concourse.bass_utils import run_bass_kernel_spmd

F32 = mybir.dt.float32
BF16 = mybir.dt.bfloat16
AF = mybir.ActivationFunctionType
ALU = mybir.AluOpType
AX = mybir.AxisListType.X
BF = ml_dtypes.bfloat16

B, N, C = 4, 8192, 1024
H = 16
NOWN = 4096
T = 512
NP1 = N // T
NP3 = NOWN // T
NKC = 8
NE, NQ, NMC, NOC = 8, 4, 8, 8
MLP = 4096
QM = MLP // NQ

SEM_NAMES = [
    "d_stage", "d_stgg", "d_wqk",
    "d_x1a", "d_x1b", "d_t1a", "d_t1b",
    "d_x3a", "d_x3b",
    "d_xp4a", "d_xp4b", "d_w4a", "d_w4b",
    "d_yla", "d_ylb", "d_b2a", "d_b2b",
    "s_ysta", "s_ystb", "s_xlw", "s_x32",
    "d_x5a", "d_x5b", "s_outa", "s_outb",
    "c_pe", "c_act", "c_dve", "c_pool",
]


class Walk:
    def __init__(self, who, eng, sems):
        self.who = who
        self.eng = eng
        self.sems = sems
        self.S = {k: 0 for k in sems}

    def inc(self, instr, sem, k=1):
        if instr is not None:
            instr.then_inc(self.sems[sem], k)
        self.S[sem] += k
        return self.S[sem]

    def wait(self, who, sem, val):
        if val <= 0:
            return
        if self.who == who:
            self.eng.wait_ge(self.sems[sem], val)

    def on(self, who):
        return self.who == who

    OWN = {"vector": "c_dve", "scalar": "c_act", "gpsimd": "c_pool",
           "tensor": "c_pe"}

    def step(self, who, instr):
        sem = self.OWN[who]
        v = self.inc(instr, sem, 1)
        self.wait(who, sem, v)
        return v


class Carver:

    def __init__(self, arena, elems):
        self.arena = arena
        self.total = elems
        self.off = 0
        self.high = 0

    def take(self, elems, parts=128):
        off = (self.off + 15) & ~15
        self.off = off + elems
        self.high = max(self.high, self.off)
        assert self.off <= self.total, (self.off, self.total)
        return self.arena[0:parts, off:off + elems]

    def mark(self):
        return self.off

    def reset(self, off):
        self.off = off


def _emit(P, tn):
    S = P.S
    nc = tn["nc"]
    d = tn

    e = P.eng if P.on("sync") else None
    for dst, src in [
        (d["selz_sb"], d["selz"]), (d["selbc_sb"], d["selbc"]),
        (d["ident8_sb"], d["ident8"]), (d["ident_sb"], d["ident"]),
        (d["maskKM_sb"], d["maskKM"]), (d["unit8_sb"], d["unit8"]),
        (d["bqk_q_sb"], d["bqk_q"]), (d["bqk_qn_sb"], d["bqk_qn"]),
        (d["lepe_w_sb"], d["lepe_wp"]),
        (d["lepe_b_sb"], d["lepe_bp"]), (d["lepe_bt"], d["lepe_bt_h"]),
        (d["lepe_w1p_sb"], d["lepe_w1p"]), (d["wg_sb"], d["wg"]),
        (d["bgate_sb"], d["bgate"]), (d["b1p_sb"], d["b1p"]),
        (d["ident64_sb"], d["ident64"]),
    ]:
        i = e.dma_start(out=dst[:, :], in_=src[:, :]) if e else None
        P.inc(i, "d_stage", 16)
    g = P.eng if P.on("gpsimd") else None
    i = g.dma_start(out=d["xseq_bf"][:, :], in_=d["xseq"][:, :]) if g else None
    P.inc(i, "d_stgg", 16)
    i = g.dma_start(out=d["xpad_bf"][:, :], in_=d["xpad"][:, :]) if g else None
    P.inc(i, "d_stgg", 16)
    i = g.dma_start(out=d["bqkk_sb"][:, :], in_=d["bqk_k"][:, :]) if g else None
    P.inc(i, "d_stgg", 16)
    i = (g.dma_start(
        out=d["wqk_sb"].rearrange("p (kc f) -> p kc f", kc=NKC),
        in_=d["wqkT"].rearrange("(kc p) f -> p kc f", p=128)[:, :, C:2 * C],
    ) if g else None)
    P.inc(i, "d_wqk", 16)
    d_stage_all = S["d_stage"]
    d_stgg_all = S["d_stgg"]

    for dst, val in [(d["ones_r_bf"], 1.0), (d["ones_c_bf"], 1.0),
                     (d["one1_bf"], 1.0), (d["ones_r_f"], 1.0),
                     (d["ones_c_f"], 1.0), (d["logits_sb"], 0.0)]:
        ms = P.eng.memset(dst[:, :], val) if P.on("vector") else None
        v_const = P.step("vector", ms)

    wqk_sb = d["wqk_sb"]
    ones_r_bf, ones_c_bf, one1_bf = d["ones_r_bf"], d["ones_c_bf"], d["one1_bf"]
    ones_r_f, ones_c_f = d["ones_r_f"], d["ones_c_f"]

    p1_xfm, p1_xtm, p1_ktm = d["p1_xfm"], d["p1_xtm"], d["p1_ktm"]
    p1_r, p1_m = d["p1_r"], d["p1_m"]
    ps_kv, ps_ks, ps_kt, ps_sm = d["ps_kv"], d["ps_ks"], d["ps_kt"], d["ps_sm"]

    xfm_free = [0, 0]
    xfm_done = [0, 0]
    xtm_free = [0, 0]
    xtm_done = [0, 0]
    elu_done = [0, 0, 0, 0]
    ktm_free = [0, 0]
    first_ks = [True, True]
    first_kv = [True, True]
    reg = 0
    pend_kv = []

    def p1_load(panel):
        par = panel % 2
        sem = "d_x1a" if par == 0 else "d_x1b"
        t0 = panel * T
        P.wait("sync", "d_stage", d_stage_all)
        P.wait("sync", "d_stgg", d_stgg_all)
        P.wait("sync", "c_pe", xfm_free[par])
        e = P.eng if P.on("sync") else None
        for cc in range(NKC):
            i = (e.dma_start_transpose(
                p1_xfm[:, (par * NKC + cc) * T:(par * NKC + cc + 1) * T],
                d["xseq_bf"][t0:t0 + T, cc * 128:(cc + 1) * 128],
            ) if e else None)
            P.inc(i, sem, 16)
        xfm_done[par] = S[sem]

    def p1_load_tm(panel):
        par = panel % 2
        sem = "d_t1a" if par == 0 else "d_t1b"
        P.wait("gpsimd", "d_stage", d_stage_all)
        P.wait("gpsimd", "d_stgg", d_stgg_all)
        P.wait("gpsimd", "c_pe", xtm_free[par])
        e = P.eng if P.on("gpsimd") else None
        i = (e.dma_start(
            out=p1_xtm.rearrange("p (b j f) -> p b j f", b=2, j=4)[:, par],
            in_=d["xseq_bf"].rearrange("(pp j p) f -> pp p j f", j=4, p=128)[panel],
        ) if e else None)
        P.inc(i, sem, 16)
        xtm_done[par] = S[sem]

    p1_load(0)
    p1_load_tm(0)
    for panel in range(NP1):
        par = panel % 2
        semf = "d_x1a" if par == 0 else "d_x1b"
        semt = "d_t1a" if par == 0 else "d_t1b"
        need_f = xfm_done[par]
        need_t = xtm_done[par]
        if panel + 1 < NP1:
            p1_load(panel + 1)
            p1_load_tm(panel + 1)
        pe_j = S["c_pe"]
        for j in range(4):
            jp = j % 2
            for half in range(2):
                bpar = reg % 4
                reg += 1
                P.wait("tensor", semf, need_f)
                if panel == 0 and j == 0:
                    P.wait("tensor", "d_wqk", 16)
                    P.wait("tensor", "d_stage", d_stage_all)
                    P.wait("tensor", "d_stgg", d_stgg_all)
                    P.wait("tensor", "c_dve", v_const)
                P.wait("tensor", "c_dve", elu_done[bpar])
                mm = None
                for kc in range(NKC):
                    if P.on("tensor"):
                        mm = P.eng.matmul(
                            ps_kt[:, bpar * T:(bpar + 1) * T],
                            p1_xfm[:, (par * NKC + kc) * T + j * 128:
                                   (par * NKC + kc) * T + (j + 1) * 128],
                            wqk_sb[:, kc * 1024 + half * T:kc * 1024 + (half + 1) * T],
                            start=(kc == 0), stop=False, skip_group_check=True)
                if P.on("tensor"):
                    mm = P.eng.matmul(
                        ps_kt[:, bpar * T:(bpar + 1) * T],
                        ones_r_bf[:, :], d["bqkk_sb"][:, half * T:(half + 1) * T],
                        start=False, stop=True, skip_group_check=True)
                pe_reg = P.inc(mm, "c_pe", 1)

                P.wait("scalar", "c_pe", pe_reg)
                P.wait("scalar", "c_dve", elu_done[bpar])
                a1 = (P.eng.activation(p1_r[:, bpar * T:(bpar + 1) * T],
                                       ps_kt[:, bpar * T:(bpar + 1) * T], AF.Relu)
                      if P.on("scalar") else None)
                P.step("scalar", a1)
                a2 = (P.eng.activation(p1_m[:, bpar * T:(bpar + 1) * T],
                                       ps_kt[:, bpar * T:(bpar + 1) * T], AF.Relu,
                                       scale=-1.0)
                      if P.on("scalar") else None)
                P.step("scalar", a2)
                a3 = (P.eng.activation(p1_m[:, bpar * T:(bpar + 1) * T],
                                       p1_m[:, bpar * T:(bpar + 1) * T], AF.Exp,
                                       scale=-1.0)
                      if P.on("scalar") else None)
                a3v = P.step("scalar", a3)
                P.wait("vector", "c_act", a3v)
                P.wait("vector", "c_pe", ktm_free[jp])
                d2 = (P.eng.tensor_tensor(
                    p1_ktm[:, jp * C + half * T:jp * C + (half + 1) * T],
                    p1_r[:, bpar * T:(bpar + 1) * T],
                    p1_m[:, bpar * T:(bpar + 1) * T], ALU.add)
                    if P.on("vector") else None)
                elu_done[bpar] = P.step("vector", d2)

            pend_kv.append((panel, j, jp, elu_done[(reg - 2) % 4],
                            elu_done[(reg - 1) % 4], xtm_done[par]))
            if len(pend_kv) >= 2 or j == 3:
                todo = pend_kv if j == 3 else [pend_kv.pop(0)]
                for (kp, kj, kjp, ed0, ed1, xtd) in todo:
                    kpar = kp % 2
                    P.wait("tensor", "c_dve", ed0)
                    P.wait("tensor", "c_dve", ed1)
                    P.wait("tensor", "d_t1a" if kpar == 0 else "d_t1b", xtd)
                    mm = None
                    for half in range(2):
                        if P.on("tensor"):
                            mm = P.eng.matmul(
                                ps_ks[:, half * T:(half + 1) * T],
                                ones_c_bf[:, :],
                                p1_ktm[:, kjp * C + half * T:kjp * C + (half + 1) * T],
                                start=first_ks[half], stop=False,
                                skip_group_check=True)
                        first_ks[half] = False
                    for hh in range(H):
                        bank = hh // 8
                        st = first_kv[bank] and (hh % 8 == 0)
                        if P.on("tensor"):
                            mm = P.eng.matmul(
                                ps_kv[:, hh * 64:(hh + 1) * 64],
                                p1_ktm[:, kjp * C + hh * 64:kjp * C + hh * 64 + 64],
                                p1_xtm[:, (kpar * 4 + kj) * C + hh * 64:
                                       (kpar * 4 + kj) * C + hh * 64 + 64],
                                start=st, stop=False, skip_group_check=True)
                        if hh % 8 == 0:
                            first_kv[bank] = False
                    ktm_free[kjp] = pe_j = P.inc(mm, "c_pe", 1)
                    if kj == 3:
                        xtm_free[kpar] = pe_j
                        xfm_free[kpar] = pe_j
                if j == 3:
                    pend_kv.clear()
    pe1_done = S["c_pe"]

    P.wait("gpsimd", "c_pe", pe1_done)
    i = (P.eng.dma_start(
        out=wqk_sb.rearrange("p (kc f) -> p kc f", kc=NKC),
        in_=d["wqkT"].rearrange("(kc p) f -> p kc f", p=128)[:, :, 0:C],
    ) if P.on("gpsimd") else None)
    P.inc(i, "d_wqk", 16)
    wqk_q_ready = S["d_wqk"]

    P.wait("scalar", "c_pe", pe1_done)
    a = (P.eng.activation(d["kv_bf"][:, :], ps_kv[:, :], AF.Identity, scale=1.0 / N)
         if P.on("scalar") else None)
    P.step("scalar", a)
    a = (P.eng.activation(d["kmean_r"][:, :], ps_ks[:, :], AF.Identity, scale=1.0 / N)
         if P.on("scalar") else None)
    act_km = P.step("scalar", a)

    P.wait("tensor", "c_act", act_km)
    mm = None
    for hf in range(2):
        if P.on("tensor"):
            mm = P.eng.matmul(
                ps_kt[64:128, hf * T:(hf + 1) * T],
                d["ident64_sb"][:, :],
                d["kv_bf"][:, hf * T:(hf + 1) * T],
                start=True, stop=True, skip_group_check=True)
    pe_kv2 = P.inc(mm, "c_pe", 1)
    P.wait("scalar", "c_pe", pe_kv2)
    a = (P.eng.activation(d["kv2_bf"][64:128, :], ps_kt[64:128, 0:1024], AF.Identity)
         if P.on("scalar") else None)
    P.step("scalar", a)
    mm = None
    for cc in range(NKC):
        if P.on("tensor"):
            mm = P.eng.matmul(
                ps_sm[:, 0:8],
                d["kmean_r"][:, cc * 128:(cc + 1) * 128],
                d["unit8_sb"][:, cc * 8:(cc + 1) * 8],
                start=(cc == 0), stop=(cc == NKC - 1), skip_group_check=True)
    pe_kme = P.inc(mm, "c_pe", 1)
    P.wait("scalar", "c_pe", pe_kme)
    a = (P.eng.activation(d["kme_pm"][:, :], ps_sm[:, 0:8], AF.Identity)
         if P.on("scalar") else None)
    act_kme = P.step("scalar", a)
    P.wait("vector", "c_act", act_kme)
    for cc in range(NKC):
        dd = (P.eng.tensor_scalar(
            d["KM_bf"][:, cc * 16:(cc + 1) * 16],
            d["maskKM_sb"][:, cc * 16:(cc + 1) * 16],
            d["kme_pm"][:, cc:cc + 1], None, ALU.mult)
            if P.on("vector") else None)
        dve_km = P.step("vector", dd)
    P.wait("tensor", "c_dve", dve_km)
    mm = None
    for cc in range(NKC):
        if P.on("tensor"):
            mm = P.eng.matmul(
                ps_sm[0:16, 8:9],
                d["KM_bf"][:, cc * 16:(cc + 1) * 16],
                ones_c_bf[:, :],
                start=(cc == 0), stop=(cc == NKC - 1), skip_group_check=True)
    pe_zb = P.inc(mm, "c_pe", 1)
    P.wait("scalar", "c_pe", pe_zb)
    a = (P.eng.activation(d["zden_b"][:, :], ps_sm[0:16, 8:9], AF.Identity)
         if P.on("scalar") else None)
    act_zb = P.step("scalar", a)
    P.wait("vector", "c_act", act_zb)
    dd = (P.eng.tensor_scalar(d["zden_b"][:, :], d["zden_b"][:, :], 1e-6, None,
                              ALU.add) if P.on("vector") else None)
    dve_zb = P.step("vector", dd)

    p3_xe, p3_qbf = d["p3_xe"], d["p3_qbf"]
    p3_r, p3_m, p3_z, p3_zbf = d["p3_r"], d["p3_m"], d["p3_z"], d["p3_zbf"]
    p3_zb, p3_t4 = d["p3_zb"], d["p3_t4"]
    p3_ln, p3_sq, p3_st, p3_ab = d["p3_ln"], d["p3_sq"], d["p3_st"], d["p3_ab"]
    p3_lt, p3_gs = d["p3_lt"], d["p3_gs"]
    ps3_q = d["ps4"]
    ps3_h4 = d["ps4"][:, 4 * T:5 * T]
    ps3_s = d["ps3_s"]
    ps3_b5 = d["ps4"][:, 5 * T:6 * T]
    ps3_b6 = d["ps4"][:, 6 * T:7 * T]
    ps3_g7 = d["ps4"][:, 7 * T:8 * T]

    x3_free_pool = [0, 0]
    x3_free_pe = [0, 0]
    x3_done = [0, 0]
    q_elu_done = [0, 0]
    ln_first_done = [0, 0]
    t4_done = [0, 0]
    zb_act = [0, 0]
    zb_free = [0, 0]
    ln_free_pe = [0, 0]
    norm_pool = [0, 0]
    ln_stores = [(0, 0), (0, 0)]
    z_evac_act = 0
    gs_free = [0, 0, 0, 0]
    logit_tp_done = 0
    qbf_free = 0
    zbf_free = 0
    g_evac_last = 0
    a_pe_end = [0, 0]
    a_dve_end = [0, 0]

    pe2_done = S["c_pe"]
    dve2_done = S["c_dve"]

    def p3_load(panel):
        par = panel % 2
        sem = "d_x3a" if par == 0 else "d_x3b"
        t0 = panel * T
        P.wait("sync", "d_stage", d_stage_all)
        P.wait("sync", "d_stgg", d_stgg_all)
        if panel <= 1:
            P.wait("sync", "c_pe", pe2_done)
            P.wait("sync", "c_dve", dve2_done)
        P.wait("sync", "c_dve", x3_free_pool[par])
        P.wait("sync", "c_pe", x3_free_pe[par])
        e = P.eng if P.on("sync") else None
        for cc in range(NKC):
            i = (e.dma_start_transpose(
                p3_xe[:, (par * NKC + cc) * 544:(par * NKC + cc + 1) * 544],
                d["xpad_bf"][t0:t0 + 544, cc * 128:(cc + 1) * 128],
            ) if e else None)
            P.inc(i, sem, 16)
        x3_done[par] = S[sem]

    def emit_A(panel):
        nonlocal qbf_free, zbf_free, z_evac_act
        par = panel % 2
        sem = "d_x3a" if par == 0 else "d_x3b"
        need_x = x3_done[par]
        t0 = panel * T
        if panel + 1 < NP3:
            p3_load(panel + 1)

        if panel == 0:
            P.wait("tensor", "d_wqk", wqk_q_ready)
            P.wait("tensor", "c_act", act_zb)

        for cc in range(NKC):
            bpar = cc % 2
            qb = cc % 4
            ps_q = ps3_q[:, qb * T:(qb + 1) * T]
            P.wait("tensor", sem, need_x)
            P.wait("tensor", "c_dve", q_elu_done[bpar])
            if qb >= 2:
                P.wait("tensor", "c_act", zb_act[qb - 2])
            else:
                P.wait("tensor", "c_dve", ln_first_done[qb])
            mm = None
            for kc in range(NKC):
                if P.on("tensor"):
                    mm = P.eng.matmul(
                        ps_q,
                        wqk_sb[:, kc * 1024 + cc * 128:kc * 1024 + (cc + 1) * 128],
                        p3_xe[:, (par * NKC + kc) * 544 + 16:(par * NKC + kc) * 544 + 528],
                        start=(kc == 0), stop=(kc == NKC - 1), skip_group_check=True)
            pe_q = P.inc(mm, "c_pe", 1)
            P.wait("scalar", "c_pe", pe_q)
            P.wait("scalar", "c_dve", q_elu_done[bpar])
            a1 = (P.eng.activation(p3_r[:, bpar * T:(bpar + 1) * T], ps_q,
                                   AF.Relu, bias=d["bqk_q_sb"][:, cc:cc + 1])
                  if P.on("scalar") else None)
            P.step("scalar", a1)
            a2 = (P.eng.activation(p3_m[:, bpar * T:(bpar + 1) * T], ps_q,
                                   AF.Relu, scale=-1.0,
                                   bias=d["bqk_qn_sb"][:, cc:cc + 1])
                  if P.on("scalar") else None)
            P.step("scalar", a2)
            a3 = (P.eng.activation(p3_m[:, bpar * T:(bpar + 1) * T],
                                   p3_m[:, bpar * T:(bpar + 1) * T], AF.Exp,
                                   scale=-1.0)
                  if P.on("scalar") else None)
            a3v = P.step("scalar", a3)
            P.wait("vector", "c_act", a3v)
            P.wait("vector", "c_pe", qbf_free)
            d3 = (P.eng.tensor_tensor(p3_qbf[:, cc * T:(cc + 1) * T],
                                      p3_r[:, bpar * T:(bpar + 1) * T],
                                      p3_m[:, bpar * T:(bpar + 1) * T], ALU.add)
                  if P.on("vector") else None)
            q_elu_done[bpar] = P.step("vector", d3)
        q_all = S["c_dve"]

        P.wait("tensor", "c_dve", q_all)
        P.wait("tensor", "c_act", z_evac_act)
        mm = None
        for cc in range(NKC):
            if P.on("tensor"):
                mm = P.eng.matmul(
                    ps3_h4[0:16, :],
                    d["KM_bf"][:, cc * 16:(cc + 1) * 16],
                    p3_qbf[:, cc * T:(cc + 1) * T],
                    start=(cc == 0), stop=(cc == NKC - 1), skip_group_check=True)
        pe_zden = P.inc(mm, "c_pe", 1)
        P.wait("scalar", "c_pe", pe_zden)
        P.wait("scalar", "c_dve", dve_zb)
        a = (P.eng.activation(p3_z[:, :], ps3_h4[0:16, :], AF.Identity,
                              bias=d["zden_b"][:, :]) if P.on("scalar") else None)
        z_evac_act = P.step("scalar", a)
        P.wait("vector", "c_act", z_evac_act)
        dd = P.eng.reciprocal(p3_z[:, :], p3_z[:, :]) if P.on("vector") else None
        P.step("vector", dd)
        P.wait("vector", "c_pe", zbf_free)
        dd = P.eng.tensor_copy(p3_zbf[:, :], p3_z[:, :]) if P.on("vector") else None
        z_ready = P.step("vector", dd)

        for cc in range(NKC):
            cpar = cc % 2
            ps_at = ps3_q[:, cpar * T:(cpar + 1) * T]
            ps_zb = ps3_q[:, (2 + cpar) * T:(3 + cpar) * T]
            P.wait("tensor", "c_dve", z_ready)
            P.wait("tensor", "c_act", zb_act[cpar])
            mm = (P.eng.matmul(
                ps_zb,
                d["selz_sb"][:, cc * 128:(cc + 1) * 128],
                p3_zbf[:, :], start=True, stop=True, skip_group_check=True)
                if P.on("tensor") else None)
            pe_zbc = P.inc(mm, "c_pe", 1)
            if cc == NKC - 1:
                zbf_free = pe_zbc
            P.wait("scalar", "c_pe", pe_zbc)
            P.wait("scalar", "c_dve", zb_free[cpar])
            a = (P.eng.activation(p3_zb[:, cpar * T:(cpar + 1) * T], ps_zb,
                                  AF.Identity) if P.on("scalar") else None)
            zb_act[cpar] = P.step("scalar", a)

            P.wait("tensor", "c_dve", ln_first_done[cpar])
            mm = None
            for hi in range(2):
                hh = 2 * cc + hi
                kvop = (d["kv_bf"][:, hh * 64:(hh + 1) * 64] if hi == 0 else
                        d["kv2_bf"][64:128, hh * 64:(hh + 1) * 64])
                if P.on("tensor"):
                    mm = P.eng.matmul(
                        ps_at[hi * 64:(hi + 1) * 64, :],
                        kvop,
                        p3_qbf[hi * 64:(hi + 1) * 64, cc * T:(cc + 1) * T],
                        start=True, stop=True, skip_group_check=True)
            pe_at = P.inc(mm, "c_pe", 1)
            if cc == NKC - 1:
                qbf_free = pe_at

            lnoff = (par * NKC + cc) * T
            P.wait("vector", "c_pe", pe_at)
            P.wait("vector", "c_act", zb_act[cpar])
            P.wait("vector", "c_pe", ln_free_pe[par])
            P.wait("vector", "c_pool", norm_pool[par])
            P.wait("vector", "s_x32", ln_stores[par][0])
            P.wait("vector", "s_xlw", ln_stores[par][1])
            dd = (P.eng.tensor_tensor(p3_ln[:, lnoff:lnoff + T],
                                      ps_at, p3_zb[:, cpar * T:(cpar + 1) * T],
                                      ALU.mult) if P.on("vector") else None)
            zb_free[cpar] = ln_first_done[cpar] = P.step("vector", dd)

            xb = (par * NKC + cc) * 544
            P.wait("vector", sem, need_x)
            P.wait("vector", "c_dve", t4_done[cpar])
            P.wait("vector", "c_dve", v_const)
            pl = (P.eng.scalar_tensor_tensor(p3_t4[:, cpar * T:(cpar + 1) * T],
                                             p3_xe[:, xb + 15:xb + 527],
                                             d["lepe_w_sb"][:, cc * 3:cc * 3 + 1],
                                             d["lepe_bt"][:, cc * T:(cc + 1) * T],
                                             ALU.mult, ALU.add)
                  if P.on("vector") else None)
            P.step("vector", pl)
            pl = (P.eng.scalar_tensor_tensor(p3_t4[:, cpar * T:(cpar + 1) * T],
                                             p3_xe[:, xb + 16:xb + 528],
                                             d["lepe_w1p_sb"][:, cc:cc + 1],
                                             p3_t4[:, cpar * T:(cpar + 1) * T],
                                             ALU.mult, ALU.add)
                  if P.on("vector") else None)
            P.step("vector", pl)
            pl = (P.eng.scalar_tensor_tensor(p3_t4[:, cpar * T:(cpar + 1) * T],
                                             p3_xe[:, xb + 17:xb + 529],
                                             d["lepe_w_sb"][:, cc * 3 + 2:cc * 3 + 3],
                                             p3_t4[:, cpar * T:(cpar + 1) * T],
                                             ALU.mult, ALU.add)
                  if P.on("vector") else None)
            P.step("vector", pl)
            dd = (P.eng.tensor_tensor(p3_ln[:, lnoff:lnoff + T],
                                      p3_ln[:, lnoff:lnoff + T],
                                      p3_t4[:, cpar * T:(cpar + 1) * T], ALU.add)
                  if P.on("vector") else None)
            t4_done[cpar] = P.step("vector", dd)
        x3_free_pool[par] = S["c_dve"]
        a_pe_end[par] = S["c_pe"]
        a_dve_end[par] = S["c_dve"]

    def emit_B(panel):
        nonlocal logit_tp_done, g_evac_last
        par = panel % 2
        t0 = panel * T
        lnb = par * NKC * T
        ln_ready = a_dve_end[par]

        mm = None
        for cc in range(NKC):
            P.wait("gpsimd", "c_dve", ln_ready)
            dd = (P.eng.tensor_tensor(p3_sq[:, :], p3_ln[:, lnb + cc * T:lnb + (cc + 1) * T],
                                      p3_ln[:, lnb + cc * T:lnb + (cc + 1) * T], ALU.mult)
                  if P.on("gpsimd") else None)
            sqv = P.step("gpsimd", dd)
            P.wait("tensor", "c_dve", ln_ready)
            P.wait("tensor", "c_pool", sqv)
            if cc == 0:
                P.wait("tensor", "c_act", g_evac_last)
            if P.on("tensor"):
                mm = P.eng.matmul(ps3_s[:, 0:T], ones_c_f[:, :],
                                  p3_ln[:, lnb + cc * T:lnb + (cc + 1) * T],
                                  start=(cc == 0), stop=(cc == NKC - 1),
                                  skip_group_check=True)
                mm = P.eng.matmul(ps3_s[:, T:2 * T], ones_c_f[:, :], p3_sq[:, :],
                                  start=(cc == 0), stop=(cc == NKC - 1),
                                  skip_group_check=True)
            pe_st = P.inc(mm, "c_pe", 1)
            P.wait("gpsimd", "c_pe", pe_st)
        pe_stats = S["c_pe"]

        P.wait("scalar", "c_pe", pe_stats)
        a = (P.eng.activation(p3_st[:, 0:T], ps3_s[:, 0:T], AF.Identity, scale=1.0 / C)
             if P.on("scalar") else None)
        P.step("scalar", a)
        a = (P.eng.activation(p3_st[:, T:2 * T], ps3_s[:, T:2 * T], AF.Identity,
                              scale=1.0 / C) if P.on("scalar") else None)
        av = P.step("scalar", a)
        P.wait("vector", "c_act", av)
        dd = (P.eng.tensor_tensor(p3_st[:, 2 * T:3 * T], p3_st[:, 0:T],
                                  p3_st[:, 0:T], ALU.mult) if P.on("vector") else None)
        P.step("vector", dd)
        dd = (P.eng.scalar_tensor_tensor(p3_st[:, 2 * T:3 * T], p3_st[:, T:2 * T],
                                         1e-5, p3_st[:, 2 * T:3 * T],
                                         ALU.add, ALU.subtract)
              if P.on("vector") else None)
        dvar = P.step("vector", dd)
        P.wait("scalar", "c_dve", dvar)
        a = (P.eng.activation(p3_st[:, 2 * T:3 * T], p3_st[:, 2 * T:3 * T], AF.Ln)
             if P.on("scalar") else None)
        P.step("scalar", a)
        a = (P.eng.activation(p3_st[:, 2 * T:3 * T], p3_st[:, 2 * T:3 * T], AF.Exp,
                              scale=-0.5) if P.on("scalar") else None)
        arstd = P.step("scalar", a)
        P.wait("vector", "c_act", arstd)
        dd = (P.eng.tensor_tensor(p3_st[:, 3 * T:4 * T], p3_st[:, 0:T],
                                  p3_st[:, 2 * T:3 * T], ALU.mult)
              if P.on("vector") else None)
        P.step("vector", dd)
        dd = (P.eng.tensor_scalar(p3_st[:, 3 * T:4 * T], p3_st[:, 3 * T:4 * T],
                                  -1.0, None, ALU.mult) if P.on("vector") else None)
        dbsh = P.step("vector", dd)
        P.wait("tensor", "c_dve", dbsh)
        mm = (P.eng.matmul(ps3_b5, ones_r_f[:, :], p3_st[:, 2 * T:3 * T],
                           start=True, stop=True, skip_group_check=True)
              if P.on("tensor") else None)
        pe_a = P.inc(mm, "c_pe", 1)
        P.wait("scalar", "c_pe", pe_a)
        a = (P.eng.activation(p3_ab[:, 0:T], ps3_b5, AF.Identity)
             if P.on("scalar") else None)
        aa = P.step("scalar", a)
        P.wait("tensor", "c_act", aa)
        mm = (P.eng.matmul(ps3_b5, ones_r_f[:, :], p3_st[:, 3 * T:4 * T],
                           start=True, stop=True, skip_group_check=True)
              if P.on("tensor") else None)
        pe_b = P.inc(mm, "c_pe", 1)
        P.wait("scalar", "c_pe", pe_b)
        a = (P.eng.activation(p3_ab[:, T:2 * T], ps3_b5, AF.Identity)
             if P.on("scalar") else None)
        ab_ready = P.step("scalar", a)

        gate_first = True
        norm_done = 0
        for cc in range(NKC):
            P.wait("gpsimd", "c_act", ab_ready)
            dd = (P.eng.tensor_tensor(p3_ln[:, lnb + cc * T:lnb + (cc + 1) * T],
                                      p3_ln[:, lnb + cc * T:lnb + (cc + 1) * T],
                                      p3_ab[:, 0:T], ALU.mult)
                  if P.on("gpsimd") else None)
            P.step("gpsimd", dd)
            dd = (P.eng.tensor_tensor(p3_ln[:, lnb + cc * T:lnb + (cc + 1) * T],
                                      p3_ln[:, lnb + cc * T:lnb + (cc + 1) * T],
                                      p3_ab[:, T:2 * T], ALU.add)
                  if P.on("gpsimd") else None)
            norm_done = P.step("gpsimd", dd)
            P.wait("tensor", "c_pool", norm_done)
            mm = (P.eng.matmul(ps3_g7[0:8, :], d["wg_sb"][:, cc * 8:(cc + 1) * 8],
                               p3_ln[:, lnb + cc * T:lnb + (cc + 1) * T],
                               start=gate_first, stop=(cc == NKC - 1),
                               skip_group_check=True)
                  if P.on("tensor") else None)
            gate_first = False
            P.inc(mm, "c_pe", 1)
        pe_gate = S["c_pe"]
        ln_free_pe[par] = pe_gate
        norm_all = norm_done
        norm_pool[par] = norm_done

        P.wait("sync", "c_pool", norm_all)
        e = P.eng if P.on("sync") else None
        for cc in range(NKC):
            i = (e.dma_start(out=d["xln32_d"][cc, :, t0:t0 + T],
                             in_=p3_ln[:, lnb + cc * T:lnb + (cc + 1) * T]) if e else None)
            P.inc(i, "s_x32", 16)
        g = P.eng if P.on("gpsimd") else None
        for cc in range(NKC):
            i2 = (g.dma_start(out=d["xlnbf_d"][cc, :, t0:t0 + T],
                              in_=p3_ln[:, lnb + cc * T:lnb + (cc + 1) * T]) if g else None)
            P.inc(i2, "s_xlw", 16)
        ln_stores[par] = (S["s_x32"], S["s_xlw"])

        P.wait("scalar", "c_pe", pe_gate)
        P.wait("scalar", "c_pe", logit_tp_done)
        a = (P.eng.activation(d["logits_sb"][0:8, :], ps3_g7[0:8, :], AF.Identity,
                              bias=d["bgate_sb"][:, :]) if P.on("scalar") else None)
        a_log = P.step("scalar", a)
        for jb in range(4):
            P.wait("tensor", "c_act", a_log)
            P.wait("tensor", "c_act", g_evac_last)
            mm = (P.eng.transpose(ps3_b6[:, 0:128],
                                  d["logits_sb"][:, jb * 128:(jb + 1) * 128],
                                  d["ident_sb"][:, :]) if P.on("tensor") else None)
            pe_t = P.inc(mm, "c_pe", 1)
            logit_tp_done = pe_t
            P.wait("vector", "c_pe", pe_t)
            dd = (P.eng.tensor_copy(p3_lt[:, jb * 8:(jb + 1) * 8], ps3_b6[:, 0:8])
                  if P.on("vector") else None)
            P.step("vector", dd)
            lt = p3_lt[:, jb * 8:(jb + 1) * 8]
            e8 = p3_gs[:, jb * 8:(jb + 1) * 8]
            oh8 = p3_gs[:, 32 + jb * 8:32 + (jb + 1) * 8]
            mx = p3_gs[:, 64 + jb:65 + jb]
            sm = p3_gs[:, 68 + jb:69 + jb]
            pc = p3_gs[:, 72 + jb:73 + jb]
            P.wait("vector", "c_pe", gs_free[jb])
            dd = P.eng.reduce_max(mx, lt, axis=AX) if P.on("vector") else None
            P.step("vector", dd)
            dd = (P.eng.tensor_scalar(e8, lt, mx, None, ALU.subtract)
                  if P.on("vector") else None)
            P.step("vector", dd)
            dd = (P.eng.tensor_scalar(oh8, lt, mx, None, ALU.is_equal)
                  if P.on("vector") else None)
            dv = P.step("vector", dd)
            P.wait("scalar", "c_dve", dv)
            a = P.eng.activation(e8, e8, AF.Exp) if P.on("scalar") else None
            ae = P.step("scalar", a)
            P.wait("vector", "c_act", ae)
            dd = P.eng.reduce_sum(sm, e8, axis=AX) if P.on("vector") else None
            P.step("vector", dd)
            dd = P.eng.reciprocal(pc, sm) if P.on("vector") else None
            P.step("vector", dd)
            dd = (P.eng.tensor_scalar(oh8, oh8, pc, None, ALU.mult)
                  if P.on("vector") else None)
            dg = P.step("vector", dd)
            P.wait("tensor", "c_dve", dg)
            mm = (P.eng.transpose(ps3_b6[0:8, 0:128], oh8, d["ident_sb"][:, :])
                  if P.on("tensor") else None)
            pe_g = P.inc(mm, "c_pe", 1)
            gs_free[jb] = pe_g
            P.wait("scalar", "c_pe", pe_g)
            a = (P.eng.activation(d["g_bf"][:, t0 + jb * 128:t0 + (jb + 1) * 128],
                                  ps3_b6[0:8, 0:128], AF.Identity)
                 if P.on("scalar") else None)
            g_evac_last = P.step("scalar", a)
        x3_free_pe[par] = max(x3_free_pe[par], a_pe_end[par])

    p3_load(0)
    emit_A(0)
    for panel in range(1, NP3):
        emit_A(panel)
        emit_B(panel - 1)
    emit_B(NP3 - 1)

    pe3_done = S["c_pe"]
    act3_done = S["c_act"]
    dve3_done = S["c_dve"]
    pool3_done = S["c_pool"]
    xlw_all = S["s_xlw"]
    x32_all = S["s_x32"]

    p4_w1, p4_w2, p4_xp = d["p4_w1"], d["p4_w2"], d["p4_xp"]
    p4_h, p4_hr, p4_gb = d["p4_h"], d["p4_hr"], d["p4_gb"]
    p4_gr, p4_b2, p4_yb = d["p4_gr"], d["p4_b2"], d["p4_yb"]
    ps4 = d["ps4"]

    w_free = [0, 0]
    w_done = [0, 0]
    xp_free = [0, 0]
    xp_done = [0, 0]
    yl_done = [0, 0]
    b2_free = [0, 0]
    b2_done = [0, 0]
    bank_evac = [0] * NOC
    h_done_m = [0] * NMC
    h_evac = [0, 0]
    hsb_used = 0
    gb_act = [0, 0]
    gr_act = [0, 0]
    gb_free = [0, 0]
    gr_free = [0, 0]
    hr_free = [0, 0]

    def w_load(it):
        e_, q_ = divmod(it, NQ)
        par = it % 2
        sem = "d_w4a" if par == 0 else "d_w4b"
        P.wait("gpsimd", "c_pe", w_free[par])
        if it == 0:
            P.wait("gpsimd", "c_pe", pe3_done)
            P.wait("gpsimd", "c_pool", pool3_done)
            P.wait("gpsimd", "c_dve", dve3_done)
            P.wait("gpsimd", "c_act", act3_done)
        g = P.eng if P.on("gpsimd") else None
        i = (g.dma_start(
            out=p4_w1.rearrange("p (b kc m) -> p b kc m", b=2, kc=NKC)[:, par],
            in_=d["w1"].rearrange("e (kc p) m -> e p kc m", p=128)[e_][:, :, q_ * QM:(q_ + 1) * QM],
        ) if g else None)
        P.inc(i, sem, 16)
        i = (g.dma_start(
            out=p4_w2.rearrange("p (b mc o) -> p b mc o", b=2, mc=NMC)[:, par],
            in_=d["w2"].rearrange("e (mc p) o -> e p mc o", p=128, mc=32)[e_][:, q_ * NMC:(q_ + 1) * NMC, :],
        ) if g else None)
        P.inc(i, sem, 16)
        w_done[par] = S[sem]

    def b2_load(e_):
        par = e_ % 2
        sem = "d_b2a" if par == 0 else "d_b2b"
        P.wait("gpsimd", "c_pe", b2_free[par])
        if e_ == 0:
            P.wait("gpsimd", "c_pe", pe3_done)
        g = P.eng if P.on("gpsimd") else None
        i = (g.dma_start(out=p4_b2[:, par * C:(par + 1) * C],
                         in_=d["b2r"][e_:e_ + 1, :]) if g else None)
        P.inc(i, sem, 16)
        b2_done[par] = S[sem]

    def xp_load(it, p):
        ip = (it * NP3 + p) % 2
        sem = "d_xp4a" if ip == 0 else "d_xp4b"
        P.wait("sync", "c_pe", xp_free[ip])
        if it == 0 and p <= 1:
            P.wait("sync", "s_xlw", xlw_all)
            P.wait("sync", "c_pe", pe3_done)
            P.wait("sync", "c_pool", pool3_done)
            P.wait("sync", "c_dve", dve3_done)
            P.wait("sync", "c_act", act3_done)
        e = P.eng if P.on("sync") else None
        i = (e.dma_start(
            out=p4_xp.rearrange("p (b kc t) -> p b kc t", b=2, kc=NKC)[:, ip],
            in_=d["xlnbf_d"].rearrange("kc p t -> p kc t")[:, :, p * T:(p + 1) * T],
        ) if e else None)
        P.inc(i, sem, 16)
        xp_done[ip] = S[sem]

    def y_load(it, p):
        ip = (it * NP3 + p) % 2
        sem = "d_yla" if ip == 0 else "d_ylb"
        ssem = "s_ysta" if ip == 0 else "s_ystb"
        P.wait("sync", ssem, S[ssem])
        e = P.eng if P.on("sync") else None
        i = (e.dma_start(
            out=p4_yb.rearrange("p (b oc t) -> p b oc t", b=2, oc=NOC)[:, ip],
            in_=d["y_d"].rearrange("oc p t -> p oc t")[:, :, p * T:(p + 1) * T],
        ) if e else None)
        P.inc(i, sem, 16)
        yl_done[ip] = S[sem]

    w_load(0)
    b2_load(0)
    xp_load(0, 0)
    for it in range(NE * NQ):
        e_, q_ = divmod(it, NQ)
        wpar = it % 2
        wsem = "d_w4a" if wpar == 0 else "d_w4b"
        this_w = w_done[wpar]
        if it + 1 < NE * NQ:
            w_load(it + 1)
        if q_ == NQ - 1 and e_ + 1 < NE:
            b2_load(e_ + 1)
        for p in range(NP3):
            ip = (it * NP3 + p) % 2
            xsem = "d_xp4a" if ip == 0 else "d_xp4b"
            this_xp = xp_done[ip]
            nit, np_ = (it, p + 1) if p + 1 < NP3 else (it + 1, 0)
            if nit < NE * NQ:
                xp_load(nit, np_)
            if it >= 1:
                y_load(it, p)

            if it == 0 and p == 0:
                P.wait("tensor", "c_act", act3_done)
                P.wait("tensor", "c_dve", dve3_done)
            P.wait("tensor", "c_dve", bank_evac[2])
            P.wait("tensor", "c_dve", bank_evac[3])
            P.wait("tensor", "c_act", gr_act[ip])
            P.wait("tensor", "c_act", gb_act[ip])
            mm = (P.eng.matmul(ps4[0:1, 2 * T:3 * T],
                               d["ident8_sb"][:, e_:e_ + 1],
                               d["g_bf"][:, p * T:(p + 1) * T],
                               start=True, stop=True, skip_group_check=True)
                  if P.on("tensor") else None)
            pe_gr = P.inc(mm, "c_pe", 1)
            mm = (P.eng.matmul(ps4[:, 3 * T:4 * T],
                               d["selbc_sb"][:, e_ * 128:(e_ + 1) * 128],
                               d["g_bf"][:, p * T:(p + 1) * T],
                               start=True, stop=True, skip_group_check=True)
                  if P.on("tensor") else None)
            pe_gb = P.inc(mm, "c_pe", 1)
            P.wait("scalar", "c_pe", pe_gr)
            P.wait("scalar", "c_pe", gr_free[ip])
            a = (P.eng.activation(p4_gr[:, ip * T:(ip + 1) * T], ps4[0:1, 2 * T:3 * T],
                                  AF.Identity) if P.on("scalar") else None)
            gr_act[ip] = P.step("scalar", a)
            P.wait("scalar", "c_pe", pe_gb)
            P.wait("scalar", "c_pool", gb_free[ip])
            a = (P.eng.activation(p4_gb[:, ip * T:(ip + 1) * T], ps4[:, 3 * T:4 * T],
                                  AF.Identity) if P.on("scalar") else None)
            gb_act[ip] = P.step("scalar", a)

            P.wait("tensor", wsem, this_w)
            P.wait("tensor", xsem, this_xp)
            for m in range(NMC):
                hb = m % 2
                P.wait("tensor", "c_dve", bank_evac[hb])
                P.wait("tensor", "c_act", h_evac[hb])
                mm = None
                for kc in range(NKC):
                    if P.on("tensor"):
                        mm = P.eng.matmul(
                            ps4[:, hb * T:(hb + 1) * T],
                            p4_w1[:, (wpar * NKC + kc) * QM + m * 128:
                                  (wpar * NKC + kc) * QM + (m + 1) * 128],
                            p4_xp[:, (ip * NKC + kc) * T:(ip * NKC + kc + 1) * T],
                            start=(kc == 0), stop=(kc == NKC - 1),
                            skip_group_check=True)
                pe_h = P.inc(mm, "c_pe", 1)
                hp = m % 2
                P.wait("scalar", "c_pe", pe_h)
                P.wait("scalar", "c_pool", hr_free[hp])
                a = (P.eng.activation(
                    p4_hr[:, hp * T:(hp + 1) * T],
                    ps4[:, hb * T:(hb + 1) * T], AF.Relu,
                    bias=d["b1p_sb"][:, e_ * 32 + q_ * NMC + m:e_ * 32 + q_ * NMC + m + 1])
                    if P.on("scalar") else None)
                h_evac[hb] = P.step("scalar", a)
                P.wait("gpsimd", "c_act", h_evac[hb])
                P.wait("gpsimd", "c_act", gb_act[ip])
                P.wait("gpsimd", "c_pe", hsb_used)
                pl = (P.eng.tensor_tensor(p4_h[:, m * T:(m + 1) * T],
                                          p4_hr[:, hp * T:(hp + 1) * T],
                                          p4_gb[:, ip * T:(ip + 1) * T], ALU.mult)
                      if P.on("gpsimd") else None)
                hr_free[hp] = gb_free[ip] = P.step("gpsimd", pl)
                h_done_m[m] = S["c_pool"]

            P.wait("tensor", "c_act", gr_act[ip])
            last_oc_mm = 0
            for oc in range(NOC):
                P.wait("tensor", "c_dve", bank_evac[oc])
                if oc < 2:
                    P.wait("tensor", "c_act", h_evac[oc])
                mm = None
                first = True
                if q_ == 0:
                    bsem = "d_b2a" if e_ % 2 == 0 else "d_b2b"
                    P.wait("tensor", bsem, b2_done[e_ % 2])
                    if P.on("tensor"):
                        mm = P.eng.matmul(
                            ps4[:, oc * T:(oc + 1) * T],
                            p4_b2[:, (e_ % 2) * C + oc * 128:(e_ % 2) * C + (oc + 1) * 128],
                            p4_gr[:, ip * T:(ip + 1) * T],
                            start=True, stop=False, skip_group_check=True)
                    first = False
                for m in range(NMC):
                    if oc == 0:
                        P.wait("tensor", "c_pool", h_done_m[m])
                    if P.on("tensor"):
                        mm = P.eng.matmul(
                            ps4[:, oc * T:(oc + 1) * T],
                            p4_w2[:, (wpar * NMC + m) * C + oc * 128:
                                  (wpar * NMC + m) * C + (oc + 1) * 128],
                            p4_h[:, m * T:(m + 1) * T],
                            start=first, stop=(m == NMC - 1),
                            skip_group_check=True)
                    first = False
                pe_oc = P.inc(mm, "c_pe", 1)
                last_oc_mm = pe_oc
                P.wait("vector", "c_pe", pe_oc)
                if it == 0:
                    if oc == 0:
                        ssem = "s_ysta" if ip == 0 else "s_ystb"
                        P.wait("vector", ssem, S[ssem])
                    dd = (P.eng.tensor_copy(
                        p4_yb[:, (ip * NOC + oc) * T:(ip * NOC + oc + 1) * T],
                        ps4[:, oc * T:(oc + 1) * T]) if P.on("vector") else None)
                else:
                    ysem = "d_yla" if ip == 0 else "d_ylb"
                    P.wait("vector", ysem, yl_done[ip])
                    dd = (P.eng.tensor_tensor(
                        p4_yb[:, (ip * NOC + oc) * T:(ip * NOC + oc + 1) * T],
                        ps4[:, oc * T:(oc + 1) * T],
                        p4_yb[:, (ip * NOC + oc) * T:(ip * NOC + oc + 1) * T],
                        ALU.add) if P.on("vector") else None)
                bank_evac[oc] = P.step("vector", dd)
            hsb_used = last_oc_mm
            b2_free[e_ % 2] = last_oc_mm
            gr_free[ip] = last_oc_mm
            w_free[wpar] = last_oc_mm
            xp_free[ip] = last_oc_mm

            P.wait("sync", "c_dve", bank_evac[NOC - 1])
            e = P.eng if P.on("sync") else None
            i = (e.dma_start(
                out=d["y_d"].rearrange("oc p t -> p oc t")[:, :, p * T:(p + 1) * T],
                in_=p4_yb.rearrange("p (b oc t) -> p b oc t", b=2, oc=NOC)[:, ip],
            ) if e else None)
            P.inc(i, "s_ysta" if ip == 0 else "s_ystb", 16)
    yst_a = S["s_ysta"]
    yst_b = S["s_ystb"]
    pe4_done = S["c_pe"]
    act4_done = S["c_act"]
    dve4_done = S["c_dve"]
    pool4_done = S["c_pool"]

    p5_x, p5_y, p5_sq = d["p5_x"], d["p5_y"], d["p5_sq"]
    p5_st, p5_ab, p5_otm = d["p5_st"], d["p5_ab"], d["p5_otm"]
    ps5_s, ps5_c, ps5_t = d["ps5_s"], d["ps5_c"], d["ps5_t"]

    x5_free = [0, 0]
    x5_pool_free = [0, 0]
    x5_done = [0, 0]

    def p5_load(panel):
        par = panel % 2
        sem = "d_x5a" if par == 0 else "d_x5b"
        P.wait("sync", "s_ysta", yst_a)
        P.wait("sync", "s_ystb", yst_b)
        P.wait("sync", "s_x32", x32_all)
        P.wait("sync", "c_pe", x5_free[par])
        P.wait("sync", "c_pool", x5_pool_free[par])
        if panel <= 1:
            P.wait("sync", "c_pe", pe4_done)
            P.wait("sync", "c_act", act4_done)
            P.wait("sync", "c_pool", pool4_done)
            P.wait("sync", "c_dve", dve4_done)
        e = P.eng if P.on("sync") else None
        i = (e.dma_start(
            out=p5_x.rearrange("p (b kc t) -> p b kc t", b=2, kc=NKC)[:, par],
            in_=d["xln32_d"].rearrange("kc p t -> p kc t")[:, :, panel * T:(panel + 1) * T],
        ) if e else None)
        P.inc(i, sem, 16)
        i = (e.dma_start(
            out=p5_y.rearrange("p (b oc t) -> p b oc t", b=2, oc=NOC)[:, par],
            in_=d["y_d"].rearrange("oc p t -> p oc t")[:, :, panel * T:(panel + 1) * T],
        ) if e else None)
        P.inc(i, sem, 16)
        x5_done[par] = S[sem]

    otm_free = [0, 0]
    tp_evac = [0, 0, 0, 0]
    norm_done_by = [0, 0]
    c_pe_end = [0, 0]

    def emit_C(panel):
        par = panel % 2
        sem = "d_x5a" if par == 0 else "d_x5b"
        need = x5_done[par]
        for cc in range(NKC):
            P.wait("gpsimd", sem, need)
            dd = (P.eng.tensor_tensor(
                p5_x[:, (par * NKC + cc) * T:(par * NKC + cc + 1) * T],
                p5_x[:, (par * NKC + cc) * T:(par * NKC + cc + 1) * T],
                p5_y[:, (par * NKC + cc) * T:(par * NKC + cc + 1) * T], ALU.add)
                if P.on("gpsimd") else None)
            P.step("gpsimd", dd)
        mm = None
        for cc in range(NKC):
            dd = (P.eng.tensor_tensor(
                p5_sq[:, :],
                p5_x[:, (par * NKC + cc) * T:(par * NKC + cc + 1) * T],
                p5_x[:, (par * NKC + cc) * T:(par * NKC + cc + 1) * T], ALU.mult)
                if P.on("gpsimd") else None)
            sqv = P.step("gpsimd", dd)
            P.wait("tensor", "c_pool", sqv)
            if P.on("tensor"):
                mm = P.eng.matmul(ps5_s[:, 0:T], ones_c_f[:, :],
                                  p5_x[:, (par * NKC + cc) * T:(par * NKC + cc + 1) * T],
                                  start=(cc == 0), stop=(cc == NKC - 1),
                                  skip_group_check=True)
                mm = P.eng.matmul(ps5_s[:, T:2 * T], ones_c_f[:, :], p5_sq[:, :],
                                  start=(cc == 0), stop=(cc == NKC - 1),
                                  skip_group_check=True)
            pe_st = P.inc(mm, "c_pe", 1)
            P.wait("gpsimd", "c_pe", pe_st)
        pe_stats = S["c_pe"]
        P.wait("scalar", "c_pe", pe_stats)
        a = (P.eng.activation(p5_st[:, 0:T], ps5_s[:, 0:T], AF.Identity, scale=1.0 / C)
             if P.on("scalar") else None)
        P.step("scalar", a)
        a = (P.eng.activation(p5_st[:, T:2 * T], ps5_s[:, T:2 * T], AF.Identity,
                              scale=1.0 / C) if P.on("scalar") else None)
        av = P.step("scalar", a)
        P.wait("vector", "c_act", av)
        dd = (P.eng.tensor_tensor(p5_st[:, 2 * T:3 * T], p5_st[:, 0:T], p5_st[:, 0:T],
                                  ALU.mult) if P.on("vector") else None)
        P.step("vector", dd)
        dd = (P.eng.scalar_tensor_tensor(p5_st[:, 2 * T:3 * T], p5_st[:, T:2 * T],
                                         1e-5, p5_st[:, 2 * T:3 * T],
                                         ALU.add, ALU.subtract)
              if P.on("vector") else None)
        dvar = P.step("vector", dd)
        P.wait("scalar", "c_dve", dvar)
        a = (P.eng.activation(p5_st[:, 2 * T:3 * T], p5_st[:, 2 * T:3 * T], AF.Ln)
             if P.on("scalar") else None)
        P.step("scalar", a)
        a = (P.eng.activation(p5_st[:, 2 * T:3 * T], p5_st[:, 2 * T:3 * T], AF.Exp,
                              scale=-0.5) if P.on("scalar") else None)
        arstd = P.step("scalar", a)
        P.wait("vector", "c_act", arstd)
        dd = (P.eng.tensor_tensor(p5_st[:, 3 * T:4 * T], p5_st[:, 0:T],
                                  p5_st[:, 2 * T:3 * T], ALU.mult)
              if P.on("vector") else None)
        P.step("vector", dd)
        dd = (P.eng.tensor_scalar(p5_st[:, 3 * T:4 * T], p5_st[:, 3 * T:4 * T], -1.0,
                                  None, ALU.mult) if P.on("vector") else None)
        dbsh = P.step("vector", dd)
        P.wait("tensor", "c_dve", dbsh)
        mm = (P.eng.matmul(ps5_c[:, :], ones_r_f[:, :], p5_st[:, 2 * T:3 * T],
                           start=True, stop=True, skip_group_check=True)
              if P.on("tensor") else None)
        pe_a = P.inc(mm, "c_pe", 1)
        P.wait("scalar", "c_pe", pe_a)
        a = (P.eng.activation(p5_ab[:, 0:T], ps5_c[:, :], AF.Identity)
             if P.on("scalar") else None)
        aa = P.step("scalar", a)
        P.wait("tensor", "c_act", aa)
        mm = (P.eng.matmul(ps5_c[:, :], ones_r_f[:, :], p5_st[:, 3 * T:4 * T],
                           start=True, stop=True, skip_group_check=True)
              if P.on("tensor") else None)
        pe_b = P.inc(mm, "c_pe", 1)
        P.wait("scalar", "c_pe", pe_b)
        a = (P.eng.activation(p5_ab[:, T:2 * T], ps5_c[:, :], AF.Identity)
             if P.on("scalar") else None)
        ab_ready = P.step("scalar", a)
        for cc in range(NKC):
            P.wait("gpsimd", "c_act", ab_ready)
            dd = (P.eng.tensor_tensor(
                p5_x[:, (par * NKC + cc) * T:(par * NKC + cc + 1) * T],
                p5_x[:, (par * NKC + cc) * T:(par * NKC + cc + 1) * T],
                p5_ab[:, 0:T], ALU.mult) if P.on("gpsimd") else None)
            P.step("gpsimd", dd)
            dd = (P.eng.tensor_tensor(
                p5_x[:, (par * NKC + cc) * T:(par * NKC + cc + 1) * T],
                p5_x[:, (par * NKC + cc) * T:(par * NKC + cc + 1) * T],
                p5_ab[:, T:2 * T], ALU.add) if P.on("gpsimd") else None)
            norm_done_by[par] = P.step("gpsimd", dd)
        x5_pool_free[par] = norm_done_by[par]

    def emit_D(panel):
        par = panel % 2
        t0 = panel * T
        nd = norm_done_by[par]
        for j in range(4):
            jp = j % 2
            for cc in range(NKC):
                tb = cc % 4
                P.wait("tensor", "c_pool", nd)
                P.wait("tensor", "c_dve", tp_evac[tb])
                mm = (P.eng.transpose(
                    ps5_t[:, tb * 512:tb * 512 + 128],
                    p5_x[:, (par * NKC + cc) * T + j * 128:
                         (par * NKC + cc) * T + (j + 1) * 128],
                    d["ident_sb"][:, :]) if P.on("tensor") else None)
                pe_t = P.inc(mm, "c_pe", 1)
                P.wait("vector", "c_pe", pe_t)
                osem = "s_outa" if jp == 0 else "s_outb"
                P.wait("vector", osem, otm_free[jp])
                dd = (P.eng.tensor_copy(
                    p5_otm[:, jp * C + cc * 128:jp * C + (cc + 1) * 128],
                    ps5_t[:, tb * 512:tb * 512 + 128]) if P.on("vector") else None)
                tp_evac[tb] = P.step("vector", dd)
            P.wait("sync", "c_dve", tp_evac[(NKC - 1) % 4])
            e = P.eng if P.on("sync") else None
            i = (e.dma_start(out=d["out"][t0 + j * 128:t0 + (j + 1) * 128, :],
                             in_=p5_otm[:, jp * C:(jp + 1) * C]) if e else None)
            otm_free[jp] = P.inc(i, "s_outa" if jp == 0 else "s_outb", 16)
        x5_free[par] = S["c_pe"]

    p5_load(0)
    emit_C(0)
    for panel in range(1, NP3):
        p5_load(panel)
        emit_D(panel - 1)
        emit_C(panel)
    emit_D(NP3 - 1)

    P.wait("sync", "s_outa", S["s_outa"])
    P.wait("sync", "s_outb", S["s_outb"])


def logits_sb_view(d):
    return d["logits_sb"]


def build_program():
    nc = bass.Bass("TRN2", target_bir_lowering=False, debug=False)
    d = {"nc": nc}

    def dram(nm, shp, dt, kind):
        d[nm] = nc.dram_tensor(nm, shp, dt, kind=kind).ap()

    dram("xseq", [N, C], F32, "ExternalInput")
    dram("xpad", [NOWN + 32, C], F32, "ExternalInput")
    dram("wqkT", [C, 2 * C], F32, "ExternalInput")
    dram("bqk_q", [128, 8], F32, "ExternalInput")
    dram("bqk_qn", [128, 8], F32, "ExternalInput")
    dram("bqk_k", [1, C], F32, "ExternalInput")
    dram("lepe_wp", [128, 24], F32, "ExternalInput")
    dram("lepe_bp", [128, 8], F32, "ExternalInput")
    dram("lepe_bt_h", [128, NKC * T], BF16, "ExternalInput")
    dram("lepe_w1p", [128, 8], F32, "ExternalInput")
    dram("wg", [128, 64], F32, "ExternalInput")
    dram("bgate", [8, 1], F32, "ExternalInput")
    dram("b1p", [128, 256], F32, "ExternalInput")
    dram("b2r", [8, C], F32, "ExternalInput")
    dram("w1", [NE, C, MLP], F32, "ExternalInput")
    dram("w2", [NE, MLP, C], F32, "ExternalInput")
    dram("ident", [128, 128], F32, "ExternalInput")
    dram("maskKM", [128, 128], F32, "ExternalInput")
    dram("selz", [16, C], BF16, "ExternalInput")
    dram("selbc", [8, C], BF16, "ExternalInput")
    dram("ident8", [8, 8], BF16, "ExternalInput")
    dram("ident64", [64, 64], BF16, "ExternalInput")
    dram("unit8", [1, 64], F32, "ExternalInput")
    dram("out", [NOWN, C], F32, "ExternalOutput")
    dram("xseq_bf", [N, C], BF16, "Internal")
    dram("xpad_bf", [NOWN + 32, C], BF16, "Internal")
    dram("xln32_d", [NKC, 128, NOWN], F32, "Internal")
    dram("xlnbf_d", [NKC, 128, NOWN], BF16, "Internal")
    dram("y_d", [NOC, 128, NOWN], BF16, "Internal")

    BF_ELEMS = 67 * 1024
    F32_ELEMS = 18 * 1024

    import contextlib
    with contextlib.ExitStack() as ctx:
        arena_bf = ctx.enter_context(nc.sbuf_tensor("arena_bf", [128, BF_ELEMS], BF16))
        arena_f = ctx.enter_context(nc.sbuf_tensor("arena_f", [128, F32_ELEMS], F32))
        cb = Carver(arena_bf, BF_ELEMS)
        cf = Carver(arena_f, F32_ELEMS)

        d["kv_bf"] = cb.take(1024, 64)
        d["kv2_bf"] = cb.take(1024)
        d["ident64_sb"] = cb.take(64, 64)
        d["g_bf"] = cb.take(4096, 8)
        d["selz_sb"] = cb.take(1024, 16)
        d["selbc_sb"] = cb.take(1024, 8)
        d["ident8_sb"] = cb.take(16, 8)[:, 0:8]
        d["KM_bf"] = cb.take(128)
        d["ones_r_bf"] = cb.take(128, 1)
        d["ones_c_bf"] = cb.take(16)[:, 0:1]
        d["one1_bf"] = cb.take(16, 1)[:, 0:1]
        d["bqkk_sb"] = cb.take(1024, 1)
        d["ident_sb"] = cf.take(128)
        d["maskKM_sb"] = cf.take(128)
        d["unit8_sb"] = cf.take(64, 1)
        d["ones_r_f"] = cf.take(128, 1)
        d["ones_c_f"] = cf.take(16)[:, 0:1]
        d["bqk_q_sb"] = cf.take(16)[:, 0:8]
        d["bqk_qn_sb"] = cf.take(16)[:, 0:8]
        d["lepe_w_sb"] = cf.take(32)[:, 0:24]
        d["lepe_b_sb"] = cf.take(16)[:, 0:8]
        d["lepe_w1p_sb"] = cf.take(16)[:, 0:8]
        d["wg_sb"] = cf.take(64)
        d["bgate_sb"] = cf.take(16, 8)[:, 0:1]
        d["b1p_sb"] = cf.take(256)
        d["kme_pm"] = cf.take(16)[:, 0:8]
        d["zden_b"] = cf.take(16, 16)[:, 0:1]
        d["kmean_r"] = cf.take(1024, 1)
        d["logits_sb"] = cf.take(512)

        ov_bf = cb.mark()
        ov_f = cf.mark()
        d["wqk_sb"] = cb.take(NKC * 1024)
        d["lepe_bt"] = cb.take(NKC * T)
        p13_bf = cb.mark()
        d["p1_xfm"] = cb.take(2 * NKC * T)
        d["p1_xtm"] = cb.take(2 * 4 * C)
        d["p1_ktm"] = cb.take(2 * C)
        cb_p1_end = cb.mark()
        cb.reset(p13_bf)
        d["p3_xe"] = cb.take(2 * NKC * 544)
        d["p3_qbf"] = cb.take(NKC * T)
        d["p3_zbf"] = cb.take(512, 16)
        d["p3_zb"] = cb.take(2 * T)
        cb_p3_end = cb.mark()
        p13_f = cf.mark()
        d["p1_r"] = cf.take(4 * T)
        d["p1_m"] = cf.take(4 * T)
        cf_p1_end = cf.mark()
        cf.reset(p13_f)
        d["p3_r"] = cf.take(2 * T)
        d["p3_m"] = cf.take(2 * T)
        d["p3_z"] = cf.take(T, 16)
        d["p3_t4"] = cf.take(2 * T)
        d["p3_ln"] = cf.take(2 * NKC * T)
        d["p3_sq"] = cf.take(T)
        d["p3_st"] = cf.take(4 * T, 1)
        d["p3_ab"] = cf.take(2 * T)
        d["p3_lt"] = cf.take(32)
        d["p3_gs"] = cf.take(96)
        cf_p3_end = cf.mark()
        cb.reset(ov_bf)
        d["p4_w1"] = cb.take(2 * NKC * QM)
        d["p4_w2"] = cb.take(2 * NMC * C)
        d["p4_xp"] = cb.take(2 * NKC * T)
        d["p4_h"] = cb.take(NMC * T)
        d["p4_gb"] = cb.take(2 * T)
        d["p4_gr"] = cb.take(2 * T, 1)
        d["p4_b2"] = cb.take(2 * C, 1)
        d["p4_yb"] = cb.take(2 * NOC * T)
        cb_p4_end = cb.mark()
        cb.reset(ov_bf)
        d["p5_y"] = cb.take(2 * NOC * T)
        cb_p5_end = cb.mark()
        cf.reset(ov_f)
        d["p4_hr"] = cf.take(2 * T)
        cf_p4_end = cf.mark()
        cf.reset(ov_f)
        d["p5_x"] = cf.take(2 * NKC * T)
        d["p5_sq"] = cf.take(T)
        d["p5_st"] = cf.take(4 * T, 1)
        d["p5_ab"] = cf.take(2 * T)
        d["p5_otm"] = cf.take(2 * C)
        cf_p5_end = cf.mark()
        assert max(cb_p1_end, cb_p3_end, cb_p4_end, cb_p5_end) <= BF_ELEMS, (
            cb_p1_end, cb_p3_end, cb_p4_end, cb_p5_end)
        assert max(cf_p1_end, cf_p3_end, cf_p4_end, cf_p5_end) <= F32_ELEMS, (
            cf_p1_end, cf_p3_end, cf_p4_end, cf_p5_end)

        with contextlib.ExitStack() as pctx:
            psA = pctx.enter_context(nc.psum_tensor("psA", [128, 4096], F32))
            d["ps_kv"] = psA[0:64, 0:1024]
            d["ps_ks"] = psA[0:1, 1024:2048]
            d["ps_kt"] = psA[:, 2048:4096]
            d["ps_sm"] = psA[:, 3072:3584]
            d["ps3_a"] = psA[:, 0:512]
            d["ps3_b"] = psA[:, 512:1024]
            d["ps3_c"] = psA[:, 1024:1536]
            d["ps3_d"] = psA[:, 1536:2048]
            d["ps3_e"] = psA[:, 2048:2560]
            d["ps3_s"] = psA[0:1, 2560:3584]
            d["ps4"] = psA[:, 0:4096]
            d["ps5_s"] = psA[0:1, 0:1024]
            d["ps5_c"] = psA[:, 1024:1536]
            d["ps5_t"] = psA[:, 1536:3584]

            sems = {}
            with contextlib.ExitStack() as sctx:
                for nm in SEM_NAMES:
                    sems[nm] = sctx.enter_context(nc.semaphore(nm))
                with nc.Block() as block:

                    @block.sync
                    def _(eng):
                        _emit(Walk("sync", eng, sems), d)

                    @block.vector
                    def _(eng):
                        _emit(Walk("vector", eng, sems), d)

                    @block.scalar
                    def _(eng):
                        _emit(Walk("scalar", eng, sems), d)

                    @block.tensor
                    def _(eng):
                        _emit(Walk("tensor", eng, sems), d)

                    @block.gpsimd
                    def _(eng):
                        _emit(Walk("gpsimd", eng, sems), d)
    return nc


def _prep_inputs(inputs):
    x = np.asarray(inputs["L0_attn_x"], dtype=np.float32)
    Wqk = np.asarray(inputs["L0_attn_W_qk"], dtype=np.float32)
    bqk = np.asarray(inputs["L0_attn_b_qk"], dtype=np.float32)
    lw = np.asarray(inputs["L0_attn_lepe_w"], dtype=np.float32)
    lb = np.asarray(inputs["L0_attn_lepe_b"], dtype=np.float32)
    wgate = np.asarray(inputs["L0_ffn_w_gate"], dtype=np.float32)
    bg = np.asarray(inputs["L0_ffn_b_gate"], dtype=np.float32)
    W1 = np.ascontiguousarray(np.asarray(inputs["L0_ffn_W1"], dtype=np.float32))
    b1 = np.asarray(inputs["L0_ffn_b1"], dtype=np.float32)
    W2 = np.ascontiguousarray(np.asarray(inputs["L0_ffn_W2"], dtype=np.float32))
    b2 = np.asarray(inputs["L0_ffn_b2"], dtype=np.float32)

    common = dict(
        wqkT=np.ascontiguousarray(Wqk.T),
        bqk_q=np.ascontiguousarray(bqk[:C].reshape(8, 128).T),
        bqk_qn=np.ascontiguousarray(-bqk[:C].reshape(8, 128).T),
        bqk_k=np.ascontiguousarray(bqk[C:].reshape(1, C)),
        lepe_wp=np.ascontiguousarray(lw.reshape(8, 128, 3).transpose(1, 0, 2).reshape(128, 24)),
        lepe_bp=np.ascontiguousarray(lb.reshape(8, 128).T),
        lepe_bt_h=np.ascontiguousarray(
            np.repeat(lb.reshape(8, 128).T.astype(BF)[:, :, None], T, axis=2
                      ).reshape(128, NKC * T)),
        lepe_w1p=np.ascontiguousarray(
            (lw.reshape(8, 128, 3).transpose(1, 0, 2)[:, :, 1] + 1.0).reshape(128, 8)),
        wg=np.ascontiguousarray(wgate.reshape(8, 128, 8).transpose(1, 0, 2).reshape(128, 64)),
        bgate=np.ascontiguousarray(bg.reshape(8, 1)),
        b1p=np.ascontiguousarray(b1.reshape(8, 32, 128).transpose(2, 0, 1).reshape(128, 256)),
        b2r=np.ascontiguousarray(b2),
        w1=W1, w2=W2,
        ident=np.eye(128, dtype=np.float32),
    )
    cc_idx = np.arange(128)[None, :] // 64
    hrow = np.arange(16)[:, None]
    selz = np.zeros((16, C), dtype=BF)
    maskKM = np.zeros((128, 128), dtype=np.float32)
    for cc in range(8):
        selz[:, cc * 128:(cc + 1) * 128] = (hrow == (2 * cc + cc_idx)).astype(BF)
        maskKM[:, cc * 16:(cc + 1) * 16] = (
            np.arange(16)[None, :] == (2 * cc + np.arange(128)[:, None] // 64)
        ).astype(np.float32)
    selbc = np.zeros((8, C), dtype=BF)
    for e in range(8):
        selbc[e, e * 128:(e + 1) * 128] = BF(1.0)
    unit8 = np.zeros((1, 64), dtype=np.float32)
    for cc in range(8):
        unit8[0, cc * 8 + cc] = 1.0
    common.update(
        maskKM=maskKM, selz=selz, selbc=selbc,
        ident8=np.eye(8, dtype=np.float32).astype(BF),
        ident64=np.eye(64, dtype=np.float32).astype(BF), unit8=unit8,
    )
    in_maps = []
    for core in range(8):
        s, half = divmod(core, 2)
        own = x[s, half * NOWN:(half + 1) * NOWN]
        prev = (x[s, NOWN - 16:NOWN] if half == 1
                else np.zeros((16, C), np.float32))
        nxt = (x[s, NOWN:NOWN + 16] if half == 0
               else np.zeros((16, C), np.float32))
        m = dict(common)
        m["xseq"] = np.ascontiguousarray(x[s])
        m["xpad"] = np.ascontiguousarray(np.concatenate([prev, own, nxt], axis=0))
        in_maps.append(m)
    return in_maps


def kernel(**inputs):
    in_maps = _prep_inputs(inputs)
    nc = build_program()
    res = run_bass_kernel_spmd(nc, in_maps, core_ids=list(range(8)))
    full = np.empty((B, N, C), dtype=np.float32)
    for core in range(8):
        s, half = divmod(core, 2)
        full[s, half * NOWN:(half + 1) * NOWN] = res.results[core]["out"]
    return full


# version 07
# speedup vs baseline: 1.0992x; 1.0025x over previous
import numpy as np
import ml_dtypes
import concourse.bass as bass
import concourse.mybir as mybir
from concourse.bass_utils import run_bass_kernel_spmd

F32 = mybir.dt.float32
BF16 = mybir.dt.bfloat16
AF = mybir.ActivationFunctionType
ALU = mybir.AluOpType
AX = mybir.AxisListType.X
BF = ml_dtypes.bfloat16

B, N, C = 4, 8192, 1024
H = 16
NOWN = 4096
T = 512
NP1 = N // T
NP3 = NOWN // T
NKC = 8
NE, NQ, NMC, NOC = 8, 4, 8, 8
MLP = 4096
QM = MLP // NQ

SEM_NAMES = [
    "d_stage", "d_stgg", "d_wqk",
    "d_x1a", "d_x1b", "d_t1a", "d_t1b",
    "d_x3a", "d_x3b",
    "d_xp4a", "d_xp4b", "d_w4a", "d_w4b",
    "d_yla", "d_ylb", "d_b2a", "d_b2b",
    "s_ysta", "s_ystb", "s_xlw", "s_x32",
    "d_x5a", "d_x5b", "s_outa", "s_outb",
    "c_pe", "c_act", "c_dve", "c_pool",
]


class Walk:
    def __init__(self, who, eng, sems):
        self.who = who
        self.eng = eng
        self.sems = sems
        self.S = {k: 0 for k in sems}

    def inc(self, instr, sem, k=1):
        if instr is not None:
            instr.then_inc(self.sems[sem], k)
        self.S[sem] += k
        return self.S[sem]

    def wait(self, who, sem, val):
        if val <= 0:
            return
        if self.who == who:
            self.eng.wait_ge(self.sems[sem], val)

    def on(self, who):
        return self.who == who

    OWN = {"vector": "c_dve", "scalar": "c_act", "gpsimd": "c_pool",
           "tensor": "c_pe"}

    def step(self, who, instr):
        sem = self.OWN[who]
        v = self.inc(instr, sem, 1)
        self.wait(who, sem, v)
        return v


class Carver:

    def __init__(self, arena, elems):
        self.arena = arena
        self.total = elems
        self.off = 0
        self.high = 0

    def take(self, elems, parts=128):
        off = (self.off + 15) & ~15
        self.off = off + elems
        self.high = max(self.high, self.off)
        assert self.off <= self.total, (self.off, self.total)
        return self.arena[0:parts, off:off + elems]

    def mark(self):
        return self.off

    def reset(self, off):
        self.off = off


def _emit(P, tn):
    S = P.S
    nc = tn["nc"]
    d = tn

    e = P.eng if P.on("sync") else None
    for dst, src in [
        (d["selz_sb"], d["selz"]), (d["selbc_sb"], d["selbc"]),
        (d["ident8_sb"], d["ident8"]), (d["ident_sb"], d["ident"]),
        (d["maskKM_sb"], d["maskKM"]), (d["unit8_sb"], d["unit8"]),
        (d["bqk_q_sb"], d["bqk_q"]), (d["bqk_qn_sb"], d["bqk_qn"]),
        (d["lepe_w_sb"], d["lepe_wp"]),
        (d["lepe_b_sb"], d["lepe_bp"]), (d["lepe_bt"], d["lepe_bt_h"]),
        (d["lepe_w1p_sb"], d["lepe_w1p"]), (d["wg_sb"], d["wg"]),
        (d["bgate_sb"], d["bgate"]), (d["b1p_sb"], d["b1p"]),
        (d["ident64_sb"], d["ident64"]),
    ]:
        i = e.dma_start(out=dst[:, :], in_=src[:, :]) if e else None
        P.inc(i, "d_stage", 16)
    g = P.eng if P.on("gpsimd") else None
    i = g.dma_start(out=d["xseq_bf"][:, :], in_=d["xseq"][:, :]) if g else None
    P.inc(i, "d_stgg", 16)
    i = g.dma_start(out=d["xpad_bf"][:, :], in_=d["xpad"][:, :]) if g else None
    P.inc(i, "d_stgg", 16)
    i = g.dma_start(out=d["bqkk_sb"][:, :], in_=d["bqk_k"][:, :]) if g else None
    P.inc(i, "d_stgg", 16)
    i = (g.dma_start(
        out=d["wqk_sb"].rearrange("p (kc f) -> p kc f", kc=NKC),
        in_=d["wqkT"].rearrange("(kc p) f -> p kc f", p=128)[:, :, C:2 * C],
    ) if g else None)
    P.inc(i, "d_wqk", 16)
    d_stage_all = S["d_stage"]
    d_stgg_all = S["d_stgg"]

    for dst, val in [(d["ones_r_bf"], 1.0), (d["ones_c_bf"], 1.0),
                     (d["one1_bf"], 1.0), (d["one1_f"], 1.0), (d["ones_r_f"], 1.0),
                     (d["ones_c_f"], 1.0), (d["logits_sb"], 0.0)]:
        ms = P.eng.memset(dst[:, :], val) if P.on("vector") else None
        v_const = P.step("vector", ms)

    wqk_sb = d["wqk_sb"]
    ones_r_bf, ones_c_bf, one1_bf = d["ones_r_bf"], d["ones_c_bf"], d["one1_bf"]
    ones_r_f, ones_c_f = d["ones_r_f"], d["ones_c_f"]

    p1_xfm, p1_xtm, p1_ktm = d["p1_xfm"], d["p1_xtm"], d["p1_ktm"]
    p1_r, p1_m = d["p1_r"], d["p1_m"]
    ps_kv, ps_ks, ps_kt, ps_sm = d["ps_kv"], d["ps_ks"], d["ps_kt"], d["ps_sm"]

    xfm_free = [0, 0]
    xfm_done = [0, 0]
    xtm_free = [0, 0]
    xtm_done = [0, 0]
    elu_done = [0, 0, 0, 0]
    ktm_free = [0, 0]
    first_ks = [True, True]
    first_kv = [True, True]
    reg = 0
    pend_kv = []

    def p1_load(panel):
        par = panel % 2
        sem = "d_x1a" if par == 0 else "d_x1b"
        t0 = panel * T
        P.wait("sync", "d_stage", d_stage_all)
        P.wait("sync", "d_stgg", d_stgg_all)
        P.wait("sync", "c_pe", xfm_free[par])
        e = P.eng if P.on("sync") else None
        for cc in range(NKC):
            i = (e.dma_start_transpose(
                p1_xfm[:, (par * NKC + cc) * T:(par * NKC + cc + 1) * T],
                d["xseq_bf"][t0:t0 + T, cc * 128:(cc + 1) * 128],
            ) if e else None)
            P.inc(i, sem, 16)
        xfm_done[par] = S[sem]

    def p1_load_tm(panel):
        par = panel % 2
        sem = "d_t1a" if par == 0 else "d_t1b"
        P.wait("gpsimd", "d_stage", d_stage_all)
        P.wait("gpsimd", "d_stgg", d_stgg_all)
        P.wait("gpsimd", "c_pe", xtm_free[par])
        e = P.eng if P.on("gpsimd") else None
        i = (e.dma_start(
            out=p1_xtm.rearrange("p (b j f) -> p b j f", b=2, j=4)[:, par],
            in_=d["xseq_bf"].rearrange("(pp j p) f -> pp p j f", j=4, p=128)[panel],
        ) if e else None)
        P.inc(i, sem, 16)
        xtm_done[par] = S[sem]

    p1_load(0)
    p1_load_tm(0)
    for panel in range(NP1):
        par = panel % 2
        semf = "d_x1a" if par == 0 else "d_x1b"
        semt = "d_t1a" if par == 0 else "d_t1b"
        need_f = xfm_done[par]
        need_t = xtm_done[par]
        if panel + 1 < NP1:
            p1_load(panel + 1)
            p1_load_tm(panel + 1)
        pe_j = S["c_pe"]
        for j in range(4):
            jp = j % 2
            for half in range(2):
                bpar = reg % 4
                reg += 1
                P.wait("tensor", semf, need_f)
                if panel == 0 and j == 0:
                    P.wait("tensor", "d_wqk", 16)
                    P.wait("tensor", "d_stage", d_stage_all)
                    P.wait("tensor", "d_stgg", d_stgg_all)
                    P.wait("tensor", "c_dve", v_const)
                P.wait("tensor", "c_dve", elu_done[bpar])
                mm = None
                for kc in range(NKC):
                    if P.on("tensor"):
                        mm = P.eng.matmul(
                            ps_kt[:, bpar * T:(bpar + 1) * T],
                            p1_xfm[:, (par * NKC + kc) * T + j * 128:
                                   (par * NKC + kc) * T + (j + 1) * 128],
                            wqk_sb[:, kc * 1024 + half * T:kc * 1024 + (half + 1) * T],
                            start=(kc == 0), stop=False, skip_group_check=True)
                if P.on("tensor"):
                    mm = P.eng.matmul(
                        ps_kt[:, bpar * T:(bpar + 1) * T],
                        ones_r_bf[:, :], d["bqkk_sb"][:, half * T:(half + 1) * T],
                        start=False, stop=True, skip_group_check=True)
                pe_reg = P.inc(mm, "c_pe", 1)

                P.wait("scalar", "c_pe", pe_reg)
                P.wait("scalar", "c_dve", elu_done[bpar])
                a1 = (P.eng.activation(p1_r[:, bpar * T:(bpar + 1) * T],
                                       ps_kt[:, bpar * T:(bpar + 1) * T], AF.Relu)
                      if P.on("scalar") else None)
                P.step("scalar", a1)
                a2 = (P.eng.activation(p1_m[:, bpar * T:(bpar + 1) * T],
                                       ps_kt[:, bpar * T:(bpar + 1) * T], AF.Relu,
                                       scale=-1.0)
                      if P.on("scalar") else None)
                P.step("scalar", a2)
                a3 = (P.eng.activation(p1_m[:, bpar * T:(bpar + 1) * T],
                                       p1_m[:, bpar * T:(bpar + 1) * T], AF.Exp,
                                       scale=-1.0)
                      if P.on("scalar") else None)
                a3v = P.step("scalar", a3)
                P.wait("vector", "c_act", a3v)
                P.wait("vector", "c_pe", ktm_free[jp])
                d2 = (P.eng.tensor_tensor(
                    p1_ktm[:, jp * C + half * T:jp * C + (half + 1) * T],
                    p1_r[:, bpar * T:(bpar + 1) * T],
                    p1_m[:, bpar * T:(bpar + 1) * T], ALU.add)
                    if P.on("vector") else None)
                elu_done[bpar] = P.step("vector", d2)

            pend_kv.append((panel, j, jp, elu_done[(reg - 2) % 4],
                            elu_done[(reg - 1) % 4], xtm_done[par]))
            if len(pend_kv) >= 2 or j == 3:
                todo = pend_kv if j == 3 else [pend_kv.pop(0)]
                for (kp, kj, kjp, ed0, ed1, xtd) in todo:
                    kpar = kp % 2
                    P.wait("tensor", "c_dve", ed0)
                    P.wait("tensor", "c_dve", ed1)
                    P.wait("tensor", "d_t1a" if kpar == 0 else "d_t1b", xtd)
                    mm = None
                    for half in range(2):
                        if P.on("tensor"):
                            mm = P.eng.matmul(
                                ps_ks[:, half * T:(half + 1) * T],
                                ones_c_bf[:, :],
                                p1_ktm[:, kjp * C + half * T:kjp * C + (half + 1) * T],
                                start=first_ks[half], stop=False,
                                skip_group_check=True)
                        first_ks[half] = False
                    for hh in range(H):
                        bank = hh // 8
                        st = first_kv[bank] and (hh % 8 == 0)
                        if P.on("tensor"):
                            mm = P.eng.matmul(
                                ps_kv[:, hh * 64:(hh + 1) * 64],
                                p1_ktm[:, kjp * C + hh * 64:kjp * C + hh * 64 + 64],
                                p1_xtm[:, (kpar * 4 + kj) * C + hh * 64:
                                       (kpar * 4 + kj) * C + hh * 64 + 64],
                                start=st, stop=False, skip_group_check=True)
                        if hh % 8 == 0:
                            first_kv[bank] = False
                    ktm_free[kjp] = pe_j = P.inc(mm, "c_pe", 1)
                    if kj == 3:
                        xtm_free[kpar] = pe_j
                        xfm_free[kpar] = pe_j
                if j == 3:
                    pend_kv.clear()
    pe1_done = S["c_pe"]

    P.wait("gpsimd", "c_pe", pe1_done)
    i = (P.eng.dma_start(
        out=wqk_sb.rearrange("p (kc f) -> p kc f", kc=NKC),
        in_=d["wqkT"].rearrange("(kc p) f -> p kc f", p=128)[:, :, 0:C],
    ) if P.on("gpsimd") else None)
    P.inc(i, "d_wqk", 16)
    wqk_q_ready = S["d_wqk"]

    P.wait("scalar", "c_pe", pe1_done)
    a = (P.eng.activation(d["kv_bf"][:, :], ps_kv[:, :], AF.Identity, scale=1.0 / N)
         if P.on("scalar") else None)
    P.step("scalar", a)
    a = (P.eng.activation(d["kmean_r"][:, :], ps_ks[:, :], AF.Identity, scale=1.0 / N)
         if P.on("scalar") else None)
    act_km = P.step("scalar", a)

    P.wait("tensor", "c_act", act_km)
    mm = None
    for hf in range(2):
        if P.on("tensor"):
            mm = P.eng.matmul(
                ps_kt[64:128, hf * T:(hf + 1) * T],
                d["ident64_sb"][:, :],
                d["kv_bf"][:, hf * T:(hf + 1) * T],
                start=True, stop=True, skip_group_check=True)
    pe_kv2 = P.inc(mm, "c_pe", 1)
    P.wait("scalar", "c_pe", pe_kv2)
    a = (P.eng.activation(d["kv2_bf"][64:128, :], ps_kt[64:128, 0:1024], AF.Identity)
         if P.on("scalar") else None)
    P.step("scalar", a)
    mm = None
    for cc in range(NKC):
        if P.on("tensor"):
            mm = P.eng.matmul(
                ps_sm[:, 0:8],
                d["kmean_r"][:, cc * 128:(cc + 1) * 128],
                d["unit8_sb"][:, cc * 8:(cc + 1) * 8],
                start=(cc == 0), stop=(cc == NKC - 1), skip_group_check=True)
    pe_kme = P.inc(mm, "c_pe", 1)
    P.wait("scalar", "c_pe", pe_kme)
    a = (P.eng.activation(d["kme_pm"][:, :], ps_sm[:, 0:8], AF.Identity)
         if P.on("scalar") else None)
    act_kme = P.step("scalar", a)
    P.wait("vector", "c_act", act_kme)
    for cc in range(NKC):
        dd = (P.eng.tensor_scalar(
            d["KM_bf"][:, cc * 16:(cc + 1) * 16],
            d["maskKM_sb"][:, cc * 16:(cc + 1) * 16],
            d["kme_pm"][:, cc:cc + 1], None, ALU.mult)
            if P.on("vector") else None)
        dve_km = P.step("vector", dd)
    P.wait("tensor", "c_dve", dve_km)
    mm = None
    for cc in range(NKC):
        if P.on("tensor"):
            mm = P.eng.matmul(
                ps_sm[0:16, 8:9],
                d["KM_bf"][:, cc * 16:(cc + 1) * 16],
                ones_c_bf[:, :],
                start=(cc == 0), stop=(cc == NKC - 1), skip_group_check=True)
    pe_zb = P.inc(mm, "c_pe", 1)
    P.wait("scalar", "c_pe", pe_zb)
    a = (P.eng.activation(d["zden_b"][:, :], ps_sm[0:16, 8:9], AF.Identity)
         if P.on("scalar") else None)
    act_zb = P.step("scalar", a)
    P.wait("vector", "c_act", act_zb)
    dd = (P.eng.tensor_scalar(d["zden_b"][:, :], d["zden_b"][:, :], 1e-6, None,
                              ALU.add) if P.on("vector") else None)
    dve_zb = P.step("vector", dd)

    p3_xe, p3_qbf = d["p3_xe"], d["p3_qbf"]
    p3_r, p3_m, p3_z, p3_zbf = d["p3_r"], d["p3_m"], d["p3_z"], d["p3_zbf"]
    p3_zb, p3_t4 = d["p3_zb"], d["p3_t4"]
    p3_ln, p3_sq, p3_st, p3_ab = d["p3_ln"], d["p3_sq"], d["p3_st"], d["p3_ab"]
    p3_lt, p3_gs = d["p3_lt"], d["p3_gs"]
    ps3_q = d["ps4"]
    ps3_h4 = d["ps4"][:, 4 * T:5 * T]
    ps3_s = d["ps3_s"]
    ps3_b5 = d["ps4"][:, 5 * T:6 * T]
    ps3_b6 = d["ps4"][:, 6 * T:7 * T]
    ps3_g7 = d["ps4"][:, 7 * T:8 * T]

    x3_free_pool = [0, 0]
    x3_free_pe = [0, 0]
    x3_done = [0, 0]
    q_elu_done = [0, 0]
    ln_first_done = [0, 0]
    t4_done = [0, 0]
    zb_act = [0, 0]
    zb_free = [0, 0]
    ln_free_pe = [0, 0]
    norm_pool = [0, 0]
    ln_stores = [(0, 0), (0, 0)]
    z_evac_act = 0
    gs_free = [0, 0, 0, 0]
    logit_tp_done = 0
    qbf_free = 0
    zbf_free = 0
    g_evac_last = 0
    a_pe_end = [0, 0]
    a_dve_end = [0, 0]

    pe2_done = S["c_pe"]
    dve2_done = S["c_dve"]

    def p3_load(panel):
        par = panel % 2
        sem = "d_x3a" if par == 0 else "d_x3b"
        t0 = panel * T
        P.wait("sync", "d_stage", d_stage_all)
        P.wait("sync", "d_stgg", d_stgg_all)
        if panel <= 1:
            P.wait("sync", "c_pe", pe2_done)
            P.wait("sync", "c_dve", dve2_done)
        P.wait("sync", "c_dve", x3_free_pool[par])
        P.wait("sync", "c_pe", x3_free_pe[par])
        e = P.eng if P.on("sync") else None
        for cc in range(NKC):
            i = (e.dma_start_transpose(
                p3_xe[:, (par * NKC + cc) * 544:(par * NKC + cc + 1) * 544],
                d["xpad_bf"][t0:t0 + 544, cc * 128:(cc + 1) * 128],
            ) if e else None)
            P.inc(i, sem, 16)
        x3_done[par] = S[sem]

    def emit_A(panel):
        nonlocal qbf_free, zbf_free, z_evac_act
        par = panel % 2
        sem = "d_x3a" if par == 0 else "d_x3b"
        need_x = x3_done[par]
        t0 = panel * T
        if panel + 1 < NP3:
            p3_load(panel + 1)

        if panel == 0:
            P.wait("tensor", "d_wqk", wqk_q_ready)
            P.wait("tensor", "c_act", act_zb)

        for cc in range(NKC):
            bpar = cc % 2
            qb = cc % 4
            ps_q = ps3_q[:, qb * T:(qb + 1) * T]
            P.wait("tensor", sem, need_x)
            P.wait("tensor", "c_dve", q_elu_done[bpar])
            if qb >= 2:
                P.wait("tensor", "c_act", zb_act[qb - 2])
            else:
                P.wait("tensor", "c_dve", ln_first_done[qb])
            mm = None
            for kc in range(NKC):
                if P.on("tensor"):
                    mm = P.eng.matmul(
                        ps_q,
                        wqk_sb[:, kc * 1024 + cc * 128:kc * 1024 + (cc + 1) * 128],
                        p3_xe[:, (par * NKC + kc) * 544 + 16:(par * NKC + kc) * 544 + 528],
                        start=(kc == 0), stop=(kc == NKC - 1), skip_group_check=True)
            pe_q = P.inc(mm, "c_pe", 1)
            P.wait("scalar", "c_pe", pe_q)
            P.wait("scalar", "c_dve", q_elu_done[bpar])
            a1 = (P.eng.activation(p3_r[:, bpar * T:(bpar + 1) * T], ps_q,
                                   AF.Relu, bias=d["bqk_q_sb"][:, cc:cc + 1])
                  if P.on("scalar") else None)
            P.step("scalar", a1)
            a2 = (P.eng.activation(p3_m[:, bpar * T:(bpar + 1) * T], ps_q,
                                   AF.Relu, scale=-1.0,
                                   bias=d["bqk_qn_sb"][:, cc:cc + 1])
                  if P.on("scalar") else None)
            P.step("scalar", a2)
            a3 = (P.eng.activation(p3_m[:, bpar * T:(bpar + 1) * T],
                                   p3_m[:, bpar * T:(bpar + 1) * T], AF.Exp,
                                   scale=-1.0)
                  if P.on("scalar") else None)
            a3v = P.step("scalar", a3)
            P.wait("vector", "c_act", a3v)
            P.wait("vector", "c_pe", qbf_free)
            d3 = (P.eng.tensor_tensor(p3_qbf[:, cc * T:(cc + 1) * T],
                                      p3_r[:, bpar * T:(bpar + 1) * T],
                                      p3_m[:, bpar * T:(bpar + 1) * T], ALU.add)
                  if P.on("vector") else None)
            q_elu_done[bpar] = P.step("vector", d3)
        q_all = S["c_dve"]

        P.wait("tensor", "c_dve", q_all)
        P.wait("tensor", "c_act", z_evac_act)
        mm = None
        for cc in range(NKC):
            if P.on("tensor"):
                mm = P.eng.matmul(
                    ps3_h4[0:16, :],
                    d["KM_bf"][:, cc * 16:(cc + 1) * 16],
                    p3_qbf[:, cc * T:(cc + 1) * T],
                    start=(cc == 0), stop=(cc == NKC - 1), skip_group_check=True)
        pe_zden = P.inc(mm, "c_pe", 1)
        P.wait("scalar", "c_pe", pe_zden)
        P.wait("scalar", "c_dve", dve_zb)
        a = (P.eng.activation(p3_z[:, :], ps3_h4[0:16, :], AF.Identity,
                              bias=d["zden_b"][:, :]) if P.on("scalar") else None)
        z_evac_act = P.step("scalar", a)
        P.wait("vector", "c_act", z_evac_act)
        dd = P.eng.reciprocal(p3_z[:, :], p3_z[:, :]) if P.on("vector") else None
        P.step("vector", dd)
        P.wait("vector", "c_pe", zbf_free)
        dd = P.eng.tensor_copy(p3_zbf[:, :], p3_z[:, :]) if P.on("vector") else None
        z_ready = P.step("vector", dd)

        for cc in range(NKC):
            cpar = cc % 2
            ps_at = ps3_q[:, cpar * T:(cpar + 1) * T]
            ps_zb = ps3_q[:, (2 + cpar) * T:(3 + cpar) * T]
            P.wait("tensor", "c_dve", z_ready)
            P.wait("tensor", "c_act", zb_act[cpar])
            mm = (P.eng.matmul(
                ps_zb,
                d["selz_sb"][:, cc * 128:(cc + 1) * 128],
                p3_zbf[:, :], start=True, stop=True, skip_group_check=True)
                if P.on("tensor") else None)
            pe_zbc = P.inc(mm, "c_pe", 1)
            if cc == NKC - 1:
                zbf_free = pe_zbc
            P.wait("scalar", "c_pe", pe_zbc)
            P.wait("scalar", "c_dve", zb_free[cpar])
            a = (P.eng.activation(p3_zb[:, cpar * T:(cpar + 1) * T], ps_zb,
                                  AF.Identity) if P.on("scalar") else None)
            zb_act[cpar] = P.step("scalar", a)

            P.wait("tensor", "c_dve", ln_first_done[cpar])
            mm = None
            for hi in range(2):
                hh = 2 * cc + hi
                kvop = (d["kv_bf"][:, hh * 64:(hh + 1) * 64] if hi == 0 else
                        d["kv2_bf"][64:128, hh * 64:(hh + 1) * 64])
                if P.on("tensor"):
                    mm = P.eng.matmul(
                        ps_at[hi * 64:(hi + 1) * 64, :],
                        kvop,
                        p3_qbf[hi * 64:(hi + 1) * 64, cc * T:(cc + 1) * T],
                        start=True, stop=True, skip_group_check=True)
            pe_at = P.inc(mm, "c_pe", 1)
            if cc == NKC - 1:
                qbf_free = pe_at

            lnoff = (par * NKC + cc) * T
            P.wait("vector", "c_pe", pe_at)
            P.wait("vector", "c_act", zb_act[cpar])
            P.wait("vector", "c_pe", ln_free_pe[par])
            P.wait("vector", "c_pool", norm_pool[par])
            P.wait("vector", "s_x32", ln_stores[par][0])
            P.wait("vector", "s_xlw", ln_stores[par][1])
            dd = (P.eng.tensor_tensor(p3_ln[:, lnoff:lnoff + T],
                                      ps_at, p3_zb[:, cpar * T:(cpar + 1) * T],
                                      ALU.mult) if P.on("vector") else None)
            zb_free[cpar] = ln_first_done[cpar] = P.step("vector", dd)

            xb = (par * NKC + cc) * 544
            P.wait("vector", sem, need_x)
            P.wait("vector", "c_dve", t4_done[cpar])
            P.wait("vector", "c_dve", v_const)
            pl = (P.eng.scalar_tensor_tensor(p3_t4[:, cpar * T:(cpar + 1) * T],
                                             p3_xe[:, xb + 15:xb + 527],
                                             d["lepe_w_sb"][:, cc * 3:cc * 3 + 1],
                                             d["lepe_bt"][:, cc * T:(cc + 1) * T],
                                             ALU.mult, ALU.add)
                  if P.on("vector") else None)
            P.step("vector", pl)
            pl = (P.eng.scalar_tensor_tensor(p3_t4[:, cpar * T:(cpar + 1) * T],
                                             p3_xe[:, xb + 16:xb + 528],
                                             d["lepe_w1p_sb"][:, cc:cc + 1],
                                             p3_t4[:, cpar * T:(cpar + 1) * T],
                                             ALU.mult, ALU.add)
                  if P.on("vector") else None)
            P.step("vector", pl)
            pl = (P.eng.scalar_tensor_tensor(p3_t4[:, cpar * T:(cpar + 1) * T],
                                             p3_xe[:, xb + 17:xb + 529],
                                             d["lepe_w_sb"][:, cc * 3 + 2:cc * 3 + 3],
                                             p3_t4[:, cpar * T:(cpar + 1) * T],
                                             ALU.mult, ALU.add)
                  if P.on("vector") else None)
            P.step("vector", pl)
            dd = (P.eng.tensor_tensor(p3_ln[:, lnoff:lnoff + T],
                                      p3_ln[:, lnoff:lnoff + T],
                                      p3_t4[:, cpar * T:(cpar + 1) * T], ALU.add)
                  if P.on("vector") else None)
            t4_done[cpar] = P.step("vector", dd)
        x3_free_pool[par] = S["c_dve"]
        a_pe_end[par] = S["c_pe"]
        a_dve_end[par] = S["c_dve"]

    def emit_B(panel):
        nonlocal logit_tp_done, g_evac_last
        par = panel % 2
        t0 = panel * T
        lnb = par * NKC * T
        ln_ready = a_dve_end[par]

        mm = None
        for cc in range(NKC):
            P.wait("gpsimd", "c_dve", ln_ready)
            dd = (P.eng.tensor_tensor(p3_sq[:, :], p3_ln[:, lnb + cc * T:lnb + (cc + 1) * T],
                                      p3_ln[:, lnb + cc * T:lnb + (cc + 1) * T], ALU.mult)
                  if P.on("gpsimd") else None)
            sqv = P.step("gpsimd", dd)
            P.wait("tensor", "c_dve", ln_ready)
            P.wait("tensor", "c_pool", sqv)
            if cc == 0:
                P.wait("tensor", "c_act", g_evac_last)
            if P.on("tensor"):
                mm = P.eng.matmul(ps3_s[:, 0:T], ones_c_f[:, :],
                                  p3_ln[:, lnb + cc * T:lnb + (cc + 1) * T],
                                  start=(cc == 0), stop=(cc == NKC - 1),
                                  skip_group_check=True)
                mm = P.eng.matmul(ps3_s[:, T:2 * T], ones_c_f[:, :], p3_sq[:, :],
                                  start=(cc == 0), stop=(cc == NKC - 1),
                                  skip_group_check=True)
            pe_st = P.inc(mm, "c_pe", 1)
            P.wait("gpsimd", "c_pe", pe_st)
        pe_stats = S["c_pe"]

        P.wait("scalar", "c_pe", pe_stats)
        a = (P.eng.activation(p3_st[:, 0:T], ps3_s[:, 0:T], AF.Identity, scale=1.0 / C)
             if P.on("scalar") else None)
        P.step("scalar", a)
        a = (P.eng.activation(p3_st[:, T:2 * T], ps3_s[:, T:2 * T], AF.Identity,
                              scale=1.0 / C) if P.on("scalar") else None)
        av = P.step("scalar", a)
        P.wait("vector", "c_act", av)
        dd = (P.eng.tensor_tensor(p3_st[:, 2 * T:3 * T], p3_st[:, 0:T],
                                  p3_st[:, 0:T], ALU.mult) if P.on("vector") else None)
        P.step("vector", dd)
        dd = (P.eng.scalar_tensor_tensor(p3_st[:, 2 * T:3 * T], p3_st[:, T:2 * T],
                                         1e-5, p3_st[:, 2 * T:3 * T],
                                         ALU.add, ALU.subtract)
              if P.on("vector") else None)
        dvar = P.step("vector", dd)
        P.wait("scalar", "c_dve", dvar)
        a = (P.eng.activation(p3_st[:, 2 * T:3 * T], p3_st[:, 2 * T:3 * T], AF.Ln)
             if P.on("scalar") else None)
        P.step("scalar", a)
        a = (P.eng.activation(p3_st[:, 2 * T:3 * T], p3_st[:, 2 * T:3 * T], AF.Exp,
                              scale=-0.5) if P.on("scalar") else None)
        arstd = P.step("scalar", a)
        P.wait("vector", "c_act", arstd)
        dd = (P.eng.tensor_tensor(p3_st[:, 3 * T:4 * T], p3_st[:, 0:T],
                                  p3_st[:, 2 * T:3 * T], ALU.mult)
              if P.on("vector") else None)
        P.step("vector", dd)
        dd = (P.eng.tensor_scalar(p3_st[:, 3 * T:4 * T], p3_st[:, 3 * T:4 * T],
                                  -1.0, None, ALU.mult) if P.on("vector") else None)
        dbsh = P.step("vector", dd)
        P.wait("tensor", "c_dve", dbsh)
        mm = (P.eng.matmul(ps3_b5, ones_r_f[:, :], p3_st[:, 2 * T:3 * T],
                           start=True, stop=True, skip_group_check=True)
              if P.on("tensor") else None)
        pe_a = P.inc(mm, "c_pe", 1)
        P.wait("scalar", "c_pe", pe_a)
        a = (P.eng.activation(p3_ab[:, 0:T], ps3_b5, AF.Identity)
             if P.on("scalar") else None)
        aa = P.step("scalar", a)
        P.wait("tensor", "c_act", aa)
        mm = (P.eng.matmul(ps3_b5, ones_r_f[:, :], p3_st[:, 3 * T:4 * T],
                           start=True, stop=True, skip_group_check=True)
              if P.on("tensor") else None)
        pe_b = P.inc(mm, "c_pe", 1)
        P.wait("scalar", "c_pe", pe_b)
        a = (P.eng.activation(p3_ab[:, T:2 * T], ps3_b5, AF.Identity)
             if P.on("scalar") else None)
        ab_ready = P.step("scalar", a)

        gate_first = True
        norm_done = 0
        for cc in range(NKC):
            P.wait("gpsimd", "c_act", ab_ready)
            dd = (P.eng.tensor_tensor(p3_ln[:, lnb + cc * T:lnb + (cc + 1) * T],
                                      p3_ln[:, lnb + cc * T:lnb + (cc + 1) * T],
                                      p3_ab[:, 0:T], ALU.mult)
                  if P.on("gpsimd") else None)
            P.step("gpsimd", dd)
            dd = (P.eng.tensor_tensor(p3_ln[:, lnb + cc * T:lnb + (cc + 1) * T],
                                      p3_ln[:, lnb + cc * T:lnb + (cc + 1) * T],
                                      p3_ab[:, T:2 * T], ALU.add)
                  if P.on("gpsimd") else None)
            norm_done = P.step("gpsimd", dd)
            P.wait("tensor", "c_pool", norm_done)
            mm = (P.eng.matmul(ps3_g7[0:8, :], d["wg_sb"][:, cc * 8:(cc + 1) * 8],
                               p3_ln[:, lnb + cc * T:lnb + (cc + 1) * T],
                               start=gate_first, stop=(cc == NKC - 1),
                               skip_group_check=True)
                  if P.on("tensor") else None)
            gate_first = False
            P.inc(mm, "c_pe", 1)
        pe_gate = S["c_pe"]
        ln_free_pe[par] = pe_gate
        norm_all = norm_done
        norm_pool[par] = norm_done

        P.wait("sync", "c_pool", norm_all)
        e = P.eng if P.on("sync") else None
        for cc in range(NKC):
            i = (e.dma_start(out=d["xln32_d"][cc, :, t0:t0 + T],
                             in_=p3_ln[:, lnb + cc * T:lnb + (cc + 1) * T]) if e else None)
            P.inc(i, "s_x32", 16)
        g = P.eng if P.on("gpsimd") else None
        for cc in range(NKC):
            i2 = (g.dma_start(out=d["xlnbf_d"][cc, :, t0:t0 + T],
                              in_=p3_ln[:, lnb + cc * T:lnb + (cc + 1) * T]) if g else None)
            P.inc(i2, "s_xlw", 16)
        ln_stores[par] = (S["s_x32"], S["s_xlw"])

        P.wait("scalar", "c_pe", pe_gate)
        P.wait("scalar", "c_pe", logit_tp_done)
        a = (P.eng.activation(d["logits_sb"][0:8, :], ps3_g7[0:8, :], AF.Identity,
                              bias=d["bgate_sb"][:, :]) if P.on("scalar") else None)
        a_log = P.step("scalar", a)
        for jb in range(4):
            P.wait("tensor", "c_act", a_log)
            P.wait("tensor", "c_act", g_evac_last)
            mm = (P.eng.transpose(ps3_b6[:, 0:128],
                                  d["logits_sb"][:, jb * 128:(jb + 1) * 128],
                                  d["ident_sb"][:, :]) if P.on("tensor") else None)
            pe_t = P.inc(mm, "c_pe", 1)
            logit_tp_done = pe_t
            P.wait("vector", "c_pe", pe_t)
            dd = (P.eng.tensor_copy(p3_lt[:, jb * 8:(jb + 1) * 8], ps3_b6[:, 0:8])
                  if P.on("vector") else None)
            P.step("vector", dd)
            lt = p3_lt[:, jb * 8:(jb + 1) * 8]
            e8 = p3_gs[:, jb * 8:(jb + 1) * 8]
            oh8 = p3_gs[:, 32 + jb * 8:32 + (jb + 1) * 8]
            mx = p3_gs[:, 64 + jb:65 + jb]
            sm = p3_gs[:, 68 + jb:69 + jb]
            pc = p3_gs[:, 72 + jb:73 + jb]
            P.wait("vector", "c_pe", gs_free[jb])
            dd = P.eng.reduce_max(mx, lt, axis=AX) if P.on("vector") else None
            P.step("vector", dd)
            dd = (P.eng.tensor_scalar(e8, lt, mx, None, ALU.subtract)
                  if P.on("vector") else None)
            P.step("vector", dd)
            dd = (P.eng.tensor_scalar(oh8, lt, mx, None, ALU.is_equal)
                  if P.on("vector") else None)
            dv = P.step("vector", dd)
            P.wait("scalar", "c_dve", dv)
            a = P.eng.activation(e8, e8, AF.Exp) if P.on("scalar") else None
            ae = P.step("scalar", a)
            P.wait("vector", "c_act", ae)
            dd = P.eng.reduce_sum(sm, e8, axis=AX) if P.on("vector") else None
            P.step("vector", dd)
            dd = P.eng.reciprocal(pc, sm) if P.on("vector") else None
            P.step("vector", dd)
            dd = (P.eng.tensor_scalar(oh8, oh8, pc, None, ALU.mult)
                  if P.on("vector") else None)
            dg = P.step("vector", dd)
            P.wait("tensor", "c_dve", dg)
            mm = (P.eng.transpose(ps3_b6[0:8, 0:128], oh8, d["ident_sb"][:, :])
                  if P.on("tensor") else None)
            pe_g = P.inc(mm, "c_pe", 1)
            gs_free[jb] = pe_g
            P.wait("scalar", "c_pe", pe_g)
            a = (P.eng.activation(d["g_bf"][:, t0 + jb * 128:t0 + (jb + 1) * 128],
                                  ps3_b6[0:8, 0:128], AF.Identity)
                 if P.on("scalar") else None)
            g_evac_last = P.step("scalar", a)
        x3_free_pe[par] = max(x3_free_pe[par], a_pe_end[par])

    p3_load(0)
    emit_A(0)
    for panel in range(1, NP3):
        emit_A(panel)
        emit_B(panel - 1)
    emit_B(NP3 - 1)

    pe3_done = S["c_pe"]
    act3_done = S["c_act"]
    dve3_done = S["c_dve"]
    pool3_done = S["c_pool"]
    xlw_all = S["s_xlw"]
    x32_all = S["s_x32"]

    p4_w1, p4_w2, p4_xp = d["p4_w1"], d["p4_w2"], d["p4_xp"]
    p4_h, p4_hr, p4_gb = d["p4_h"], d["p4_hr"], d["p4_gb"]
    p4_gr, p4_b2, p4_yb = d["p4_gr"], d["p4_b2"], d["p4_yb"]
    ps4 = d["ps4"]

    w_free = [0, 0]
    w_done = [0, 0]
    xp_free = [0, 0]
    xp_done = [0, 0]
    yl_done = [0, 0]
    b2_free = [0, 0]
    b2_done = [0, 0]
    bank_evac = [0] * NOC
    h_done_m = [0] * NMC
    h_evac = [0, 0]
    hsb_used = 0
    gb_act = [0, 0]
    gr_act = [0, 0]
    gb_free = [0, 0]
    gr_free = [0, 0]
    hr_free = [0, 0]

    def w_load(it):
        e_, q_ = divmod(it, NQ)
        par = it % 2
        sem = "d_w4a" if par == 0 else "d_w4b"
        P.wait("gpsimd", "c_pe", w_free[par])
        if it == 0:
            P.wait("gpsimd", "c_pe", pe3_done)
            P.wait("gpsimd", "c_pool", pool3_done)
            P.wait("gpsimd", "c_dve", dve3_done)
            P.wait("gpsimd", "c_act", act3_done)
        g = P.eng if P.on("gpsimd") else None
        i = (g.dma_start(
            out=p4_w1.rearrange("p (b kc m) -> p b kc m", b=2, kc=NKC)[:, par],
            in_=d["w1"].rearrange("e (kc p) m -> e p kc m", p=128)[e_][:, :, q_ * QM:(q_ + 1) * QM],
        ) if g else None)
        P.inc(i, sem, 16)
        i = (g.dma_start(
            out=p4_w2.rearrange("p (b mc o) -> p b mc o", b=2, mc=NMC)[:, par],
            in_=d["w2"].rearrange("e (mc p) o -> e p mc o", p=128, mc=32)[e_][:, q_ * NMC:(q_ + 1) * NMC, :],
        ) if g else None)
        P.inc(i, sem, 16)
        w_done[par] = S[sem]

    def b2_load(e_):
        par = e_ % 2
        sem = "d_b2a" if par == 0 else "d_b2b"
        P.wait("gpsimd", "c_pe", b2_free[par])
        if e_ == 0:
            P.wait("gpsimd", "c_pe", pe3_done)
        g = P.eng if P.on("gpsimd") else None
        i = (g.dma_start(out=p4_b2[:, par * C:(par + 1) * C],
                         in_=d["b2r"][e_:e_ + 1, :]) if g else None)
        P.inc(i, sem, 16)
        b2_done[par] = S[sem]

    def xp_load(it, p):
        ip = (it * NP3 + p) % 2
        sem = "d_xp4a" if ip == 0 else "d_xp4b"
        P.wait("sync", "c_pe", xp_free[ip])
        if it == 0 and p <= 1:
            P.wait("sync", "s_xlw", xlw_all)
            P.wait("sync", "c_pe", pe3_done)
            P.wait("sync", "c_pool", pool3_done)
            P.wait("sync", "c_dve", dve3_done)
            P.wait("sync", "c_act", act3_done)
        e = P.eng if P.on("sync") else None
        i = (e.dma_start(
            out=p4_xp.rearrange("p (b kc t) -> p b kc t", b=2, kc=NKC)[:, ip],
            in_=d["xlnbf_d"].rearrange("kc p t -> p kc t")[:, :, p * T:(p + 1) * T],
        ) if e else None)
        P.inc(i, sem, 16)
        xp_done[ip] = S[sem]

    def y_load(it, p):
        ip = (it * NP3 + p) % 2
        sem = "d_yla" if ip == 0 else "d_ylb"
        ssem = "s_ysta" if ip == 0 else "s_ystb"
        P.wait("sync", ssem, S[ssem])
        e = P.eng if P.on("sync") else None
        i = (e.dma_start(
            out=p4_yb.rearrange("p (b oc t) -> p b oc t", b=2, oc=NOC)[:, ip],
            in_=d["y_d"].rearrange("oc p t -> p oc t")[:, :, p * T:(p + 1) * T],
        ) if e else None)
        P.inc(i, sem, 16)
        yl_done[ip] = S[sem]

    w_load(0)
    b2_load(0)
    xp_load(0, 0)
    for it in range(NE * NQ):
        e_, q_ = divmod(it, NQ)
        wpar = it % 2
        wsem = "d_w4a" if wpar == 0 else "d_w4b"
        this_w = w_done[wpar]
        if it + 1 < NE * NQ:
            w_load(it + 1)
        if q_ == NQ - 1 and e_ + 1 < NE:
            b2_load(e_ + 1)
        for p in range(NP3):
            ip = (it * NP3 + p) % 2
            xsem = "d_xp4a" if ip == 0 else "d_xp4b"
            this_xp = xp_done[ip]
            nit, np_ = (it, p + 1) if p + 1 < NP3 else (it + 1, 0)
            if nit < NE * NQ:
                xp_load(nit, np_)
            if it >= 1:
                y_load(it, p)

            if it == 0 and p == 0:
                P.wait("tensor", "c_act", act3_done)
                P.wait("tensor", "c_dve", dve3_done)
            P.wait("tensor", "c_dve", bank_evac[2])
            P.wait("tensor", "c_dve", bank_evac[3])
            P.wait("tensor", "c_act", gr_act[ip])
            P.wait("tensor", "c_act", gb_act[ip])
            mm = (P.eng.matmul(ps4[0:1, 2 * T:3 * T],
                               d["ident8_sb"][:, e_:e_ + 1],
                               d["g_bf"][:, p * T:(p + 1) * T],
                               start=True, stop=True, skip_group_check=True)
                  if P.on("tensor") else None)
            pe_gr = P.inc(mm, "c_pe", 1)
            mm = (P.eng.matmul(ps4[:, 3 * T:4 * T],
                               d["selbc_sb"][:, e_ * 128:(e_ + 1) * 128],
                               d["g_bf"][:, p * T:(p + 1) * T],
                               start=True, stop=True, skip_group_check=True)
                  if P.on("tensor") else None)
            pe_gb = P.inc(mm, "c_pe", 1)
            P.wait("scalar", "c_pe", pe_gr)
            P.wait("scalar", "c_pe", gr_free[ip])
            a = (P.eng.activation(p4_gr[:, ip * T:(ip + 1) * T], ps4[0:1, 2 * T:3 * T],
                                  AF.Identity) if P.on("scalar") else None)
            gr_act[ip] = P.step("scalar", a)
            P.wait("scalar", "c_pe", pe_gb)
            P.wait("scalar", "c_pool", gb_free[ip])
            a = (P.eng.activation(p4_gb[:, ip * T:(ip + 1) * T], ps4[:, 3 * T:4 * T],
                                  AF.Identity) if P.on("scalar") else None)
            gb_act[ip] = P.step("scalar", a)

            P.wait("tensor", wsem, this_w)
            P.wait("tensor", xsem, this_xp)
            for m in range(NMC):
                hb = m % 2
                P.wait("tensor", "c_dve", bank_evac[hb])
                P.wait("tensor", "c_act", h_evac[hb])
                mm = None
                for kc in range(NKC):
                    if P.on("tensor"):
                        mm = P.eng.matmul(
                            ps4[:, hb * T:(hb + 1) * T],
                            p4_w1[:, (wpar * NKC + kc) * QM + m * 128:
                                  (wpar * NKC + kc) * QM + (m + 1) * 128],
                            p4_xp[:, (ip * NKC + kc) * T:(ip * NKC + kc + 1) * T],
                            start=(kc == 0), stop=(kc == NKC - 1),
                            skip_group_check=True)
                pe_h = P.inc(mm, "c_pe", 1)
                hp = m % 2
                P.wait("scalar", "c_pe", pe_h)
                P.wait("scalar", "c_pool", hr_free[hp])
                a = (P.eng.activation(
                    p4_hr[:, hp * T:(hp + 1) * T],
                    ps4[:, hb * T:(hb + 1) * T], AF.Relu,
                    bias=d["b1p_sb"][:, e_ * 32 + q_ * NMC + m:e_ * 32 + q_ * NMC + m + 1])
                    if P.on("scalar") else None)
                h_evac[hb] = P.step("scalar", a)
                P.wait("gpsimd", "c_act", h_evac[hb])
                P.wait("gpsimd", "c_act", gb_act[ip])
                P.wait("gpsimd", "c_pe", hsb_used)
                pl = (P.eng.tensor_tensor(p4_h[:, m * T:(m + 1) * T],
                                          p4_hr[:, hp * T:(hp + 1) * T],
                                          p4_gb[:, ip * T:(ip + 1) * T], ALU.mult)
                      if P.on("gpsimd") else None)
                hr_free[hp] = gb_free[ip] = P.step("gpsimd", pl)
                h_done_m[m] = S["c_pool"]

            P.wait("tensor", "c_act", gr_act[ip])
            last_oc_mm = 0
            for oc in range(NOC):
                P.wait("tensor", "c_dve", bank_evac[oc])
                if oc < 2:
                    P.wait("tensor", "c_act", h_evac[oc])
                mm = None
                first = True
                if q_ == 0:
                    bsem = "d_b2a" if e_ % 2 == 0 else "d_b2b"
                    P.wait("tensor", bsem, b2_done[e_ % 2])
                    if P.on("tensor"):
                        mm = P.eng.matmul(
                            ps4[:, oc * T:(oc + 1) * T],
                            p4_b2[:, (e_ % 2) * C + oc * 128:(e_ % 2) * C + (oc + 1) * 128],
                            p4_gr[:, ip * T:(ip + 1) * T],
                            start=True, stop=False, skip_group_check=True)
                    first = False
                for m in range(NMC):
                    if oc == 0:
                        P.wait("tensor", "c_pool", h_done_m[m])
                    if P.on("tensor"):
                        mm = P.eng.matmul(
                            ps4[:, oc * T:(oc + 1) * T],
                            p4_w2[:, (wpar * NMC + m) * C + oc * 128:
                                  (wpar * NMC + m) * C + (oc + 1) * 128],
                            p4_h[:, m * T:(m + 1) * T],
                            start=first, stop=(m == NMC - 1),
                            skip_group_check=True)
                    first = False
                pe_oc = P.inc(mm, "c_pe", 1)
                last_oc_mm = pe_oc
                P.wait("vector", "c_pe", pe_oc)
                if it == 0:
                    if oc == 0:
                        ssem = "s_ysta" if ip == 0 else "s_ystb"
                        P.wait("vector", ssem, S[ssem])
                    dd = (P.eng.tensor_copy(
                        p4_yb[:, (ip * NOC + oc) * T:(ip * NOC + oc + 1) * T],
                        ps4[:, oc * T:(oc + 1) * T]) if P.on("vector") else None)
                else:
                    ysem = "d_yla" if ip == 0 else "d_ylb"
                    P.wait("vector", ysem, yl_done[ip])
                    dd = (P.eng.tensor_tensor(
                        p4_yb[:, (ip * NOC + oc) * T:(ip * NOC + oc + 1) * T],
                        ps4[:, oc * T:(oc + 1) * T],
                        p4_yb[:, (ip * NOC + oc) * T:(ip * NOC + oc + 1) * T],
                        ALU.add) if P.on("vector") else None)
                bank_evac[oc] = P.step("vector", dd)
            hsb_used = last_oc_mm
            b2_free[e_ % 2] = last_oc_mm
            gr_free[ip] = last_oc_mm
            w_free[wpar] = last_oc_mm
            xp_free[ip] = last_oc_mm

            P.wait("sync", "c_dve", bank_evac[NOC - 1])
            e = P.eng if P.on("sync") else None
            i = (e.dma_start(
                out=d["y_d"].rearrange("oc p t -> p oc t")[:, :, p * T:(p + 1) * T],
                in_=p4_yb.rearrange("p (b oc t) -> p b oc t", b=2, oc=NOC)[:, ip],
            ) if e else None)
            P.inc(i, "s_ysta" if ip == 0 else "s_ystb", 16)
    yst_a = S["s_ysta"]
    yst_b = S["s_ystb"]
    pe4_done = S["c_pe"]
    act4_done = S["c_act"]
    dve4_done = S["c_dve"]
    pool4_done = S["c_pool"]

    p5_x, p5_y, p5_sq = d["p5_x"], d["p5_y"], d["p5_sq"]
    p5_st, p5_ab, p5_otm = d["p5_st"], d["p5_ab"], d["p5_otm"]
    ps5_s, ps5_c, ps5_t = d["ps5_s"], d["ps5_c"], d["ps5_t"]

    x5_free = [0, 0]
    x5_pool_free = [0, 0]
    x5_done = [0, 0]

    def p5_load(panel):
        par = panel % 2
        sem = "d_x5a" if par == 0 else "d_x5b"
        P.wait("sync", "s_ysta", yst_a)
        P.wait("sync", "s_ystb", yst_b)
        P.wait("sync", "s_x32", x32_all)
        P.wait("sync", "c_pe", x5_free[par])
        P.wait("sync", "c_pool", x5_pool_free[par])
        if panel <= 1:
            P.wait("sync", "c_pe", pe4_done)
            P.wait("sync", "c_act", act4_done)
            P.wait("sync", "c_pool", pool4_done)
            P.wait("sync", "c_dve", dve4_done)
        e = P.eng if P.on("sync") else None
        i = (e.dma_start(
            out=p5_x.rearrange("p (b kc t) -> p b kc t", b=2, kc=NKC)[:, par],
            in_=d["xln32_d"].rearrange("kc p t -> p kc t")[:, :, panel * T:(panel + 1) * T],
        ) if e else None)
        P.inc(i, sem, 16)
        i = (e.dma_start(
            out=p5_y.rearrange("p (b oc t) -> p b oc t", b=2, oc=NOC)[:, par],
            in_=d["y_d"].rearrange("oc p t -> p oc t")[:, :, panel * T:(panel + 1) * T],
        ) if e else None)
        P.inc(i, sem, 16)
        x5_done[par] = S[sem]

    otm_free = [0, 0]
    tp_evac = [0, 0, 0, 0]
    ln2_done_by = [0, 0]
    ab_tm_done = [0, 0]

    def emit_C(panel):
        par = panel % 2
        sem = "d_x5a" if par == 0 else "d_x5b"
        need = x5_done[par]
        for cc in range(NKC):
            P.wait("gpsimd", sem, need)
            dd = (P.eng.tensor_tensor(
                p5_x[:, (par * NKC + cc) * T:(par * NKC + cc + 1) * T],
                p5_x[:, (par * NKC + cc) * T:(par * NKC + cc + 1) * T],
                p5_y[:, (par * NKC + cc) * T:(par * NKC + cc + 1) * T], ALU.add)
                if P.on("gpsimd") else None)
            ln2_done_by[par] = P.step("gpsimd", dd)
        mm = None
        for cc in range(NKC):
            dd = (P.eng.tensor_tensor(
                p5_sq[:, :],
                p5_x[:, (par * NKC + cc) * T:(par * NKC + cc + 1) * T],
                p5_x[:, (par * NKC + cc) * T:(par * NKC + cc + 1) * T], ALU.mult)
                if P.on("gpsimd") else None)
            sqv = P.step("gpsimd", dd)
            P.wait("tensor", "c_pool", sqv)
            if P.on("tensor"):
                mm = P.eng.matmul(ps5_s[:, 0:T], ones_c_f[:, :],
                                  p5_x[:, (par * NKC + cc) * T:(par * NKC + cc + 1) * T],
                                  start=(cc == 0), stop=(cc == NKC - 1),
                                  skip_group_check=True)
                mm = P.eng.matmul(ps5_s[:, T:2 * T], ones_c_f[:, :], p5_sq[:, :],
                                  start=(cc == 0), stop=(cc == NKC - 1),
                                  skip_group_check=True)
            pe_st = P.inc(mm, "c_pe", 1)
            P.wait("gpsimd", "c_pe", pe_st)
        pe_stats = S["c_pe"]
        P.wait("scalar", "c_pe", pe_stats)
        a = (P.eng.activation(p5_st[:, 0:T], ps5_s[:, 0:T], AF.Identity, scale=1.0 / C)
             if P.on("scalar") else None)
        P.step("scalar", a)
        a = (P.eng.activation(p5_st[:, T:2 * T], ps5_s[:, T:2 * T], AF.Identity,
                              scale=1.0 / C) if P.on("scalar") else None)
        av = P.step("scalar", a)
        P.wait("vector", "c_act", av)
        dd = (P.eng.tensor_tensor(p5_st[:, 2 * T:3 * T], p5_st[:, 0:T], p5_st[:, 0:T],
                                  ALU.mult) if P.on("vector") else None)
        P.step("vector", dd)
        dd = (P.eng.scalar_tensor_tensor(p5_st[:, 2 * T:3 * T], p5_st[:, T:2 * T],
                                         1e-5, p5_st[:, 2 * T:3 * T],
                                         ALU.add, ALU.subtract)
              if P.on("vector") else None)
        dvar = P.step("vector", dd)
        P.wait("scalar", "c_dve", dvar)
        a = (P.eng.activation(p5_st[:, 2 * T:3 * T], p5_st[:, 2 * T:3 * T], AF.Ln)
             if P.on("scalar") else None)
        P.step("scalar", a)
        a = (P.eng.activation(p5_st[:, 2 * T:3 * T], p5_st[:, 2 * T:3 * T], AF.Exp,
                              scale=-0.5) if P.on("scalar") else None)
        arstd = P.step("scalar", a)
        P.wait("vector", "c_act", arstd)
        dd = (P.eng.tensor_tensor(p5_st[:, 3 * T:4 * T], p5_st[:, 0:T],
                                  p5_st[:, 2 * T:3 * T], ALU.mult)
              if P.on("vector") else None)
        P.step("vector", dd)
        dd = (P.eng.tensor_scalar(p5_st[:, 3 * T:4 * T], p5_st[:, 3 * T:4 * T], -1.0,
                                  None, ALU.mult) if P.on("vector") else None)
        dbsh = P.step("vector", dd)
        P.wait("tensor", "c_dve", dbsh)
        P.wait("tensor", "c_act", ab_tm_done[0])
        P.wait("tensor", "c_act", ab_tm_done[1])
        mm = None
        for j in range(4):
            if P.on("tensor"):
                mm = P.eng.matmul(
                    ps5_c[:, j:j + 1],
                    p5_st[0:1, 2 * T + j * 128:2 * T + (j + 1) * 128],
                    d["one1_f"][:, :], start=(j == 0), stop=False,
                    skip_group_check=True)
                mm = P.eng.matmul(
                    ps5_c[:, 4 + j:5 + j],
                    p5_st[0:1, 3 * T + j * 128:3 * T + (j + 1) * 128],
                    d["one1_f"][:, :], start=False, stop=(j == 3),
                    skip_group_check=True)
        pe_ab = P.inc(mm, "c_pe", 1)
        P.wait("scalar", "c_pe", pe_ab)
        a = (P.eng.activation(p5_ab[:, par * 16:par * 16 + 8], ps5_c[:, 0:8],
                              AF.Identity) if P.on("scalar") else None)
        ab_tm_done[par] = P.step("scalar", a)

    def emit_D(panel):
        par = panel % 2
        t0 = panel * T
        nd = ln2_done_by[par]
        for j in range(4):
            jp = j % 2
            for cc in range(NKC):
                tb = cc % 4
                P.wait("tensor", "c_pool", nd)
                P.wait("tensor", "c_dve", tp_evac[tb])
                mm = (P.eng.transpose(
                    ps5_t[:, tb * 512:tb * 512 + 128],
                    p5_x[:, (par * NKC + cc) * T + j * 128:
                         (par * NKC + cc) * T + (j + 1) * 128],
                    d["ident_sb"][:, :]) if P.on("tensor") else None)
                pe_t = P.inc(mm, "c_pe", 1)
                P.wait("vector", "c_pe", pe_t)
                osem = "s_outa" if jp == 0 else "s_outb"
                P.wait("vector", osem, otm_free[jp])
                dd = (P.eng.tensor_copy(
                    p5_otm[:, jp * C + cc * 128:jp * C + (cc + 1) * 128],
                    ps5_t[:, tb * 512:tb * 512 + 128]) if P.on("vector") else None)
                tp_evac[tb] = P.step("vector", dd)
            P.wait("vector", "c_act", ab_tm_done[par])
            dd = (P.eng.tensor_scalar(p5_otm[:, jp * C:(jp + 1) * C],
                                      p5_otm[:, jp * C:(jp + 1) * C],
                                      p5_ab[:, par * 16 + j:par * 16 + j + 1],
                                      p5_ab[:, par * 16 + 4 + j:par * 16 + 5 + j],
                                      ALU.mult, ALU.add)
                  if P.on("vector") else None)
            nrm = P.step("vector", dd)
            P.wait("sync", "c_dve", nrm)
            e = P.eng if P.on("sync") else None
            i = (e.dma_start(out=d["out"][t0 + j * 128:t0 + (j + 1) * 128, :],
                             in_=p5_otm[:, jp * C:(jp + 1) * C]) if e else None)
            otm_free[jp] = P.inc(i, "s_outa" if jp == 0 else "s_outb", 16)
        x5_free[par] = S["c_pe"]
        x5_pool_free[par] = ln2_done_by[par]

    p5_load(0)
    emit_C(0)
    for panel in range(1, NP3):
        p5_load(panel)
        emit_D(panel - 1)
        emit_C(panel)
    emit_D(NP3 - 1)

    P.wait("sync", "s_outa", S["s_outa"])
    P.wait("sync", "s_outb", S["s_outb"])


def logits_sb_view(d):
    return d["logits_sb"]


def build_program():
    nc = bass.Bass("TRN2", target_bir_lowering=False, debug=False)
    d = {"nc": nc}

    def dram(nm, shp, dt, kind):
        d[nm] = nc.dram_tensor(nm, shp, dt, kind=kind).ap()

    dram("xseq", [N, C], F32, "ExternalInput")
    dram("xpad", [NOWN + 32, C], F32, "ExternalInput")
    dram("wqkT", [C, 2 * C], F32, "ExternalInput")
    dram("bqk_q", [128, 8], F32, "ExternalInput")
    dram("bqk_qn", [128, 8], F32, "ExternalInput")
    dram("bqk_k", [1, C], F32, "ExternalInput")
    dram("lepe_wp", [128, 24], F32, "ExternalInput")
    dram("lepe_bp", [128, 8], F32, "ExternalInput")
    dram("lepe_bt_h", [128, NKC * T], BF16, "ExternalInput")
    dram("lepe_w1p", [128, 8], F32, "ExternalInput")
    dram("wg", [128, 64], F32, "ExternalInput")
    dram("bgate", [8, 1], F32, "ExternalInput")
    dram("b1p", [128, 256], F32, "ExternalInput")
    dram("b2r", [8, C], F32, "ExternalInput")
    dram("w1", [NE, C, MLP], F32, "ExternalInput")
    dram("w2", [NE, MLP, C], F32, "ExternalInput")
    dram("ident", [128, 128], F32, "ExternalInput")
    dram("maskKM", [128, 128], F32, "ExternalInput")
    dram("selz", [16, C], BF16, "ExternalInput")
    dram("selbc", [8, C], BF16, "ExternalInput")
    dram("ident8", [8, 8], BF16, "ExternalInput")
    dram("ident64", [64, 64], BF16, "ExternalInput")
    dram("unit8", [1, 64], F32, "ExternalInput")
    dram("out", [NOWN, C], F32, "ExternalOutput")
    dram("xseq_bf", [N, C], BF16, "Internal")
    dram("xpad_bf", [NOWN + 32, C], BF16, "Internal")
    dram("xln32_d", [NKC, 128, NOWN], F32, "Internal")
    dram("xlnbf_d", [NKC, 128, NOWN], BF16, "Internal")
    dram("y_d", [NOC, 128, NOWN], BF16, "Internal")

    BF_ELEMS = 67 * 1024
    F32_ELEMS = 18 * 1024

    import contextlib
    with contextlib.ExitStack() as ctx:
        arena_bf = ctx.enter_context(nc.sbuf_tensor("arena_bf", [128, BF_ELEMS], BF16))
        arena_f = ctx.enter_context(nc.sbuf_tensor("arena_f", [128, F32_ELEMS], F32))
        cb = Carver(arena_bf, BF_ELEMS)
        cf = Carver(arena_f, F32_ELEMS)

        d["kv_bf"] = cb.take(1024, 64)
        d["kv2_bf"] = cb.take(1024)
        d["ident64_sb"] = cb.take(64, 64)
        d["g_bf"] = cb.take(4096, 8)
        d["selz_sb"] = cb.take(1024, 16)
        d["selbc_sb"] = cb.take(1024, 8)
        d["ident8_sb"] = cb.take(16, 8)[:, 0:8]
        d["KM_bf"] = cb.take(128)
        d["ones_r_bf"] = cb.take(128, 1)
        d["ones_c_bf"] = cb.take(16)[:, 0:1]
        d["one1_bf"] = cb.take(16, 1)[:, 0:1]
        d["bqkk_sb"] = cb.take(1024, 1)
        d["ident_sb"] = cf.take(128)
        d["maskKM_sb"] = cf.take(128)
        d["unit8_sb"] = cf.take(64, 1)
        d["ones_r_f"] = cf.take(128, 1)
        d["ones_c_f"] = cf.take(16)[:, 0:1]
        d["one1_f"] = cf.take(16, 1)[:, 0:1]
        d["bqk_q_sb"] = cf.take(16)[:, 0:8]
        d["bqk_qn_sb"] = cf.take(16)[:, 0:8]
        d["lepe_w_sb"] = cf.take(32)[:, 0:24]
        d["lepe_b_sb"] = cf.take(16)[:, 0:8]
        d["lepe_w1p_sb"] = cf.take(16)[:, 0:8]
        d["wg_sb"] = cf.take(64)
        d["bgate_sb"] = cf.take(16, 8)[:, 0:1]
        d["b1p_sb"] = cf.take(256)
        d["kme_pm"] = cf.take(16)[:, 0:8]
        d["zden_b"] = cf.take(16, 16)[:, 0:1]
        d["kmean_r"] = cf.take(1024, 1)
        d["logits_sb"] = cf.take(512)

        ov_bf = cb.mark()
        ov_f = cf.mark()
        d["wqk_sb"] = cb.take(NKC * 1024)
        d["lepe_bt"] = cb.take(NKC * T)
        p13_bf = cb.mark()
        d["p1_xfm"] = cb.take(2 * NKC * T)
        d["p1_xtm"] = cb.take(2 * 4 * C)
        d["p1_ktm"] = cb.take(2 * C)
        cb_p1_end = cb.mark()
        cb.reset(p13_bf)
        d["p3_xe"] = cb.take(2 * NKC * 544)
        d["p3_qbf"] = cb.take(NKC * T)
        d["p3_zbf"] = cb.take(512, 16)
        d["p3_zb"] = cb.take(2 * T)
        cb_p3_end = cb.mark()
        p13_f = cf.mark()
        d["p1_r"] = cf.take(4 * T)
        d["p1_m"] = cf.take(4 * T)
        cf_p1_end = cf.mark()
        cf.reset(p13_f)
        d["p3_r"] = cf.take(2 * T)
        d["p3_m"] = cf.take(2 * T)
        d["p3_z"] = cf.take(T, 16)
        d["p3_t4"] = cf.take(2 * T)
        d["p3_ln"] = cf.take(2 * NKC * T)
        d["p3_sq"] = cf.take(T)
        d["p3_st"] = cf.take(4 * T, 1)
        d["p3_ab"] = cf.take(2 * T)
        d["p3_lt"] = cf.take(32)
        d["p3_gs"] = cf.take(96)
        cf_p3_end = cf.mark()
        cb.reset(ov_bf)
        d["p4_w1"] = cb.take(2 * NKC * QM)
        d["p4_w2"] = cb.take(2 * NMC * C)
        d["p4_xp"] = cb.take(2 * NKC * T)
        d["p4_h"] = cb.take(NMC * T)
        d["p4_gb"] = cb.take(2 * T)
        d["p4_gr"] = cb.take(2 * T, 1)
        d["p4_b2"] = cb.take(2 * C, 1)
        d["p4_yb"] = cb.take(2 * NOC * T)
        cb_p4_end = cb.mark()
        cb.reset(ov_bf)
        d["p5_y"] = cb.take(2 * NOC * T)
        cb_p5_end = cb.mark()
        cf.reset(ov_f)
        d["p4_hr"] = cf.take(2 * T)
        cf_p4_end = cf.mark()
        cf.reset(ov_f)
        d["p5_x"] = cf.take(2 * NKC * T)
        d["p5_sq"] = cf.take(T)
        d["p5_st"] = cf.take(4 * T, 1)
        d["p5_ab"] = cf.take(32)
        d["p5_otm"] = cf.take(2 * C)
        cf_p5_end = cf.mark()
        assert max(cb_p1_end, cb_p3_end, cb_p4_end, cb_p5_end) <= BF_ELEMS, (
            cb_p1_end, cb_p3_end, cb_p4_end, cb_p5_end)
        assert max(cf_p1_end, cf_p3_end, cf_p4_end, cf_p5_end) <= F32_ELEMS, (
            cf_p1_end, cf_p3_end, cf_p4_end, cf_p5_end)

        with contextlib.ExitStack() as pctx:
            psA = pctx.enter_context(nc.psum_tensor("psA", [128, 4096], F32))
            d["ps_kv"] = psA[0:64, 0:1024]
            d["ps_ks"] = psA[0:1, 1024:2048]
            d["ps_kt"] = psA[:, 2048:4096]
            d["ps_sm"] = psA[:, 3072:3584]
            d["ps3_a"] = psA[:, 0:512]
            d["ps3_b"] = psA[:, 512:1024]
            d["ps3_c"] = psA[:, 1024:1536]
            d["ps3_d"] = psA[:, 1536:2048]
            d["ps3_e"] = psA[:, 2048:2560]
            d["ps3_s"] = psA[0:1, 2560:3584]
            d["ps4"] = psA[:, 0:4096]
            d["ps5_s"] = psA[0:1, 0:1024]
            d["ps5_c"] = psA[:, 1024:1536]
            d["ps5_t"] = psA[:, 1536:3584]

            sems = {}
            with contextlib.ExitStack() as sctx:
                for nm in SEM_NAMES:
                    sems[nm] = sctx.enter_context(nc.semaphore(nm))
                with nc.Block() as block:

                    @block.sync
                    def _(eng):
                        _emit(Walk("sync", eng, sems), d)

                    @block.vector
                    def _(eng):
                        _emit(Walk("vector", eng, sems), d)

                    @block.scalar
                    def _(eng):
                        _emit(Walk("scalar", eng, sems), d)

                    @block.tensor
                    def _(eng):
                        _emit(Walk("tensor", eng, sems), d)

                    @block.gpsimd
                    def _(eng):
                        _emit(Walk("gpsimd", eng, sems), d)
    return nc


def _prep_inputs(inputs):
    x = np.asarray(inputs["L0_attn_x"], dtype=np.float32)
    Wqk = np.asarray(inputs["L0_attn_W_qk"], dtype=np.float32)
    bqk = np.asarray(inputs["L0_attn_b_qk"], dtype=np.float32)
    lw = np.asarray(inputs["L0_attn_lepe_w"], dtype=np.float32)
    lb = np.asarray(inputs["L0_attn_lepe_b"], dtype=np.float32)
    wgate = np.asarray(inputs["L0_ffn_w_gate"], dtype=np.float32)
    bg = np.asarray(inputs["L0_ffn_b_gate"], dtype=np.float32)
    W1 = np.ascontiguousarray(np.asarray(inputs["L0_ffn_W1"], dtype=np.float32))
    b1 = np.asarray(inputs["L0_ffn_b1"], dtype=np.float32)
    W2 = np.ascontiguousarray(np.asarray(inputs["L0_ffn_W2"], dtype=np.float32))
    b2 = np.asarray(inputs["L0_ffn_b2"], dtype=np.float32)

    common = dict(
        wqkT=np.ascontiguousarray(Wqk.T),
        bqk_q=np.ascontiguousarray(bqk[:C].reshape(8, 128).T),
        bqk_qn=np.ascontiguousarray(-bqk[:C].reshape(8, 128).T),
        bqk_k=np.ascontiguousarray(bqk[C:].reshape(1, C)),
        lepe_wp=np.ascontiguousarray(lw.reshape(8, 128, 3).transpose(1, 0, 2).reshape(128, 24)),
        lepe_bp=np.ascontiguousarray(lb.reshape(8, 128).T),
        lepe_bt_h=np.ascontiguousarray(
            np.repeat(lb.reshape(8, 128).T.astype(BF)[:, :, None], T, axis=2
                      ).reshape(128, NKC * T)),
        lepe_w1p=np.ascontiguousarray(
            (lw.reshape(8, 128, 3).transpose(1, 0, 2)[:, :, 1] + 1.0).reshape(128, 8)),
        wg=np.ascontiguousarray(wgate.reshape(8, 128, 8).transpose(1, 0, 2).reshape(128, 64)),
        bgate=np.ascontiguousarray(bg.reshape(8, 1)),
        b1p=np.ascontiguousarray(b1.reshape(8, 32, 128).transpose(2, 0, 1).reshape(128, 256)),
        b2r=np.ascontiguousarray(b2),
        w1=W1, w2=W2,
        ident=np.eye(128, dtype=np.float32),
    )
    cc_idx = np.arange(128)[None, :] // 64
    hrow = np.arange(16)[:, None]
    selz = np.zeros((16, C), dtype=BF)
    maskKM = np.zeros((128, 128), dtype=np.float32)
    for cc in range(8):
        selz[:, cc * 128:(cc + 1) * 128] = (hrow == (2 * cc + cc_idx)).astype(BF)
        maskKM[:, cc * 16:(cc + 1) * 16] = (
            np.arange(16)[None, :] == (2 * cc + np.arange(128)[:, None] // 64)
        ).astype(np.float32)
    selbc = np.zeros((8, C), dtype=BF)
    for e in range(8):
        selbc[e, e * 128:(e + 1) * 128] = BF(1.0)
    unit8 = np.zeros((1, 64), dtype=np.float32)
    for cc in range(8):
        unit8[0, cc * 8 + cc] = 1.0
    common.update(
        maskKM=maskKM, selz=selz, selbc=selbc,
        ident8=np.eye(8, dtype=np.float32).astype(BF),
        ident64=np.eye(64, dtype=np.float32).astype(BF), unit8=unit8,
    )
    in_maps = []
    for core in range(8):
        s, half = divmod(core, 2)
        own = x[s, half * NOWN:(half + 1) * NOWN]
        prev = (x[s, NOWN - 16:NOWN] if half == 1
                else np.zeros((16, C), np.float32))
        nxt = (x[s, NOWN:NOWN + 16] if half == 0
               else np.zeros((16, C), np.float32))
        m = dict(common)
        m["xseq"] = np.ascontiguousarray(x[s])
        m["xpad"] = np.ascontiguousarray(np.concatenate([prev, own, nxt], axis=0))
        in_maps.append(m)
    return in_maps


def kernel(**inputs):
    in_maps = _prep_inputs(inputs)
    nc = build_program()
    res = run_bass_kernel_spmd(nc, in_maps, core_ids=list(range(8)))
    full = np.empty((B, N, C), dtype=np.float32)
    for core in range(8):
        s, half = divmod(core, 2)
        full[s, half * NOWN:(half + 1) * NOWN] = res.results[core]["out"]
    return full
